# version 20 (final)
# baseline (speedup 1.0000x reference)
import numpy as np
import ml_dtypes
import concourse.bass as bass
import concourse.mybir as mybir
from concourse.bass_utils import run_bass_kernel_spmd

F32 = mybir.dt.float32
BF16 = mybir.dt.bfloat16
F8 = mybir.dt.float8e4
AF = mybir.ActivationFunctionType
ALU = mybir.AluOpType

DM = 2048
DIP = 8512
DI = 4096
NCT = 67
NK = DM // 128
CW = 34
EPS = 1e-5


class Sched:

    DEF_COST = {"sync": 1200, "gpsimd": 1600, "vector": 650, "scalar": 650,
                "tensor": 400}

    def __init__(self):
        self.ops = []
        self.last_write = {}
        self.readers = {}

    def add(self, engine, fn, reads=(), writes=(), dma=None, tag="", cost=None):
        idx = len(self.ops)
        deps = set()
        for t in reads:
            if t in self.last_write:
                deps.add(self.last_write[t])
        for t in writes:
            if t in self.last_write:
                deps.add(self.last_write[t])
            for r in self.readers.get(t, ()):
                deps.add(r)
        self.ops.append(dict(i=idx, engine=engine, fn=fn, deps=deps, dma=dma,
                             inc=False, tag=tag,
                             cost=cost if cost else self.DEF_COST[engine]))
        for t in reads:
            self.readers.setdefault(t, []).append(idx)
        for t in writes:
            self.last_write[t] = idx
            self.readers[t] = []
        return idx

    def reorder(self):
        import heapq
        ops = self.ops
        n = len(ops)
        assert all(op["i"] == k for k, op in enumerate(ops))
        children = [[] for _ in range(n)]
        byid0 = {op["i"]: op for op in ops}
        ndep = [0] * n
        for op in ops:
            for d in op["deps"]:
                children[d].append(op["i"])
                ndep[op["i"]] += 1
        LAT = 500
        ISSUE = 600
        cp = [0] * n
        for i in range(n - 1, -1, -1):
            m = 0
            ei = ops[i]["engine"]
            for c in children[i]:
                v = cp[c] + (LAT if ops[c]["engine"] != ei else 0)
                if v > m:
                    m = v
            cp[i] = ops[i]["cost"] + m
        engines = ["sync", "gpsimd", "vector", "scalar", "tensor"]
        ready = {e: [] for e in engines}
        for op in ops:
            if ndep[op["i"]] == 0:
                heapq.heappush(ready[op["engine"]], (-cp[op["i"]], op["i"]))
        etime = {e: 0 for e in engines}
        done = [0] * n
        order = []
        remaining = n
        while remaining:
            best = None
            for e in engines:
                while ready[e]:
                    i = ready[e][0][1]
                    break
                if not ready[e]:
                    continue
                i = ready[e][0][1]
                start = etime[e]
                for d in ops[i]["deps"]:
                    dd = done[d] + (LAT if byid0[d]["engine"] != e else 0)
                    if dd > start:
                        start = dd
                if best is None or start < best[0]:
                    best = (start, e, i)
            start, e, i = best
            heapq.heappop(ready[e])
            fin = start + ops[i]["cost"]
            if ops[i]["dma"] is not None:
                etime[e] = start + ISSUE
            else:
                etime[e] = fin
            done[i] = fin
            order.append(i)
            remaining -= 1
            for c in children[i]:
                ndep[c] -= 1
                if ndep[c] == 0:
                    heapq.heappush(ready[ops[c]["engine"]], (-cp[c], c))
        self.ops = [ops[i] for i in order]
        return max(etime.values())

    def emit(self, nc, stack):
        est = self.reorder()
        print(f"[sched] est span {est/1e6:.2f} ms over {len(self.ops)} ops")
        ops = self.ops
        byid = {op["i"]: op for op in ops}
        for op in ops:
            for d in op["deps"]:
                byid[d]["inc"] = True
        for op in ops:
            if op["dma"] is not None:
                op["inc"] = True
        engines = ["sync", "gpsimd", "vector", "scalar", "tensor"]
        dma_groups = sorted({op["dma"] for op in ops if op["dma"] is not None})
        R = 4
        esem = {e: stack.enter_context(nc.semaphore(f"e_{e}")) for e in engines}
        dsem = {(g, r): stack.enter_context(nc.semaphore(f"d_{g}{r}"))
                for g in dma_groups for r in range(R)}
        ecnt = {e: 0 for e in engines}
        dcnt = {g: 0 for g in dma_groups}
        for op in ops:
            if op["dma"] is not None:
                k = dcnt[op["dma"]]
                dcnt[op["dma"]] += 1
                sub, rnd = k % R, k // R
                op["ev"] = ("d", (op["dma"], sub), 16 * (rnd + 1))
                op["issue_wait"] = ((op["dma"], sub), 16 * rnd) if rnd else None
            elif op["inc"]:
                ecnt[op["engine"]] += 1
                op["ev"] = ("e", op["engine"], ecnt[op["engine"]])
                op["issue_wait"] = None
            else:
                op["ev"] = None
                op["issue_wait"] = None
        wm = {}
        for op in ops:
            need = {}
            for d in op["deps"]:
                pr = byid[d]
                if pr["ev"] is None:
                    continue
                kind, key, val = pr["ev"]
                sk = (kind, key)
                need[sk] = max(need.get(sk, 0), val)
            if op["issue_wait"] is not None:
                sk, val = ("d", op["issue_wait"][0]), op["issue_wait"][1]
                need[sk] = max(need.get(sk, 0), val)
            waits = []
            for sk, val in sorted(need.items(), key=str):
                w = (op["engine"], sk)
                if wm.get(w, -1) >= val:
                    continue
                wm[w] = val
                sem = esem[sk[1]] if sk[0] == "e" else dsem[sk[1]]
                waits.append((sem, val))
            op["waits"] = waits
        self.totals = ([(esem[e], ecnt[e]) for e in engines if ecnt[e]] +
                       [(dsem[(g, r)], 16 * (dcnt[g] // R + (1 if r < dcnt[g] % R else 0)))
                        for g in dma_groups for r in range(R)
                        if dcnt[g] // R + (1 if r < dcnt[g] % R else 0) > 0])
        blk = stack.enter_context(nc.Block())

        def body_for(name):
            def body(e):
                for op in ops:
                    if op["engine"] != name:
                        continue
                    for sem, val in op["waits"]:
                        e.wait_ge(sem, val)
                    inst = op["fn"](e)
                    if op["dma"] is not None:
                        assert inst is not None, f"op {op['i']} {op['tag']} returned None"
                        inst.then_inc(dsem[op["ev"][1]], 16)
                    elif op["inc"]:
                        assert inst is not None, f"op {op['i']} {op['tag']} returned None"
                        inst.then_inc(esem[op["engine"]], 1)
            return body

        blk.sync(body_for("sync"))
        blk.gpsimd(body_for("gpsimd"))
        blk.vector(body_for("vector"))
        blk.scalar(body_for("scalar"))
        blk.tensor(body_for("tensor"))
        totals = self.totals

        def fin(e):
            for sem, val in totals:
                e.wait_ge(sem, val)
        blk.sync(fin)


def build_program(NST=8):
    T = 512
    L = NST * T
    NCH = 4
    nc = bass.Bass("TRN2", target_bir_lowering=False)

    def inp(name, shape, dt=F32):
        return nc.dram_tensor(name, list(shape), dt, kind="ExternalInput").ap()

    u_t = inp("u_t", (NK, 128, L))
    u_tm = inp("u_tm", (L, DM))
    wi_t = inp("wi_t", (NCT, 128, NK * 128))
    wo_t = inp("wo_t", (32, 128, DM))
    wf_t = inp("wf_t", (16, 128, 2048))
    convw = inp("convw", (128, CW, 4))
    convb = inp("convb", (128, CW))
    dtb = inp("dtb", (64, 1))
    alog = inp("alog", (64, 1))
    dp_in = inp("dp_in", (128, 32))
    normw = inp("normw", (128, 32))
    eye_b = inp("eye_b", (128, 128), BF16)
    eye_f = inp("eye_f", (128, 128))
    mask01 = inp("mask01", (128, 128), BF16)
    ones_col_b = inp("ones_col_b", (128, 1), BF16)
    ones_row_f = inp("ones_row_f", (1, 128))
    ind4 = inp("ind4", (8, 1024), BF16)

    out = nc.dram_tensor("out", [L, DM], F32, kind="ExternalOutput").ap()

    wi_sc = nc.dram_tensor("wi_sc", [NCT, 128, NK * 128], BF16, kind="Internal").ap()
    wo_sc = nc.dram_tensor("wo_sc", [32, 128, DM], BF16, kind="Internal").ap()
    wf_sc = nc.dram_tensor("wf_sc", [16, 128, 2048], BF16, kind="Internal").ap()
    z_sc = nc.dram_tensor("z_sc", [32, 128, L], BF16, kind="Internal").ap()
    x_sc = nc.dram_tensor("x_sc", [32, 128, L], BF16, kind="Internal").ap()
    fl_sc = nc.dram_tensor("fl_sc", [3, 64, 128], F32, kind="Internal").ap()
    fl_b = nc.dram_tensor("fl_b", [64, 128], BF16, kind="Internal").ap()
    fl_b2 = nc.dram_tensor("fl_b2", [2, 64, 128], BF16, kind="Internal").ap()
    fl_b3 = nc.dram_tensor("fl_b3", [2, 64, 128], BF16, kind="Internal").ap()

    s = Sched()
    from contextlib import ExitStack
    stack = ExitStack()

    def sb(name, shape, dt):
        return nc.alloc_sbuf_tensor(name, list(shape), dt)

    su_bf = sb("su_bf", [128, NK * T], BF16)
    wi_sb = [sb(f"wi_sb{i}", [128, NK * 128], BF16) for i in range(2)]
    strip2 = [sb(f"strip{i}", [128, 515], BF16) for i in range(2)]
    halo = sb("halo", [128, CW * 3], BF16)
    conv_a = sb("conv_a", [128, T], F32)
    conv_b2 = sb("conv_b2", [128, T], F32)
    sB_fm = sb("sB_fm", [128, T], BF16)
    sC_fm = sb("sC_fm", [128, T], BF16)
    sxst2 = [sb(f"sxst{i}", [128, T], BF16) for i in range(2)]
    sxch = sb("sxch", [128, 32 * 128], BF16)
    sdt0 = sb("sdt0", [64, T], F32)
    sdt = sb("sdt", [64, T], F32)
    sdt_t1 = sb("sdt_t1", [64, T], F32)
    sdt_e = sb("sdt_e", [64, T], F32)
    sacs = sb("sacs", [64, T], F32)
    slnma = sb("slnma", [64, T], F32)
    ssig = sb("ssig", [64, T], F32)
    sw_fm = sb("sw_fm", [64, 128], F32)
    scd = sb("scd", [64, 4], F32)
    sones64 = sb("sones64", [64, 128], F32)
    sconst1 = sb("sconst1", [128, 1], F32)
    sconstm05 = sb("sconstm05", [128, 1], F32)
    sconstm1 = sb("sconstm1", [128, 1], F32)
    lnd5x = [sb(f"lnd5_{i}", [10, 256], BF16) for i in range(2)]
    shilo = sb("shilo", [64, 256], BF16)
    rhs5 = sb("rhs5", [10, 1024], BF16)
    x_tm = sb("x_tm", [128, 32 * 128], BF16)
    sxw = sb("sxw", [128, 32 * 128], BF16)
    sBtm = sb("sBtm", [128, 128], BF16)
    swT = sb("swT", [128, 64], F32)
    ssm = sb("ssm", [128, 128], BF16)
    smind2 = [sb(f"smind{i}", [128, 512], F32) for i in range(2)]
    sexpD2 = [sb(f"sexpD{i}", [128, 512], BF16) for i in range(2)]
    sSL = [sb(f"sSL{i}", [128, 512], BF16) for i in range(2)]
    S = sb("S", [128, 32 * 128], F32)
    scast = sb("scast", [128, 512], BF16)
    cdP = sb("cdP", [128, 64], F32)
    sscan = sb("sscan", [128, 512], F32)
    s_sF = [sb(f"s_sF{i}", [128, 512], BF16) for i in range(2)]
    sta = sb("sta", [128, 512], F32)
    stb = sb("stb", [128, 512], F32)
    sxd = sb("sxd", [128, 512], BF16)
    sg = sb("sg", [128, 512], F32)
    sy = sb("sy", [128, 32 * 128], BF16)
    sq8 = sy[:, 0:1024].bitcast(F8)
    sy2 = [sb(f"sy2_{i}", [128, 512], BF16) for i in range(2)]
    sstat = [sb(f"sstat{i}", [1, 128], F32) for i in range(2)]
    sv1 = sb("sv1", [1, 128], F32)
    sv2 = sb("sv2", [1, 128], F32)
    srstd1 = sb("srstd1", [1, 128], F32)
    syn = sb("syn", [128, 32 * T], BF16)
    sz_ep = [sb(f"sz_ep{i}", [128, 512], BF16) for i in range(2)]
    wo_sb = [sb(f"wo_sb{i}", [128, DM], BF16) for i in range(2)]
    su_res = [sb(f"su_res{i}", [128, 512], F32) for i in range(2)]
    x1 = [sb(f"x1_{i}", [128, DM], F32) for i in range(2)]
    ssum2_ = [sb(f"lnsum{i}", [128, 1], F32) for i in range(2)]
    ssq2_ = [sb(f"lnsq{i}", [128, 1], F32) for i in range(2)]
    smean2_ = [sb(f"lnmean{i}", [128, 1], F32) for i in range(2)]
    sm22_ = [sb(f"lnm2{i}", [128, 1], F32) for i in range(2)]
    svar2_ = [sb(f"lnvar{i}", [128, 1], F32) for i in range(2)]
    srstd2_ = [sb(f"lnrstd{i}", [128, 1], F32) for i in range(2)]
    samax = sb("samax", [128, 16], F32)
    srcp = sb("srcp", [128, 16], F32)
    sr224 = sb("sr224", [128, 16], F32)
    sa224 = sb("sa224", [128, 16], F32)
    sq8_view = None
    sxdq = sb("sxdq", [128, DM], BF16)
    sxdq_fm = [sb(f"sxdq_fm{i}", [128, 16 * 128], BF16) for i in range(2)]
    wdq_sb = [sb(f"wdq_sb{i}", [128, 1024], BF16) for i in range(2)]
    sfb = scast
    sh2 = sxw[:].bitcast(F32)
    sh2b = sy[:].bitcast(F32)
    sout = [smind2[0], sscan]
    samx2 = samax
    samx3 = sb("samx3", [16, 1], F32)
    samx4 = sa224[0:1, :]
    swr = srcp
    swri = sr224
    snw = sb("snw", [128, 32], F32)
    sDp = sb("sDp", [128, 32], BF16)
    sconvw = sb("sconvw", [128, CW * 4], F32)
    sconvb = sb("sconvb", [128, CW], F32)
    sdtb = sb("sdtb", [64, 1], F32)
    salog = sb("salog", [64, 1], F32)
    saneg = sb("saneg", [64, 1], F32)
    seyeb = sb("seyeb", [128, 128], BF16)
    seyef = sb("seyef", [128, 128], F32)
    smask = sb("smask", [128, 128], BF16)
    sonescol = sb("sonescol", [128, 1], BF16)
    sonesrow = sb("sonesrow", [1, 128], F32)

    p = [nc.alloc_psum_tensor(f"p{i}", [128, 512], F32) for i in range(8)]
    p2b = p[2][:].bitcast(BF16)

    A = s.add

    def dma(e_dst, src):
        def f(e):
            return e.dma_start(out=e_dst, in_=src)
        return f

    A("sync", dma(sconvw[:], convw.rearrange("p j k -> p (j k)")), writes=["convw"], dma="m")
    A("sync", dma(sconvb[:], convb), writes=["convb"], dma="m")
    A("sync", dma(sdtb[:], dtb), writes=["dtb"], dma="m")
    A("sync", dma(salog[:], alog), writes=["alog"], dma="m")
    A("gpsimd", dma(sDp[:], dp_in), writes=["dp"], dma="mg")
    A("sync", dma(snw[:], normw), writes=["nw"], dma="m")
    A("sync", dma(seyeb[:], eye_b), writes=["eyeb"], dma="m")
    A("sync", dma(seyef[:], eye_f), writes=["eyef"], dma="m")
    A("sync", dma(smask[:], mask01), writes=["mask"], dma="m")
    A("sync", dma(sonescol[:], ones_col_b), writes=["onescol"], dma="m")
    A("sync", dma(sonesrow[:], ones_row_f), writes=["onesrow"], dma="m")
    A("sync", dma(rhs5[2:10, :], ind4), writes=["rhs5c"], dma="m")
    A("vector", lambda e: e.memset(sones64[:], 1.0), writes=["pk4"])
    A("vector", lambda e: e.memset(sconst1[:], 1.0), writes=["cpool"])
    A("vector", lambda e: e.memset(sconstm05[:], -0.5), writes=["cpool"])
    A("vector", lambda e: e.memset(sconstm1[:], -1.0), writes=["cpool"])
    A("vector", lambda e: e.memset(lnd5x[0][0:2, :], 1.0), writes=["lnd5r0"])
    A("vector", lambda e: e.memset(lnd5x[1][0:2, :], 1.0), writes=["lnd5r0"])
    A("vector", lambda e: e.memset(S[:], 0.0), writes=[("S", i) for i in range(8)])
    A("vector", lambda e: e.memset(halo[:], 0.0), writes=["halo"])
    A("scalar", lambda e: e.activation(out=saneg[:], in_=salog[:], func=AF.Exp),
      reads=["alog"], writes=["aneg"])
    A("vector", lambda e: e.tensor_scalar_mul(saneg[:], saneg[:], -1.0),
      reads=["aneg"], writes=["aneg"])

    for i in range(16):
        A("sync", dma(sh2[:], wf_t[i]), reads=["sxw"], writes=["sxw"], dma="w")
        A("vector", lambda e: e.tensor_reduce(
            out=samx2[:, :, None], in_=sh2[:].rearrange("p (a b) -> p a b", b=128),
            axis=mybir.AxisListType.X, op=ALU.max, apply_absolute_value=True),
          reads=["sxw"], writes=["samax"])
        A("tensor", lambda e: e.transpose(p[3][0:16, 0:128], samx2[:], seyef[:]),
          reads=["samax", "eyef"], writes=["p3"])
        A("vector", lambda e: e.tensor_reduce(
            out=samx3[:, None, :], in_=p[3][0:16, None, 0:128],
            axis=mybir.AxisListType.X, op=ALU.max),
          reads=["p3"], writes=["samx3"])
        A("vector", lambda e: e.tensor_scalar_max(samx3[:], samx3[:], 1e-4),
          reads=["samx3"], writes=["samx3"])
        A("tensor", lambda e: e.transpose(p[4][0:1, 0:16], samx3[:], seyef[0:16, 0:16]),
          reads=["samx3", "eyef"], writes=["p4"])
        A("vector", lambda e: e.tensor_copy(out=samx4[:], in_=p[4][0:1, 0:16]),
          reads=["p4"], writes=["sa224"])
        A("tensor", lambda e: e.matmul(p[4][:, 0:16], sonesrow[:], samx4[:],
                                       start=True, stop=True),
          reads=["sa224", "onesrow"], writes=["p4"])
        A("vector", lambda e: e.reciprocal(out=swr[:], in_=p[4][:, 0:16]),
          reads=["p4"], writes=["srcp"])
        A("vector", lambda e: e.tensor_scalar_mul(swri[:], swr[:], 224.0),
          reads=["srcp"], writes=["sr224"])
        A("vector", lambda e: e.tensor_scalar_mul(swr[:], p[4][:, 0:16], 1.0 / 224.0),
          reads=["p4"], writes=["srcp"])
        A("vector", lambda e: e.tensor_tensor(
            out=sq8[:].rearrange("p (a b) -> p a b", b=128),
            in0=sh2[:].rearrange("p (a b) -> p a b", b=128),
            in1=swri[:, :, None].broadcast_to([128, 16, 128]), op=ALU.mult),
          reads=["sxw", "sr224"], writes=["sy"])
        A("vector", lambda e: e.tensor_tensor(
            out=sxdq[:].rearrange("p (a b) -> p a b", b=128),
            in0=sq8[:].rearrange("p (a b) -> p a b", b=128),
            in1=swr[:, :, None].broadcast_to([128, 16, 128]), op=ALU.mult),
          reads=["sy", "srcp"], writes=["sxdq"])
        A("sync", dma(wf_sc[i], sxdq[:]), reads=["sxdq"], writes=[("wf", i), "sxdq"],
          dma="w")

    for st in range(NST):
        t0 = st * T
        A("gpsimd", dma(su_bf[:], u_t[:, :, t0:t0 + T].rearrange("k p t -> p k t")),
          writes=["su_bf"], dma="ug", tag=f"u{st}", cost=12000)

        wi3 = [wi_sb[0], wi_sb[1], sxdq]
        for c in range(NCT):
            par = c % 3
            wtok = ("wi_sb", par) if par < 2 else "sxdq"
            if st == 0:
                A("gpsimd", (lambda c=c, par=par: dma(wi3[par][:], wi_t[c]))(),
                  reads=[wtok], writes=[wtok], dma="wig", tag=f"wiL{c}")
                A("sync", (lambda c=c, par=par: dma(wi_sc[c], wi3[par][:]))(),
                  reads=[wtok], writes=[("wisc", c)], dma="wi")
            else:
                A("sync", (lambda c=c, par=par: dma(wi3[par][:], wi_sc[c]))(),
                  reads=[wtok, ("wisc", c)], writes=[wtok], dma="wi",
                  tag=f"wiL{st}_{c}")
            pb = c % 2
            def mk_mm(c=c, par=par, pb=pb):
                def f(e):
                    r = None
                    for k in range(NK):
                        r = e.matmul(p[pb][:], wi3[par][:, k * 128:(k + 1) * 128],
                                     su_bf[:, k * T:(k + 1) * T],
                                     start=(k == 0), stop=(k == NK - 1))
                    return r
                return f
            A("tensor", mk_mm(), reads=[wtok, "su_bf"], writes=[f"p{pb}"],
              tag=f"ip{st}_{c}", cost=3600)
            if c < 32:
                sp = c % 2
                A("vector", (lambda pb=pb, sp=sp: lambda e: e.tensor_copy(
                    out=sxst2[sp][:], in_=p[pb][:]))(),
                  reads=[f"p{pb}", ("sxst", sp)], writes=[("sxst", sp)],
                  tag=f"zev{st}_{c}")
                A("sync", (lambda c=c, t0=t0, sp=sp: dma(
                    z_sc[c][:, t0:t0 + T], sxst2[sp][:]))(),
                  reads=[("sxst", sp)], writes=[("z", st, c), ("sxst", sp)], dma="z")
            elif c < 66:
                j = c - 32
                sp = c % 2
                strip = strip2[sp]
                stok = ("strip", sp)
                A("vector", (lambda j=j, strip=strip, sp=sp: lambda e: e.tensor_copy(
                    out=strip[:, 0:3], in_=halo[:, 3 * j:3 * j + 3]))(),
                  reads=["halo", stok], writes=[stok], tag=f"hin{st}_{j}")
                A("scalar", (lambda pb=pb, strip=strip: lambda e: e.copy(
                    out=strip[:, 3:515], in_=p[pb][:]))(),
                  reads=[f"p{pb}", stok], writes=[stok], tag=f"xbev{st}_{j}")
                A("vector", (lambda j=j, strip=strip: lambda e: e.tensor_copy(
                    out=halo[:, 3 * j:3 * j + 3], in_=strip[:, 512:515]))(),
                  reads=[stok], writes=["halo"], tag=f"hout{st}_{j}")
                A("vector", (lambda j=j, strip=strip: lambda e: e.tensor_scalar(
                    out=conv_a[:], in0=strip[:, 0:T], scalar1=sconvw[:, 4 * j:4 * j + 1],
                    scalar2=None, op0=ALU.mult))(),
                  reads=[stok, "convw"], writes=["conv_a"], tag=f"cv0_{st}_{j}")
                for k in (1, 2):
                    src_, dst_ = (conv_a, conv_b2) if k == 1 else (conv_b2, conv_a)
                    A("vector", (lambda j=j, k=k, src_=src_, dst_=dst_, strip=strip:
                      lambda e: e.scalar_tensor_tensor(
                          out=dst_[:], in0=strip[:, k:k + T],
                          scalar=sconvw[:, 4 * j + k:4 * j + k + 1], in1=src_[:],
                          op0=ALU.mult, op1=ALU.add))(),
                      reads=[stok, "convw", "conv_a" if k == 1 else "conv_b2"],
                      writes=["conv_b2" if k == 1 else "conv_a"], tag=f"cv{k}_{st}_{j}")
                A("vector", (lambda j=j, strip=strip: lambda e: e.scalar_tensor_tensor(
                    out=conv_b2[:], in0=strip[:, 3:3 + T],
                    scalar=sconvw[:, 4 * j + 3:4 * j + 4], in1=conv_a[:],
                    op0=ALU.mult, op1=ALU.add))(),
                  reads=[stok, "convw", "conv_a"], writes=["conv_b2"],
                  tag=f"cv3_{st}_{j}")
                if j < 32:
                    A("scalar", (lambda j=j, sp=sp: lambda e: e.activation(
                        out=sxst2[sp][:], in_=conv_b2[:], func=AF.Silu,
                        bias=sconvb[:, j:j + 1]))(),
                      reads=["conv_b2", "convb", ("sxst", sp)], writes=[("sxst", sp)],
                      tag=f"silu{st}_{j}")
                    A("sync", (lambda j=j, t0=t0, sp=sp: dma(
                        x_sc[j][:, t0:t0 + T], sxst2[sp][:]))(),
                      reads=[("sxst", sp)], writes=[("x", st, j), ("sxst", sp)],
                      dma="z")
                elif j == 32:
                    A("scalar", (lambda j=j: lambda e: e.activation(
                        out=sB_fm[:], in_=conv_b2[:], func=AF.Silu,
                        bias=sconvb[:, j:j + 1]))(),
                      reads=["conv_b2", "convb", "sB_fm"], writes=["sB_fm"],
                      tag=f"silu{st}_{j}")
                else:
                    A("scalar", (lambda j=j: lambda e: e.activation(
                        out=sC_fm[:], in_=conv_b2[:], func=AF.Silu,
                        bias=sconvb[:, j:j + 1]))(),
                      reads=["conv_b2", "convb", "sC_fm"], writes=["sC_fm"],
                      tag=f"silu{st}_{j}")
            else:
                A("vector", (lambda pb=pb: lambda e: e.tensor_scalar(
                    out=sdt0[:], in0=p[pb][0:64, :], scalar1=sdtb[:], scalar2=None,
                    op0=ALU.add))(),
                  reads=[f"p{pb}", "dtb"], writes=["pk1"], tag=f"dt0_{st}")
                A("scalar", lambda e: e.activation(out=sdt_t1[:], in_=sdt0[:],
                                                   func=AF.Abs),
                  reads=["pk1"], writes=["pk2"])
                A("scalar", lambda e: e.activation(out=sdt_e[:], in_=sdt_t1[:],
                                                   func=AF.Exp,
                                                   scale=sconstm1[0:64, :]),
                  reads=["pk2", "cpool"], writes=["pk2"])
                A("scalar", lambda e: e.activation(out=sdt_t1[:], in_=sdt_e[:],
                                                   func=AF.Ln, bias=sconst1[0:64, :]),
                  reads=["pk2", "cpool"], writes=["pk2"])
                A("vector", lambda e: e.tensor_scalar(
                    out=sdt_e[:], in0=sdt0[:], scalar1=0.0, scalar2=None,
                    op0=ALU.max),
                  reads=["pk1"], writes=["pk2"])
                A("vector", lambda e: e.tensor_add(sdt[:], sdt_t1[:], sdt_e[:]),
                  reads=["pk2"], writes=["pk1"], tag=f"dt_{st}")

        A("vector", lambda e: e.tensor_scalar(out=sdt0[:], in0=sdt[:], scalar1=1e-30,
                                              scalar2=None, op0=ALU.max),
          reads=["pk1"], writes=["pk1"])
        A("scalar", lambda e: e.activation(out=sdt_e[:], in_=sdt0[:], func=AF.Ln),
          reads=["pk1"], writes=["pk2"])
        A("vector", lambda e: e.tensor_scalar(out=sdt_t1[:], in0=sdt[:], scalar1=saneg[:],
                                              scalar2=None, op0=ALU.mult),
          reads=["pk1", "aneg"], writes=["pk2"], tag=f"dA_{st}")
        for cch in range(NCH):
            A("vector", (lambda cch=cch: lambda e: e.tensor_tensor_scan(
                out=sacs[:, cch * 128:(cch + 1) * 128], data0=sones64[:],
                data1=sdt_t1[:, cch * 128:(cch + 1) * 128], initial=0.0,
                op0=ALU.mult, op1=ALU.add))(),
              reads=["pk2", "pk4"], writes=["pk3"], tag=f"acs{st}_{cch}")
        A("vector", lambda e: e.tensor_sub(slnma[:], sdt_e[:], sacs[:]),
          reads=["pk2", "pk3"], writes=["pk3"])
        A("scalar", lambda e: e.activation(out=ssig[:], in_=sacs[:], func=AF.Exp),
          reads=["pk3"], writes=["pk4"])
        A("scalar", lambda e: e.activation(
            out=scd[:, :, None],
            in_=sacs[:].rearrange("h (c l) -> h c l", l=128)[:, :, 127:128],
            func=AF.Exp),
          reads=["pk3"], writes=["pk4"], tag=f"cd{st}")

        for cch in range(NCH):
            cg = st * NCH + cch
            tc0 = cch * 128
            A("scalar", (lambda cch=cch: lambda e: e.activation(
                out=sw_fm[:], in_=slnma[:, cch * 128:(cch + 1) * 128], func=AF.Exp,
                bias=sacs[:, cch * 128 + 127:cch * 128 + 128]))(),
              reads=["pk3"], writes=["pk4"], tag=f"wfm{cg}")
            A("vector", (lambda cch=cch: lambda e: e.tensor_copy(
                out=shilo[:, 0:128], in_=sacs[:, cch * 128:(cch + 1) * 128]))(),
              reads=["pk3"], writes=["shilo"], tag=f"hiA{cg}")
            A("vector", (lambda cch=cch: lambda e: e.tensor_sub(
                shilo[:, 128:256], sacs[:, cch * 128:(cch + 1) * 128],
                shilo[:, 0:128]))(),
              reads=["pk3", "shilo"], writes=["shilo"], tag=f"loA{cg}")
            A("sync", dma(fl_b2.rearrange("a h s -> h a s"),
                          shilo[:].rearrange("h (a s) -> h a s", s=128)),
              reads=["shilo"], writes=["fl0"], dma="f")
            A("vector", (lambda cch=cch: lambda e: e.tensor_copy(
                out=shilo[:, 0:128], in_=slnma[:, cch * 128:(cch + 1) * 128]))(),
              reads=["pk3", "shilo", "fl0"], writes=["shilo"], tag=f"hiL{cg}")
            A("vector", (lambda cch=cch: lambda e: e.tensor_sub(
                shilo[:, 128:256], slnma[:, cch * 128:(cch + 1) * 128],
                shilo[:, 0:128]))(),
              reads=["pk3", "shilo"], writes=["shilo"], tag=f"loL{cg}")
            A("sync", dma(fl_b3.rearrange("a h s -> h a s"),
                          shilo[:].rearrange("h (a s) -> h a s", s=128)),
              reads=["shilo"], writes=["fl1"], dma="f")
            A("gpsimd", (lambda cch=cch: dma(fl_b, ssig[:, cch * 128:(cch + 1) * 128]))(),
              reads=["pk4"], writes=["fl2"], dma="mg")
            A("tensor", (lambda cch=cch: lambda e: e.transpose(
                p[3][0:1, 0:64], scd[:, cch:cch + 1], seyef[0:64, 0:64]))(),
              reads=["pk4", "eyef"], writes=["p3"], tag=f"cdT{cg}")
            A("vector", lambda e: e.tensor_copy(out=sv1[0:1, 0:64],
                                                in_=p[3][0:1, 0:64]),
              reads=["p3"], writes=["sv1"])
            A("tensor", lambda e: e.matmul(p[3][:, 0:64], sonesrow[:], sv1[0:1, 0:64],
                                           start=True, stop=True),
              reads=["sv1", "onesrow"], writes=["p3"])
            A("vector", lambda e: e.tensor_copy(out=cdP[:], in_=p[3][:, 0:64]),
              reads=["p3"], writes=["cdP"])
            A("tensor", (lambda cch=cch: lambda e: e.matmul(
                p[3][:, 0:128], sB_fm[:, cch * 128:(cch + 1) * 128],
                sC_fm[:, cch * 128:(cch + 1) * 128], start=True, stop=True))(),
              reads=["sB_fm", "sC_fm"], writes=["p3"], tag=f"sc{cg}")
            A("vector", lambda e: e.tensor_tensor(out=ssm[:], in0=p[3][:, 0:128],
                                                  in1=smask[:], op=ALU.mult),
              reads=["p3", "mask"], writes=["ssm"])
            A("sync", (lambda t0=t0, tc0=tc0: dma(
                sxch[:].rearrange("p (j t) -> p j t", t=128),
                x_sc[:, :, t0 + tc0:t0 + tc0 + 128].rearrange("j p t -> p j t")))(),
              reads=[("x", st, j) for j in range(32)] + ["sxch"],
              writes=["sxch"], dma="z", tag=f"xl{cg}", cost=3500)
            for jq in range(4):
                for jj in range(8):
                    j = jq * 8 + jj
                    A("tensor", (lambda j=j, jj=jj: lambda e: e.transpose(
                        p2b[:, jj * 128:(jj + 1) * 128],
                        sxch[:, j * 128:(j + 1) * 128], seyeb[:]))(),
                      reads=["sxch", "eyeb"], writes=["p2"], tag=f"xT{cg}_{j}")
                A("scalar", (lambda jq=jq: lambda e: e.copy(
                    out=x_tm[:, jq * 1024:(jq + 1) * 1024], in_=p2b[:, 0:1024]))(),
                  reads=["p2"], writes=["x_tm"], tag=f"xTe{cg}_{jq}")
            A("tensor", (lambda tc0=tc0: lambda e: e.transpose(
                p2b[:, 0:128], sB_fm[:, tc0:tc0 + 128], seyeb[:]))(),
              reads=["sB_fm", "eyeb"], writes=["p2"], tag=f"BT{cg}")
            A("scalar", lambda e: e.copy(out=sBtm[:], in_=p2b[:, 0:128]),
              reads=["p2"], writes=["sBtm"])
            A("tensor", lambda e: e.transpose(p[3][0:128, 0:64], sw_fm[:], seyef[0:64, 0:64]),
              reads=["pk4", "eyef"], writes=["p3"], tag=f"wT{cg}")
            A("vector", lambda e: e.tensor_copy(out=swT[:], in_=p[3][:, 0:64]),
              reads=["p3"], writes=["swT"])
            A("vector", lambda e: e.tensor_tensor(
                out=sxw[:].rearrange("s (h q) -> s h q", q=64),
                in0=x_tm[:].rearrange("s (h q) -> s h q", q=64),
                in1=swT[:, :, None].broadcast_to([128, 64, 64]), op=ALU.mult),
              reads=["x_tm", "swT"], writes=["sxw"], tag=f"xw{cg}")

            for i in range(8):
                ip = i % 2
                for ab in range(2):
                    A("sync", (lambda i=i, ip=ip, ab=ab: dma(
                        lnd5x[ip][2 + 4 * ab:6 + 4 * ab, :]
                        .rearrange("k (g s) -> k g s", g=2),
                        fl_b3[ab].rearrange("(g k) s -> k g s", k=4)
                        [:, 2 * i:2 * i + 2, :]))(),
                      reads=["fl1", ("lnd5", ip)], writes=[("lnd5", ip)], dma="f",
                      tag=f"l5_{cg}_{i}_{ab}")
                A("sync", (lambda i=i: dma(
                    rhs5[0:2, :],
                    fl_b2.rearrange("a h s -> a (h s)")[:, i * 1024:(i + 1) * 1024]))(),
                  reads=["fl0", "rhs5"], writes=["rhs5"], dma="f", tag=f"r5_{cg}_{i}")
                for kh in range(2):
                    A("scalar", (lambda i=i, ip=ip, kh=kh: lambda e: e.dma_start(
                        out=s_sF[ip][64 * kh:64 * (kh + 1), :]
                        .rearrange("p (j l) -> p j l", l=128),
                        in_=fl_b.rearrange("(j k) l -> k j l", k=2)
                        [kh, 4 * i:4 * i + 4, :][None, :, :]
                        .broadcast_to([64, 4, 128])))(),
                      reads=["fl2", ("s_sF", ip)], writes=[("s_sF", ip)],
                      dma="fs", tag=f"sF{cg}_{i}_{kh}")
                A("scalar", (lambda i=i, ip=ip, t0=t0, tc0=tc0: dma(
                    sz_ep[ip][:].rearrange("p (j t) -> p j t", t=128),
                    z_sc[4 * i:4 * i + 4, :, t0 + tc0:t0 + tc0 + 128]
                    .rearrange("j p t -> p j t")))(),
                  reads=[("z", st, 4 * i + jx) for jx in range(4)] + [("sz_ep", ip)],
                  writes=[("sz_ep", ip)], dma="zs", tag=f"zl{cg}_{i}")
                for gg in range(2):
                    g = 2 * i + gg
                    pd = 3 + gg
                    sl = sSL[gg]
                    A("tensor", (lambda pd=pd, ip=ip, gg=gg: lambda e: e.matmul(
                        p[pd][:, 0:512], lnd5x[ip][:, gg * 128:(gg + 1) * 128],
                        rhs5[:, gg * 512:(gg + 1) * 512], start=True, stop=True))(),
                      reads=[("lnd5", ip), "lnd5r0", "rhs5", "rhs5c"],
                      writes=[f"p{pd}"], tag=f"D{cg}_{g}", cost=350)
                    A("vector", (lambda pd=pd, gg=gg: lambda e: e.tensor_scalar(
                        out=smind2[gg][:], in0=p[pd][:, 0:512], scalar1=30.0,
                        scalar2=None, op0=ALU.min))(),
                      reads=[f"p{pd}"], writes=[("smind", gg)], tag=f"Dm{cg}_{g}")
                    A("scalar", (lambda gg=gg: lambda e: e.activation(
                        out=sexpD2[gg][:], in_=smind2[gg][:], func=AF.Exp))(),
                      reads=[("smind", gg)], writes=[("sexpD", gg)], tag=f"De{cg}_{g}")
                    A("vector", (lambda sl=sl, gg=gg: lambda e: e.tensor_tensor(
                        out=sl[:].rearrange("s (h l) -> s h l", l=128),
                        in0=sexpD2[gg][:].rearrange("s (h l) -> s h l", l=128),
                        in1=ssm[:, None, :].broadcast_to([128, 4, 128]),
                        op=ALU.mult))(),
                      reads=[("sexpD", gg), "ssm"], writes=[("sSL", gg)], tag=f"SL{cg}_{g}")
                def ydiag(i=i):
                    def f(e):
                        r = None
                        for jj in range(4):
                            pr = 4 * i + jj
                            gg, kk = jj // 2, jj % 2
                            sl = sSL[gg]
                            r = e.matmul(p[6][0:64, jj * 128:(jj + 1) * 128],
                                         x_tm[:, pr * 128:pr * 128 + 64],
                                         sl[:, (2 * kk) * 128:(2 * kk + 1) * 128],
                                         start=True, stop=True, tile_position=(0, 0))
                            r = e.matmul(p[6][64:128, jj * 128:(jj + 1) * 128],
                                         x_tm[:, pr * 128 + 64:pr * 128 + 128],
                                         sl[:, (2 * kk + 1) * 128:(2 * kk + 2) * 128],
                                         start=True, stop=True, tile_position=(0, 64))
                        return r
                    return f
                A("tensor", ydiag(), reads=["x_tm", ("sSL", 0), ("sSL", 1)],
                  writes=["p6"], tag=f"Yd{cg}_{i}", cost=900)
                A("vector", (lambda i=i: lambda e: e.tensor_copy(
                    out=scast[:], in_=S[:, i * 512:(i + 1) * 512]))(),
                  reads=[("S", i)], writes=["scast"], tag=f"Sc{cg}_{i}")
                def yoff(i=i, cch=cch):
                    def f(e):
                        r = None
                        for jj in range(4):
                            for hh in range(2):
                                base = (2 * jj + hh) * 64
                                r = e.matmul(
                                    p[7][hh * 64:(hh + 1) * 64, jj * 128:(jj + 1) * 128],
                                    scast[:, base:base + 64],
                                    sC_fm[:, cch * 128:(cch + 1) * 128],
                                    start=True, stop=True,
                                    tile_position=(0, hh * 64))
                        return r
                    return f
                A("tensor", yoff(), reads=["scast", "sC_fm"], writes=["p7"],
                  tag=f"Yo{cg}_{i}", cost=900)
                A("tensor", (lambda i=i: lambda e: e.matmul(
                    p[5][:, 0:512], sBtm[:], sxw[:, i * 512:(i + 1) * 512],
                    start=True, stop=True))(),
                  reads=["sBtm", "sxw"], writes=["p5"], tag=f"St{cg}_{i}")
                A("vector", (lambda i=i: lambda e: e.tensor_tensor(
                    out=sscan[:].rearrange("n (h q) -> n h q", q=64),
                    in0=S[:, i * 512:(i + 1) * 512].rearrange("n (h q) -> n h q", q=64),
                    in1=cdP[:, 8 * i:8 * i + 8, None].broadcast_to([128, 8, 64]),
                    op=ALU.mult))(),
                  reads=[("S", i), "cdP", "scast"], writes=["sscan"], tag=f"scn{cg}_{i}")
                A("vector", (lambda i=i: lambda e: e.tensor_add(
                    S[:, i * 512:(i + 1) * 512], sscan[:], p[5][:, 0:512]))(),
                  reads=["sscan", "p5"], writes=[("S", i)], tag=f"scn2{cg}_{i}")
                A("vector", (lambda ip=ip: lambda e: e.tensor_tensor(
                    out=sta[:], in0=p[7][:, 0:512], in1=s_sF[ip][:], op=ALU.mult))(),
                  reads=["p7", ("s_sF", ip)], writes=["sta"], tag=f"e1_{cg}_{i}")
                A("vector", lambda e: e.tensor_add(stb[:], sta[:], p[6][:, 0:512]),
                  reads=["sta", "p6"], writes=["stb"], tag=f"e2_{cg}_{i}")
                A("gpsimd", (lambda i=i: lambda e: e.tensor_tensor(
                    out=sxd[:].rearrange("p (j l) -> p j l", l=128),
                    in0=sxch[:].rearrange("p (j t) -> p j t", t=128)
                    [:, 4 * i:4 * i + 4, :],
                    in1=sDp[:, 4 * i:4 * i + 4, None].broadcast_to([128, 4, 128]),
                    op=ALU.mult))(),
                  reads=["sxch", "dp"], writes=["sxd"], tag=f"e3_{cg}_{i}")
                A("gpsimd", lambda e: e.tensor_add(sta[:], stb[:], sxd[:]),
                  reads=["stb", "sxd"], writes=["sta"], tag=f"e4_{cg}_{i}")
                A("scalar", (lambda ip=ip: lambda e: e.activation(
                    out=sg[:], in_=sz_ep[ip][:], func=AF.Silu))(),
                  reads=[("sz_ep", ip)], writes=["sg"], tag=f"e5_{cg}_{i}")
                A("vector", (lambda i=i: lambda e: e.tensor_mul(
                    sy[:, i * 512:(i + 1) * 512], sta[:], sg[:]))(),
                  reads=["sta", "sg"], writes=["sy"], tag=f"e6_{cg}_{i}")
                A("scalar", (lambda i=i, ip=ip: lambda e: e.activation(
                    out=sy2[ip][:], in_=sy[:, i * 512:(i + 1) * 512], func=AF.Square))(),
                  reads=["sy"], writes=[("sy2", ip)], tag=f"e7_{cg}_{i}")
                def em_stats(i):
                    ipx = i % 2
                    def stats(e):
                        r = None
                        for jj in range(4):
                            r = e.matmul(p[2][0:1, 0:128], sonescol[:],
                                         sy2[ipx][:, jj * 128:(jj + 1) * 128],
                                         start=(jj == 0), stop=(jj == 3))
                        return r
                    A("tensor", stats, reads=[("sy2", ipx), "onescol"], writes=["p2"],
                      tag=f"st2_{cg}_{i}")
                    spar = i % 2
                    if i == 0:
                        A("vector", lambda e: e.tensor_copy(out=sstat[0][:],
                                                            in_=p[2][0:1, 0:128]),
                          reads=["p2"], writes=[("sstat", 0)], tag=f"sa{cg}_{i}")
                    else:
                        A("vector", (lambda spar=spar: lambda e: e.tensor_add(
                            sstat[spar][:], sstat[1 - spar][:], p[2][0:1, 0:128]))(),
                          reads=[("sstat", 1 - spar), "p2"], writes=[("sstat", spar)],
                          tag=f"sa{cg}_{i}")
                if i > 0:
                    em_stats(i - 1)
                if i == 7:
                    em_stats(7)
            A("vector", lambda e: e.tensor_scalar(
                out=sv1[:], in0=sstat[1][:], scalar1=1.0 / DI, scalar2=EPS,
                op0=ALU.mult, op1=ALU.add),
              reads=[("sstat", 1)], writes=["sv1"], tag=f"rms{cg}")
            A("scalar", lambda e: e.activation(out=sv2[:], in_=sv1[:], func=AF.Ln),
              reads=["sv1"], writes=["sv2"])
            A("scalar", lambda e: e.activation(out=srstd1[:], in_=sv2[:], func=AF.Exp,
                                               scale=sconstm05[0:1, :]),
              reads=["sv2", "cpool"], writes=["srstd1"])
            A("tensor", lambda e: e.matmul(p[3][:, 0:128], sonesrow[:], srstd1[:],
                                           start=True, stop=True),
              reads=["srstd1", "onesrow"], writes=["p3"], tag=f"rE{cg}")
            for i in range(8):
                A("vector", (lambda i=i, tc0=tc0: lambda e: e.tensor_tensor(
                    out=syn[:].rearrange("p (j t) -> p j t", t=T)
                    [:, 4 * i:4 * i + 4, tc0:tc0 + 128],
                    in0=sy[:, i * 512:(i + 1) * 512].rearrange("p (j l) -> p j l", l=128),
                    in1=p[3][:, None, 0:128].broadcast_to([128, 4, 128]),
                    op=ALU.mult))(),
                  reads=["sy", "p3"], writes=["syn"], tag=f"yn{cg}_{i}")

        for half in range(2):
            tcs = (2 * half, 2 * half + 1)
            wo3 = [wo_sb[0], wo_sb[1], sy[:, 0:2048]]
            for k in range(32):
                par = k % 3
                wtok = ("wo_sb", par) if par < 2 else "sy"
                if st == 0 and half == 0:
                    A("gpsimd", (lambda k=k, par=par: dma(wo3[par][:], wo_t[k]))(),
                      reads=[wtok], writes=[wtok], dma="wog", tag=f"woL{k}")
                    A("vector", (lambda k=k, par=par: lambda e: e.tensor_scalar(
                        out=wo3[par][:], in0=wo3[par][:], scalar1=snw[:, k:k + 1],
                        scalar2=None, op0=ALU.mult))(),
                      reads=[wtok, "nw"], writes=[wtok], tag=f"woF{k}")
                    A("sync", (lambda k=k, par=par: dma(wo_sc[k], wo3[par][:]))(),
                      reads=[wtok], writes=[("wosc", k)], dma="wo")
                else:
                    A("sync", (lambda k=k, par=par: dma(wo3[par][:], wo_sc[k]))(),
                      reads=[wtok, ("wosc", k)], writes=[wtok], dma="wo",
                      tag=f"woL{st}_{half}_{k}")
                def opmm(k=k, par=par, tcs=tcs):
                    def f(e):
                        r = None
                        for ti, tcx in enumerate(tcs):
                            for o in range(4):
                                r = e.matmul(p[ti * 4 + o][:, 0:512],
                                             syn[:, k * T + tcx * 128:k * T + tcx * 128 + 128],
                                             wo3[par][:, o * 512:(o + 1) * 512],
                                             start=(k == 0), stop=(k == 31))
                        return r
                    return f
                A("tensor", opmm(), reads=[wtok, "syn"],
                  writes=[f"p{b}" for b in range(8)], tag=f"op{st}_{half}_{k}", cost=1900)
            for ti, tcx in enumerate(tcs):
                x1t = x1[ti]
                for o in range(4):
                    rpar = o % 2
                    A("sync", (lambda t0=t0, tcx=tcx, o=o, rpar=rpar: dma(
                        su_res[rpar][:],
                        u_tm[t0 + tcx * 128:t0 + (tcx + 1) * 128,
                             o * 512:(o + 1) * 512]))(),
                      reads=[("su_res", rpar)], writes=[("su_res", rpar)], dma="u",
                      tag=f"ur{st}_{tcx}_{o}")
                    A("vector", (lambda ti=ti, o=o, rpar=rpar, x1t=x1t: lambda e:
                      e.tensor_add(x1t[:, o * 512:(o + 1) * 512],
                                   p[ti * 4 + o][:, 0:512], su_res[rpar][:]))(),
                      reads=[f"p{ti * 4 + o}", ("su_res", rpar)],
                      writes=[("x1", ti)], tag=f"h1{st}_{tcx}_{o}")
                A("scalar", (lambda x1t=x1t, ti=ti: lambda e: e.activation(
                    out=x_tm[:, 0:2048], in_=x1t[:], func=AF.Copy,
                    accum_out=ssum2_[ti][:]))(),
                  reads=[("x1", ti)], writes=[("ssum", ti), "x_tm"], tag=f"ls{st}_{tcx}")
                A("scalar", (lambda x1t=x1t, ti=ti: lambda e: e.activation(
                    out=x_tm[:, 0:2048], in_=x1t[:], func=AF.Square,
                    accum_out=ssq2_[ti][:]))(),
                  reads=[("x1", ti)], writes=[("ssq", ti), "x_tm"])
                A("vector", (lambda ti=ti: lambda e: e.tensor_scalar_mul(smean2_[ti][:], ssum2_[ti][:], 1.0 / DM))(),
                  reads=[("ssum", ti)], writes=[("smean", ti)])
                A("vector", (lambda ti=ti: lambda e: e.tensor_scalar(
                    out=sm22_[ti][:], in0=smean2_[ti][:], scalar1=smean2_[ti][:], scalar2=EPS,
                    op0=ALU.mult, op1=ALU.subtract))(),
                  reads=[("smean", ti)], writes=[("sm2", ti)])
                A("vector", (lambda ti=ti: lambda e: e.tensor_scalar(
                    out=svar2_[ti][:], in0=ssq2_[ti][:], scalar1=1.0 / DM, scalar2=None,
                    op0=ALU.mult))(),
                  reads=[("ssq", ti)], writes=[("svar", ti)])
                A("vector", (lambda ti=ti: lambda e: e.tensor_sub(svar2_[ti][:], svar2_[ti][:], sm22_[ti][:]))(),
                  reads=[("svar", ti), ("sm2", ti)], writes=[("svar", ti)])
                A("scalar", (lambda ti=ti: lambda e: e.activation(out=svar2_[ti][:], in_=svar2_[ti][:], func=AF.Ln))(),
                  reads=[("svar", ti)], writes=[("svar", ti)])
                A("scalar", (lambda ti=ti: lambda e: e.activation(out=srstd2_[ti][:], in_=svar2_[ti][:],
                                                   func=AF.Exp, scale=sconstm05[:]))(),
                  reads=[("svar", ti), "cpool"], writes=[("srstd", ti)])
                A("vector", (lambda x1t=x1t, ti=ti: lambda e: e.tensor_scalar(
                    out=x1t[:], in0=x1t[:], scalar1=smean2_[ti][:], scalar2=srstd2_[ti][:],
                    op0=ALU.subtract, op1=ALU.mult))(),
                  reads=[("smean", ti), ("srstd", ti), ("x1", ti)], writes=[("x1", ti)],
                  tag=f"ln1_{st}_{tcx}")
                A("vector", (lambda x1t=x1t: lambda e: e.tensor_reduce(
                    out=samax[:, :, None],
                    in_=x1t[:].rearrange("p (a b) -> p a b", b=128),
                    axis=mybir.AxisListType.X, op=ALU.max,
                    apply_absolute_value=True))(),
                  reads=[("x1", ti)], writes=["samax"], tag=f"qa{st}_{tcx}")
                A("vector", lambda e: e.tensor_scalar_max(samax[:], samax[:], 1e-4),
                  reads=["samax"], writes=["samax"])
                A("vector", lambda e: e.reciprocal(out=srcp[:], in_=samax[:]),
                  reads=["samax"], writes=["srcp"])
                A("vector", lambda e: e.tensor_scalar_mul(sr224[:], srcp[:], 224.0),
                  reads=["srcp"], writes=["sr224"])
                A("vector", lambda e: e.tensor_scalar_mul(sa224[:], samax[:], 1.0 / 224.0),
                  reads=["samax"], writes=["sa224"])
                A("vector", (lambda x1t=x1t: lambda e: e.tensor_tensor(
                    out=sq8[:].rearrange("p (a b) -> p a b", b=128),
                    in0=x1t[:].rearrange("p (a b) -> p a b", b=128),
                    in1=sr224[:, :, None].broadcast_to([128, 16, 128]), op=ALU.mult))(),
                  reads=[("x1", ti), "sr224"], writes=["sy"], tag=f"q8{st}_{tcx}")
                A("vector", lambda e: e.tensor_tensor(
                    out=sxdq[:].rearrange("p (a b) -> p a b", b=128),
                    in0=sq8[:].rearrange("p (a b) -> p a b", b=128),
                    in1=sa224[:, :, None].broadcast_to([128, 16, 128]), op=ALU.mult),
                  reads=["sy", "sa224"], writes=["sxdq"], tag=f"dq{st}_{tcx}")
                for iq in range(2):
                    for ii in range(8):
                        i8 = iq * 8 + ii
                        A("tensor", (lambda i8=i8, ii=ii: lambda e: e.transpose(
                            p2b[:, ii * 128:(ii + 1) * 128],
                            sxdq[:, i8 * 128:(i8 + 1) * 128], seyeb[:]))(),
                          reads=["sxdq", "eyeb"], writes=["p2"],
                          tag=f"qT{st}_{tcx}_{i8}")
                    A("scalar", (lambda ti=ti, iq=iq: lambda e: e.copy(
                        out=sxdq_fm[ti][:, iq * 1024:(iq + 1) * 1024],
                        in_=p2b[:, 0:1024]))(),
                      reads=["p2"], writes=[("xdq", ti)], tag=f"qTe{st}_{tcx}_{iq}")
            wdq3 = [wdq_sb[0], wdq_sb[1], sy[:, 2048:3072]]
            for ohalf in range(2):
                for i in range(16):
                    par = i % 3
                    wtok = ("wdq", par) if par < 2 else "sy"
                    A("sync", (lambda i=i, par=par, ohalf=ohalf: dma(
                        wdq3[par][:], wf_sc[i][:, ohalf * 1024:(ohalf + 1) * 1024]))(),
                      reads=[wtok, ("wf", i)], writes=[wtok], dma="wo",
                      tag=f"wfL{st}_{half}_{ohalf}_{i}")
                    def ffmm(i=i, par=par):
                        def f(e):
                            r = None
                            for ti in range(2):
                                for o2 in range(2):
                                    r = e.matmul(p[4 + ti * 2 + o2][:, 0:512],
                                                 sxdq_fm[ti][:, i * 128:(i + 1) * 128],
                                                 wdq3[par][:, o2 * 512:(o2 + 1) * 512],
                                                 start=(i == 0), stop=(i == 15))
                            return r
                        return f
                    A("tensor", ffmm(), reads=[wtok, ("xdq", 0), ("xdq", 1)],
                      writes=[f"p{b}" for b in range(4, 8)], tag=f"ff{st}_{half}_{ohalf}_{i}", cost=1000)
                for ti in range(2):
                    sh2x = sh2 if ti == 0 else sh2b
                    htok = "sxw" if ti == 0 else "sy"
                    for o2 in range(2):
                        o = ohalf * 2 + o2
                        A("scalar", (lambda ti=ti, o2=o2: lambda e: e.copy(
                            out=sfb[:], in_=p[4 + ti * 2 + o2][:, 0:512]))(),
                          reads=[f"p{4 + ti * 2 + o2}"], writes=["scast"],
                          tag=f"fb{st}_{half}_{ti}_{o}")
                        A("vector", (lambda ti=ti, o=o, sh2x=sh2x, htok=htok: lambda e:
                          e.tensor_add(sh2x[:, o * 512:(o + 1) * 512],
                                       x1[ti][:, o * 512:(o + 1) * 512], sfb[:]))(),
                          reads=["scast", ("x1", ti)], writes=[htok],
                          tag=f"h2{st}_{half}_{ti}_{o}")
            for ti, tcx in enumerate(tcs):
                sh2x = sh2 if ti == 0 else sh2b
                htok = "sxw" if ti == 0 else "sy"
                A("scalar", (lambda sh2x=sh2x, ti=ti: lambda e: e.activation(
                    out=x_tm[:, 0:2048], in_=sh2x[:], func=AF.Copy,
                    accum_out=ssum2_[ti][:]))(),
                  reads=[htok], writes=[("ssum", ti), "x_tm"])
                A("scalar", (lambda sh2x=sh2x, ti=ti: lambda e: e.activation(
                    out=x_tm[:, 0:2048], in_=sh2x[:], func=AF.Square,
                    accum_out=ssq2_[ti][:]))(),
                  reads=[htok], writes=[("ssq", ti), "x_tm"])
                A("vector", (lambda ti=ti: lambda e: e.tensor_scalar_mul(smean2_[ti][:], ssum2_[ti][:], 1.0 / DM))(),
                  reads=[("ssum", ti)], writes=[("smean", ti)])
                A("vector", (lambda ti=ti: lambda e: e.tensor_scalar(
                    out=sm22_[ti][:], in0=smean2_[ti][:], scalar1=smean2_[ti][:], scalar2=EPS,
                    op0=ALU.mult, op1=ALU.subtract))(),
                  reads=[("smean", ti)], writes=[("sm2", ti)])
                A("vector", (lambda ti=ti: lambda e: e.tensor_scalar(
                    out=svar2_[ti][:], in0=ssq2_[ti][:], scalar1=1.0 / DM, scalar2=None,
                    op0=ALU.mult))(),
                  reads=[("ssq", ti)], writes=[("svar", ti)])
                A("vector", (lambda ti=ti: lambda e: e.tensor_sub(svar2_[ti][:], svar2_[ti][:], sm22_[ti][:]))(),
                  reads=[("svar", ti), ("sm2", ti)], writes=[("svar", ti)])
                A("scalar", (lambda ti=ti: lambda e: e.activation(out=svar2_[ti][:], in_=svar2_[ti][:],
                                                   func=AF.Ln))(),
                  reads=[("svar", ti)], writes=[("svar", ti)])
                A("scalar", (lambda ti=ti: lambda e: e.activation(out=srstd2_[ti][:], in_=svar2_[ti][:],
                                                   func=AF.Exp, scale=sconstm05[:]))(),
                  reads=[("svar", ti), "cpool"], writes=[("srstd", ti)])
                for o in range(4):
                    opar = o % 2
                    A("vector", (lambda o=o, opar=opar, sh2x=sh2x, htok=htok, ti=ti: lambda e:
                      e.tensor_scalar(
                        out=sout[opar][:], in0=sh2x[:, o * 512:(o + 1) * 512],
                        scalar1=smean2_[ti][:], scalar2=srstd2_[ti][:],
                        op0=ALU.subtract, op1=ALU.mult))(),
                      reads=[htok, ("smean", ti), ("srstd", ti),
                             ("smind", 0) if opar == 0 else "sscan"],
                      writes=[("smind", 0) if opar == 0 else "sscan"],
                      tag=f"ln2_{st}_{tcx}_{o}")
                    A("sync", (lambda t0=t0, tcx=tcx, o=o, opar=opar: dma(
                        out[t0 + tcx * 128:t0 + (tcx + 1) * 128,
                            o * 512:(o + 1) * 512], sout[opar][:]))(),
                      reads=[("smind", 0) if opar == 0 else "sscan"],
                      writes=[("smind", 0) if opar == 0 else "sscan"],
                      dma="out", tag=f"out{st}_{tcx}_{o}")

    s.emit(nc, stack)
    return nc, stack


def _host_prep(inputs, NST=8):
    L = NST * 512
    u = np.ascontiguousarray(np.asarray(inputs["L0_attn_u"], dtype=np.float32))
    Wi = np.asarray(inputs["L0_attn_in_proj_kernel"], dtype=np.float32)
    Wo = np.asarray(inputs["L0_attn_out_proj_kernel"], dtype=np.float32)
    Wf = np.asarray(inputs["L0_ffn_weight"], dtype=np.float32)
    convw = np.asarray(inputs["L0_attn_conv1d_weight"], dtype=np.float32)
    convb = np.asarray(inputs["L0_attn_conv1d_bias"], dtype=np.float32)
    dtb = np.asarray(inputs["L0_attn_dt_bias"], dtype=np.float32)
    alog = np.asarray(inputs["L0_attn_A_log"], dtype=np.float32)
    Dv = np.asarray(inputs["L0_attn_D"], dtype=np.float32)
    nw = np.asarray(inputs["L0_attn_norm_weight"], dtype=np.float32)

    Wi_p = np.zeros((2048, NCT * 128), dtype=np.float32)
    Wi_p[:, :DIP] = Wi
    wi_t = np.ascontiguousarray(
        Wi_p.reshape(16, 128, NCT, 128).transpose(2, 1, 0, 3).reshape(NCT, 128, 2048))
    wo_t = np.ascontiguousarray(Wo.reshape(32, 128, DM))
    wf_t = np.ascontiguousarray(Wf.T.reshape(16, 128, 2048))
    convw_t = np.ascontiguousarray(convw.reshape(CW, 128, 4).transpose(1, 0, 2))
    convb_t = np.ascontiguousarray(convb.reshape(CW, 128).T)
    dp = np.empty((128, 32), dtype=np.float32)
    for pp in range(128):
        for j in range(32):
            dp[pp, j] = Dv[2 * j + pp // 64]
    nw_t = np.ascontiguousarray(nw.reshape(32, 128).T)
    eye_b = np.eye(128, dtype=ml_dtypes.bfloat16)
    eye_f = np.eye(128, dtype=np.float32)
    m01 = np.tril(np.ones((128, 128), np.float32)).T.astype(ml_dtypes.bfloat16)
    ones_col_b = np.ones((128, 1), dtype=ml_dtypes.bfloat16)
    ones_row_f = np.ones((1, 128), dtype=np.float32)
    ind4 = np.zeros((8, 2, 4, 128), dtype=np.float32)
    for k in range(4):
        ind4[k, :, k, :] = 1.0
        ind4[4 + k, :, k, :] = 1.0
    ind4 = ind4.reshape(8, 1024).astype(ml_dtypes.bfloat16)

    shared = dict(wi_t=wi_t, wo_t=wo_t, wf_t=wf_t, convw=convw_t, convb=convb_t,
                  dtb=dtb.reshape(64, 1), alog=alog.reshape(64, 1), dp_in=dp,
                  normw=nw_t, eye_b=eye_b, eye_f=eye_f, mask01=m01,
                  ones_col_b=ones_col_b, ones_row_f=ones_row_f, ind4=ind4)
    in_maps = []
    for b in range(u.shape[0]):
        m = dict(shared)
        m["u_t"] = np.ascontiguousarray(u[b, :L].T.reshape(16, 128, L))
        m["u_tm"] = np.ascontiguousarray(u[b, :L])
        in_maps.append(m)
    return in_maps


def kernel(**inputs):
    NST = 8
    nc, stack = build_program(NST)
    in_maps = _host_prep(inputs, NST)
    try:
        res = run_bass_kernel_spmd(nc, in_maps, core_ids=list(range(len(in_maps))))
    finally:
        stack.close()
    return np.stack([np.asarray(r["out"], dtype=np.float32) for r in res.results])


# version 21
# speedup vs baseline: 1.0245x; 1.0245x over previous
import numpy as np
import ml_dtypes
import concourse.bass as bass
import concourse.mybir as mybir
from concourse.bass_utils import run_bass_kernel_spmd

F32 = mybir.dt.float32
BF16 = mybir.dt.bfloat16
F8 = mybir.dt.float8e4
AF = mybir.ActivationFunctionType
ALU = mybir.AluOpType

DM = 2048
DIP = 8512
DI = 4096
NCT = 67
NK = DM // 128
CW = 34
EPS = 1e-5


class Sched:

    DEF_COST = {"sync": 1200, "gpsimd": 1600, "vector": 650, "scalar": 650,
                "tensor": 400}

    def __init__(self):
        self.ops = []
        self.last_write = {}
        self.readers = {}

    def add(self, engine, fn, reads=(), writes=(), dma=None, tag="", cost=None):
        idx = len(self.ops)
        deps = set()
        for t in reads:
            if t in self.last_write:
                deps.add(self.last_write[t])
        for t in writes:
            if t in self.last_write:
                deps.add(self.last_write[t])
            for r in self.readers.get(t, ()):
                deps.add(r)
        self.ops.append(dict(i=idx, engine=engine, fn=fn, deps=deps, dma=dma,
                             inc=False, tag=tag, r=tuple(reads), w=tuple(writes),
                             cost=cost if cost else self.DEF_COST[engine]))
        for t in reads:
            self.readers.setdefault(t, []).append(idx)
        for t in writes:
            self.last_write[t] = idx
            self.readers[t] = []
        return idx

    def reorder(self):
        import heapq
        ops = self.ops
        n = len(ops)
        assert all(op["i"] == k for k, op in enumerate(ops))
        children = [[] for _ in range(n)]
        byid0 = {op["i"]: op for op in ops}
        ndep = [0] * n
        for op in ops:
            for d in op["deps"]:
                children[d].append(op["i"])
                ndep[op["i"]] += 1
        LAT = 500
        ISSUE = 600
        cp = [0] * n
        for i in range(n - 1, -1, -1):
            m = 0
            ei = ops[i]["engine"]
            for c in children[i]:
                v = cp[c] + (LAT if ops[c]["engine"] != ei else 0)
                if v > m:
                    m = v
            cp[i] = ops[i]["cost"] + m
        engines = ["sync", "gpsimd", "vector", "scalar", "tensor"]
        ready = {e: [] for e in engines}
        for op in ops:
            if ndep[op["i"]] == 0:
                heapq.heappush(ready[op["engine"]], (-cp[op["i"]], op["i"]))
        etime = {e: 0 for e in engines}
        done = [0] * n
        order = []
        remaining = n
        while remaining:
            best = None
            for e in engines:
                while ready[e]:
                    i = ready[e][0][1]
                    break
                if not ready[e]:
                    continue
                i = ready[e][0][1]
                start = etime[e]
                for d in ops[i]["deps"]:
                    dd = done[d] + (LAT if byid0[d]["engine"] != e else 0)
                    if dd > start:
                        start = dd
                if best is None or start < best[0]:
                    best = (start, e, i)
            start, e, i = best
            heapq.heappop(ready[e])
            fin = start + ops[i]["cost"]
            if ops[i]["dma"] is not None:
                etime[e] = start + ISSUE
            else:
                etime[e] = fin
            done[i] = fin
            order.append(i)
            remaining -= 1
            for c in children[i]:
                ndep[c] -= 1
                if ndep[c] == 0:
                    heapq.heappush(ready[ops[c]["engine"]], (-cp[c], c))
        self.ops = [ops[i] for i in order]
        return max(etime.values())

    def emit(self, nc, stack):
        est = self.reorder()
        import os as _os
        if _os.environ.get("DBG_TOK"):
            t = _os.environ["DBG_TOK"]
            for k, op in enumerate(self.ops):
                if t in op["r"] or t in op["w"]:
                    rw = "W" if t in op["w"] else "R"
                    print(f"[tok] pos={k} id={op['i']} {rw} {op['engine']:7s} "
                          f"tag={op['tag']} r={op['r']} w={op['w']}")
        print(f"[sched] est span {est/1e6:.2f} ms over {len(self.ops)} ops")
        ops = self.ops
        byid = {op["i"]: op for op in ops}
        for op in ops:
            for d in op["deps"]:
                byid[d]["inc"] = True
        for op in ops:
            if op["dma"] is not None:
                op["inc"] = True
        engines = ["sync", "gpsimd", "vector", "scalar", "tensor"]
        dma_groups = sorted({op["dma"] for op in ops if op["dma"] is not None})
        R = 4
        esem = {e: stack.enter_context(nc.semaphore(f"e_{e}")) for e in engines}
        dsem = {(g, r): stack.enter_context(nc.semaphore(f"d_{g}{r}"))
                for g in dma_groups for r in range(R)}
        ecnt = {e: 0 for e in engines}
        dcnt = {g: 0 for g in dma_groups}
        for op in ops:
            if op["dma"] is not None:
                k = dcnt[op["dma"]]
                dcnt[op["dma"]] += 1
                sub, rnd = k % R, k // R
                op["ev"] = ("d", (op["dma"], sub), 16 * (rnd + 1))
                op["issue_wait"] = ((op["dma"], sub), 16 * rnd) if rnd else None
            elif op["inc"]:
                ecnt[op["engine"]] += 1
                op["ev"] = ("e", op["engine"], ecnt[op["engine"]])
                op["issue_wait"] = None
            else:
                op["ev"] = None
                op["issue_wait"] = None
        wm = {}
        for op in ops:
            need = {}
            for d in op["deps"]:
                pr = byid[d]
                if pr["ev"] is None:
                    continue
                kind, key, val = pr["ev"]
                sk = (kind, key)
                need[sk] = max(need.get(sk, 0), val)
            if op["issue_wait"] is not None:
                sk, val = ("d", op["issue_wait"][0]), op["issue_wait"][1]
                need[sk] = max(need.get(sk, 0), val)
            waits = []
            for sk, val in sorted(need.items(), key=str):
                w = (op["engine"], sk)
                if wm.get(w, -1) >= val:
                    continue
                wm[w] = val
                sem = esem[sk[1]] if sk[0] == "e" else dsem[sk[1]]
                waits.append((sem, val))
            op["waits"] = waits
        self.totals = ([(esem[e], ecnt[e]) for e in engines if ecnt[e]] +
                       [(dsem[(g, r)], 16 * (dcnt[g] // R + (1 if r < dcnt[g] % R else 0)))
                        for g in dma_groups for r in range(R)
                        if dcnt[g] // R + (1 if r < dcnt[g] % R else 0) > 0])
        blk = stack.enter_context(nc.Block())

        def body_for(name):
            def body(e):
                for op in ops:
                    if op["engine"] != name:
                        continue
                    for sem, val in op["waits"]:
                        e.wait_ge(sem, val)
                    inst = op["fn"](e)
                    if op["dma"] is not None:
                        assert inst is not None, f"op {op['i']} {op['tag']} returned None"
                        inst.then_inc(dsem[op["ev"][1]], 16)
                    elif op["inc"]:
                        assert inst is not None, f"op {op['i']} {op['tag']} returned None"
                        inst.then_inc(esem[op["engine"]], 1)
            return body

        blk.sync(body_for("sync"))
        blk.gpsimd(body_for("gpsimd"))
        blk.vector(body_for("vector"))
        blk.scalar(body_for("scalar"))
        blk.tensor(body_for("tensor"))
        totals = self.totals

        def fin(e):
            for sem, val in totals:
                e.wait_ge(sem, val)
        blk.sync(fin)


def build_program(NST=8):
    T = 512
    L = NST * T
    NCH = 4
    nc = bass.Bass("TRN2", target_bir_lowering=False)

    def inp(name, shape, dt=F32):
        return nc.dram_tensor(name, list(shape), dt, kind="ExternalInput").ap()

    u_t = inp("u_t", (NK, 128, L))
    u_tm = inp("u_tm", (L, DM))
    wi_t = inp("wi_t", (NCT, 128, NK * 128))
    wo_t = inp("wo_t", (32, 128, DM))
    wf_t = inp("wf_t", (16, 128, 2048))
    convw = inp("convw", (128, CW, 4))
    convb = inp("convb", (128, CW))
    dtb = inp("dtb", (64, 1))
    alog = inp("alog", (64, 1))
    dp_in = inp("dp_in", (128, 32))
    normw = inp("normw", (128, 32))
    eye_b = inp("eye_b", (128, 128), BF16)
    eye_f = inp("eye_f", (128, 128))
    mask01 = inp("mask01", (128, 128), BF16)
    ones_col_b = inp("ones_col_b", (128, 1), BF16)
    ones_row_f = inp("ones_row_f", (1, 128))
    ind4 = inp("ind4", (8, 1024), BF16)

    out = nc.dram_tensor("out", [L, DM], F32, kind="ExternalOutput").ap()

    wi_sc = nc.dram_tensor("wi_sc", [NCT, 128, NK * 128], BF16, kind="Internal").ap()
    wo_sc = nc.dram_tensor("wo_sc", [32, 128, DM], BF16, kind="Internal").ap()
    wf_sc = nc.dram_tensor("wf_sc", [16, 128, 2048], BF16, kind="Internal").ap()
    z_sc = nc.dram_tensor("z_sc", [32, 128, L], BF16, kind="Internal").ap()
    x_sc = nc.dram_tensor("x_sc", [32, 128, L], BF16, kind="Internal").ap()
    fl_sc = nc.dram_tensor("fl_sc", [3, 64, 128], F32, kind="Internal").ap()
    fl_b = nc.dram_tensor("fl_b", [64, 128], BF16, kind="Internal").ap()
    fl_b2 = nc.dram_tensor("fl_b2", [2, 64, 128], BF16, kind="Internal").ap()
    fl_b3 = nc.dram_tensor("fl_b3", [2, 64, 128], BF16, kind="Internal").ap()

    s = Sched()
    from contextlib import ExitStack
    stack = ExitStack()

    def sb(name, shape, dt):
        return nc.alloc_sbuf_tensor(name, list(shape), dt)

    su_bf = sb("su_bf", [128, NK * T], BF16)
    wi_sb = [sb(f"wi_sb{i}", [128, NK * 128], BF16) for i in range(2)]
    strip2 = [sb(f"strip{i}", [128, 515], BF16) for i in range(2)]
    halo = sb("halo", [128, CW * 3], BF16)
    conv_a = sb("conv_a", [128, T], F32)
    conv_b2 = sb("conv_b2", [128, T], F32)
    sB_fm = sb("sB_fm", [128, T], BF16)
    sC_fm = sb("sC_fm", [128, T], BF16)
    sxst2 = [sb(f"sxst{i}", [128, T], BF16) for i in range(2)]
    sxch = sb("sxch", [128, 32 * 128], BF16)
    sdt0 = sb("sdt0", [64, T], F32)
    sdt = sb("sdt", [64, T], F32)
    sdt_t1 = sb("sdt_t1", [64, T], F32)
    sdt_e = sb("sdt_e", [64, T], F32)
    sacs = sb("sacs", [64, T], F32)
    slnma = sb("slnma", [64, T], F32)
    ssig = sb("ssig", [64, T], F32)
    sw_fm = sb("sw_fm", [64, 128], F32)
    scd = sb("scd", [64, 4], F32)
    sones64 = sb("sones64", [64, 128], F32)
    sconst1 = sb("sconst1", [128, 1], F32)
    sconstm05 = sb("sconstm05", [128, 1], F32)
    sconstm1 = sb("sconstm1", [128, 1], F32)
    lnd5x = [sb(f"lnd5_{i}", [10, 256], BF16) for i in range(2)]
    shilo = sb("shilo", [64, 256], BF16)
    rhs5 = sb("rhs5", [10, 1024], BF16)
    x_tm = sb("x_tm", [128, 32 * 128], BF16)
    sxw = sb("sxw", [128, 32 * 128], BF16)
    sBtm = sb("sBtm", [128, 128], BF16)
    swT = sb("swT", [128, 64], F32)
    ssm = sb("ssm", [128, 128], BF16)
    smind2 = [sb(f"smind{i}", [128, 512], F32) for i in range(2)]
    sexpD2 = [sb(f"sexpD{i}", [128, 512], BF16) for i in range(2)]
    sSL = [sb(f"sSL{i}", [128, 512], BF16) for i in range(2)]
    S = sb("S", [128, 32 * 128], F32)
    scast = sb("scast", [128, 512], BF16)
    cdP = sb("cdP", [128, 64], F32)
    sscan = sb("sscan", [128, 512], F32)
    s_sF = [sb(f"s_sF{i}", [128, 512], BF16) for i in range(2)]
    sta = sb("sta", [128, 512], F32)
    sq8 = sb("sq8", [128, 2048], F8)
    stb = sq8[:].bitcast(F32)
    sxd = sb("sxd", [128, 512], BF16)
    sg = sb("sg", [128, 512], F32)
    sy = sb("sy", [128, 32 * 128], BF16)
    sy2 = [sb(f"sy2_{i}", [128, 512], BF16) for i in range(2)]
    sstat = [sb(f"sstat{i}", [1, 128], F32) for i in range(2)]
    sv1 = sb("sv1", [1, 128], F32)
    sv2 = sdt_e[0:1, 0:128]
    srstd1 = sb("srstd1", [1, 128], F32)
    syn = sb("syn", [128, 32 * T], BF16)
    sz_ep = [sb(f"sz_ep{i}", [128, 512], BF16) for i in range(2)]
    wo_sb = [sb(f"wo_sb{i}", [128, DM], BF16) for i in range(2)]
    su_res = [sb(f"su_res{i}", [128, 512], F32) for i in range(2)]
    x1 = [sb(f"x1_{i}", [128, DM], F32) for i in range(2)]
    ssum2_ = [sb(f"lnsum{i}", [128, 1], F32) for i in range(2)]
    ssq2_ = [sb(f"lnsq{i}", [128, 1], F32) for i in range(2)]
    smean2_ = [sb(f"lnmean{i}", [128, 1], F32) for i in range(2)]
    sm22_ = [sb(f"lnm2{i}", [128, 1], F32) for i in range(2)]
    svar2_ = [sb(f"lnvar{i}", [128, 1], F32) for i in range(2)]
    srstd2_ = [sb(f"lnrstd{i}", [128, 1], F32) for i in range(2)]
    samax2_ = [sb(f"qamax{i}", [128, 16], F32) for i in range(2)]
    srcp2_ = [sb(f"qrcp{i}", [128, 16], F32) for i in range(2)]
    sr2242_ = [sb(f"qr224{i}", [128, 16], F32) for i in range(2)]
    sa2242_ = [sb(f"qa224{i}", [128, 16], F32) for i in range(2)]
    samax = samax2_[0]
    srcp = srcp2_[0]
    sr224 = sr2242_[0]
    sa224 = sa2242_[0]
    sxdq = sb("sxdq", [128, DM], BF16)
    sxdq_fm = [sb(f"sxdq_fm{i}", [128, 16 * 128], BF16) for i in range(2)]
    wdq_sb = [sb(f"wdq_sb{i}", [128, 1024], BF16) for i in range(2)]
    sfb = scast
    sh2 = sxw[:].bitcast(F32)
    sh2b = sy[:].bitcast(F32)
    samx2 = samax
    samx3 = sb("samx3", [16, 1], F32)
    samx4 = sa224[0:1, :]
    swr = srcp
    swri = sr224
    snw = sb("snw", [128, 32], F32)
    sDp = sb("sDp", [128, 32], BF16)
    sconvw = sb("sconvw", [128, CW * 4], F32)
    sconvb = sb("sconvb", [128, CW], F32)
    sdtb = sb("sdtb", [64, 1], F32)
    salog = sb("salog", [64, 1], F32)
    saneg = sb("saneg", [64, 1], F32)
    seyeb = sb("seyeb", [128, 128], BF16)
    seyef = sb("seyef", [128, 128], F32)
    smask = sb("smask", [128, 128], BF16)
    sonescol = sb("sonescol", [128, 1], BF16)
    sonesrow = None

    p = [nc.alloc_psum_tensor(f"p{i}", [128, 512], F32) for i in range(8)]
    p2b = p[2][:].bitcast(BF16)

    A = s.add

    def dma(e_dst, src):
        def f(e):
            return e.dma_start(out=e_dst, in_=src)
        return f

    A("sync", dma(sconvw[:], convw.rearrange("p j k -> p (j k)")), writes=["convw"], dma="m")
    A("sync", dma(sconvb[:], convb), writes=["convb"], dma="m")
    A("sync", dma(sdtb[:], dtb), writes=["dtb"], dma="m")
    A("sync", dma(salog[:], alog), writes=["alog"], dma="m")
    A("gpsimd", dma(sDp[:], dp_in), writes=["dp"], dma="mg")
    A("sync", dma(snw[:], normw), writes=["nw"], dma="m")
    A("sync", dma(seyeb[:], eye_b), writes=["eyeb"], dma="m")
    A("sync", dma(seyef[:], eye_f), writes=["eyef"], dma="m")
    A("sync", dma(smask[:], mask01), writes=["mask"], dma="m")
    A("sync", dma(sonescol[:], ones_col_b), writes=["onescol"], dma="m")
    A("sync", dma(rhs5[2:10, :], ind4), writes=["rhs5c"], dma="m")
    A("vector", lambda e: e.memset(sones64[:], 1.0), writes=["pk4"])
    A("vector", lambda e: e.memset(sconst1[:], 1.0), writes=["cpool"])
    A("vector", lambda e: e.memset(sconstm05[:], -0.5), writes=["cpool"])
    A("vector", lambda e: e.memset(sconstm1[:], -1.0), writes=["cpool"])
    A("vector", lambda e: e.memset(lnd5x[0][0:2, :], 1.0), writes=["lnd5r0"])
    A("vector", lambda e: e.memset(lnd5x[1][0:2, :], 1.0), writes=["lnd5r0"])
    A("vector", lambda e: e.memset(S[:], 0.0), writes=[("S", i) for i in range(8)])
    A("vector", lambda e: e.memset(halo[:], 0.0), writes=["halo"])
    A("scalar", lambda e: e.activation(out=saneg[:], in_=salog[:], func=AF.Exp),
      reads=["alog"], writes=["aneg"])
    A("vector", lambda e: e.tensor_scalar_mul(saneg[:], saneg[:], -1.0),
      reads=["aneg"], writes=["aneg"])

    for i in range(16):
        A("sync", dma(sh2[:], wf_t[i]), reads=["sxw"], writes=["sxw"], dma="w")
        A("vector", lambda e: e.tensor_reduce(
            out=samx2[:, :, None], in_=sh2[:].rearrange("p (a b) -> p a b", b=128),
            axis=mybir.AxisListType.X, op=ALU.max, apply_absolute_value=True),
          reads=["sxw"], writes=["samax"])
        A("tensor", lambda e: e.transpose(p[3][0:16, 0:128], samx2[:], seyef[:]),
          reads=["samax", "eyef"], writes=["p3"])
        A("vector", lambda e: e.tensor_reduce(
            out=samx3[:, None, :], in_=p[3][0:16, None, 0:128],
            axis=mybir.AxisListType.X, op=ALU.max),
          reads=["p3"], writes=["samx3"])
        A("vector", lambda e: e.tensor_scalar_max(samx3[:], samx3[:], 1e-4),
          reads=["samx3"], writes=["samx3"])
        A("tensor", lambda e: e.transpose(p[4][0:1, 0:16], samx3[:], seyef[0:16, 0:16]),
          reads=["samx3", "eyef"], writes=["p4"])
        A("vector", lambda e: e.tensor_copy(out=samx4[:], in_=p[4][0:1, 0:16]),
          reads=["p4"], writes=["sa224"])
        A("tensor", lambda e: e.matmul(p[4][:, 0:16], sones64[0:1, :], samx4[:],
                                       start=True, stop=True),
          reads=["sa224", "pk4"], writes=["p4"])
        A("vector", lambda e: e.reciprocal(out=swr[:], in_=p[4][:, 0:16]),
          reads=["p4"], writes=["srcp"])
        A("vector", lambda e: e.tensor_scalar_mul(swri[:], swr[:], 224.0),
          reads=["srcp"], writes=["sr224"])
        A("vector", lambda e: e.tensor_scalar_mul(swr[:], p[4][:, 0:16], 1.0 / 224.0),
          reads=["p4"], writes=["srcp"])
        A("vector", lambda e: e.tensor_tensor(
            out=sq8[:].rearrange("p (a b) -> p a b", b=128),
            in0=sh2[:].rearrange("p (a b) -> p a b", b=128),
            in1=swri[:, :, None].broadcast_to([128, 16, 128]), op=ALU.mult),
          reads=["sxw", "sr224"], writes=["stb"])
        A("vector", lambda e: e.tensor_tensor(
            out=sxdq[:].rearrange("p (a b) -> p a b", b=128),
            in0=sq8[:].rearrange("p (a b) -> p a b", b=128),
            in1=swr[:, :, None].broadcast_to([128, 16, 128]), op=ALU.mult),
          reads=["stb", "srcp"], writes=["sxdq"])
        A("sync", dma(wf_sc[i], sxdq[:]), reads=["sxdq"], writes=[("wf", i), "sxdq"],
          dma="w")

    for st in range(NST):
        t0 = st * T
        A("gpsimd", dma(su_bf[:], u_t[:, :, t0:t0 + T].rearrange("k p t -> p k t")),
          writes=["su_bf"], dma="ug", tag=f"u{st}", cost=12000)

        wi3 = [wi_sb[0], wi_sb[1], sxdq]
        for c in range(NCT):
            par = c % 3
            wtok = ("wi_sb", par) if par < 2 else "sxdq"
            if st == 0:
                A("gpsimd", (lambda c=c, par=par: dma(wi3[par][:], wi_t[c]))(),
                  reads=[wtok], writes=[wtok], dma="wig", tag=f"wiL{c}")
                A("sync", (lambda c=c, par=par: dma(wi_sc[c], wi3[par][:]))(),
                  reads=[wtok], writes=[("wisc", c)], dma="wi")
            else:
                A("sync", (lambda c=c, par=par: dma(wi3[par][:], wi_sc[c]))(),
                  reads=[wtok, ("wisc", c)], writes=[wtok], dma="wi",
                  tag=f"wiL{st}_{c}")
            pb = c % 2
            def mk_mm(c=c, par=par, pb=pb):
                def f(e):
                    r = None
                    for k in range(NK):
                        r = e.matmul(p[pb][:], wi3[par][:, k * 128:(k + 1) * 128],
                                     su_bf[:, k * T:(k + 1) * T],
                                     start=(k == 0), stop=(k == NK - 1))
                    return r
                return f
            A("tensor", mk_mm(), reads=[wtok, "su_bf"], writes=[f"p{pb}"],
              tag=f"ip{st}_{c}", cost=3600)
            if c < 32:
                sp = c % 2
                A("vector", (lambda pb=pb, sp=sp: lambda e: e.tensor_copy(
                    out=sxst2[sp][:], in_=p[pb][:]))(),
                  reads=[f"p{pb}", ("sxst", sp)], writes=[("sxst", sp)],
                  tag=f"zev{st}_{c}")
                A("sync", (lambda c=c, t0=t0, sp=sp: dma(
                    z_sc[c][:, t0:t0 + T], sxst2[sp][:]))(),
                  reads=[("sxst", sp)], writes=[("z", st, c), ("sxst", sp)], dma="z")
            elif c < 66:
                j = c - 32
                sp = c % 2
                strip = strip2[sp]
                stok = ("strip", sp)
                A("vector", (lambda j=j, strip=strip, sp=sp: lambda e: e.tensor_copy(
                    out=strip[:, 0:3], in_=halo[:, 3 * j:3 * j + 3]))(),
                  reads=["halo", stok], writes=[stok], tag=f"hin{st}_{j}")
                A("scalar", (lambda pb=pb, strip=strip: lambda e: e.copy(
                    out=strip[:, 3:515], in_=p[pb][:]))(),
                  reads=[f"p{pb}", stok], writes=[stok], tag=f"xbev{st}_{j}")
                A("vector", (lambda j=j, strip=strip: lambda e: e.tensor_copy(
                    out=halo[:, 3 * j:3 * j + 3], in_=strip[:, 512:515]))(),
                  reads=[stok], writes=["halo"], tag=f"hout{st}_{j}")
                A("vector", (lambda j=j, strip=strip: lambda e: e.tensor_scalar(
                    out=conv_a[:], in0=strip[:, 0:T], scalar1=sconvw[:, 4 * j:4 * j + 1],
                    scalar2=None, op0=ALU.mult))(),
                  reads=[stok, "convw"], writes=["conv_a"], tag=f"cv0_{st}_{j}")
                for k in (1, 2):
                    src_, dst_ = (conv_a, conv_b2) if k == 1 else (conv_b2, conv_a)
                    A("vector", (lambda j=j, k=k, src_=src_, dst_=dst_, strip=strip:
                      lambda e: e.scalar_tensor_tensor(
                          out=dst_[:], in0=strip[:, k:k + T],
                          scalar=sconvw[:, 4 * j + k:4 * j + k + 1], in1=src_[:],
                          op0=ALU.mult, op1=ALU.add))(),
                      reads=[stok, "convw", "conv_a" if k == 1 else "conv_b2"],
                      writes=["conv_b2" if k == 1 else "conv_a"], tag=f"cv{k}_{st}_{j}")
                A("vector", (lambda j=j, strip=strip: lambda e: e.scalar_tensor_tensor(
                    out=conv_b2[:], in0=strip[:, 3:3 + T],
                    scalar=sconvw[:, 4 * j + 3:4 * j + 4], in1=conv_a[:],
                    op0=ALU.mult, op1=ALU.add))(),
                  reads=[stok, "convw", "conv_a"], writes=["conv_b2"],
                  tag=f"cv3_{st}_{j}")
                if j < 32:
                    A("scalar", (lambda j=j, sp=sp: lambda e: e.activation(
                        out=sxst2[sp][:], in_=conv_b2[:], func=AF.Silu,
                        bias=sconvb[:, j:j + 1]))(),
                      reads=["conv_b2", "convb", ("sxst", sp)], writes=[("sxst", sp)],
                      tag=f"silu{st}_{j}")
                    A("sync", (lambda j=j, t0=t0, sp=sp: dma(
                        x_sc[j][:, t0:t0 + T], sxst2[sp][:]))(),
                      reads=[("sxst", sp)], writes=[("x", st, j), ("sxst", sp)],
                      dma="z")
                elif j == 32:
                    A("scalar", (lambda j=j: lambda e: e.activation(
                        out=sB_fm[:], in_=conv_b2[:], func=AF.Silu,
                        bias=sconvb[:, j:j + 1]))(),
                      reads=["conv_b2", "convb", "sB_fm"], writes=["sB_fm"],
                      tag=f"silu{st}_{j}")
                else:
                    A("scalar", (lambda j=j: lambda e: e.activation(
                        out=sC_fm[:], in_=conv_b2[:], func=AF.Silu,
                        bias=sconvb[:, j:j + 1]))(),
                      reads=["conv_b2", "convb", "sC_fm"], writes=["sC_fm"],
                      tag=f"silu{st}_{j}")
            else:
                A("vector", (lambda pb=pb: lambda e: e.tensor_scalar(
                    out=sdt0[:], in0=p[pb][0:64, :], scalar1=sdtb[:], scalar2=None,
                    op0=ALU.add))(),
                  reads=[f"p{pb}", "dtb"], writes=["pk1"], tag=f"dt0_{st}")
                A("scalar", lambda e: e.activation(out=sdt_t1[:], in_=sdt0[:],
                                                   func=AF.Abs),
                  reads=["pk1"], writes=["pk2"])
                A("scalar", lambda e: e.activation(out=sdt_e[:], in_=sdt_t1[:],
                                                   func=AF.Exp,
                                                   scale=sconstm1[0:64, :]),
                  reads=["pk2", "cpool"], writes=["pk2"])
                A("scalar", lambda e: e.activation(out=sdt_t1[:], in_=sdt_e[:],
                                                   func=AF.Ln, bias=sconst1[0:64, :]),
                  reads=["pk2", "cpool"], writes=["pk2"])
                A("vector", lambda e: e.tensor_scalar(
                    out=sdt_e[:], in0=sdt0[:], scalar1=0.0, scalar2=None,
                    op0=ALU.max),
                  reads=["pk1"], writes=["pk2"])
                A("vector", lambda e: e.tensor_add(sdt[:], sdt_t1[:], sdt_e[:]),
                  reads=["pk2"], writes=["pk1"], tag=f"dt_{st}")

        A("vector", lambda e: e.tensor_scalar(out=sdt0[:], in0=sdt[:], scalar1=1e-30,
                                              scalar2=None, op0=ALU.max),
          reads=["pk1"], writes=["pk1"])
        A("scalar", lambda e: e.activation(out=sdt_e[:], in_=sdt0[:], func=AF.Ln),
          reads=["pk1"], writes=["pk2"])
        A("vector", lambda e: e.tensor_scalar(out=sdt_t1[:], in0=sdt[:], scalar1=saneg[:],
                                              scalar2=None, op0=ALU.mult),
          reads=["pk1", "aneg"], writes=["pk2"], tag=f"dA_{st}")
        for cch in range(NCH):
            A("vector", (lambda cch=cch: lambda e: e.tensor_tensor_scan(
                out=sacs[:, cch * 128:(cch + 1) * 128], data0=sones64[:],
                data1=sdt_t1[:, cch * 128:(cch + 1) * 128], initial=0.0,
                op0=ALU.mult, op1=ALU.add))(),
              reads=["pk2", "pk4"], writes=["pk3"], tag=f"acs{st}_{cch}")
        A("vector", lambda e: e.tensor_sub(slnma[:], sdt_e[:], sacs[:]),
          reads=["pk2", "pk3"], writes=["pk3"])
        A("scalar", lambda e: e.activation(out=ssig[:], in_=sacs[:], func=AF.Exp),
          reads=["pk3"], writes=["pk4"])
        A("scalar", lambda e: e.activation(
            out=scd[:, :, None],
            in_=sacs[:].rearrange("h (c l) -> h c l", l=128)[:, :, 127:128],
            func=AF.Exp),
          reads=["pk3"], writes=["pk4"], tag=f"cd{st}")

        for cch in range(NCH):
            cg = st * NCH + cch
            tc0 = cch * 128
            A("scalar", (lambda cch=cch: lambda e: e.activation(
                out=sw_fm[:], in_=slnma[:, cch * 128:(cch + 1) * 128], func=AF.Exp,
                bias=sacs[:, cch * 128 + 127:cch * 128 + 128]))(),
              reads=["pk3"], writes=["pk4"], tag=f"wfm{cg}")
            A("vector", (lambda cch=cch: lambda e: e.tensor_copy(
                out=shilo[:, 0:128], in_=sacs[:, cch * 128:(cch + 1) * 128]))(),
              reads=["pk3"], writes=["shilo"], tag=f"hiA{cg}")
            A("vector", (lambda cch=cch: lambda e: e.tensor_sub(
                shilo[:, 128:256], sacs[:, cch * 128:(cch + 1) * 128],
                shilo[:, 0:128]))(),
              reads=["pk3", "shilo"], writes=["shilo"], tag=f"loA{cg}")
            A("sync", dma(fl_b2.rearrange("a h s -> h a s"),
                          shilo[:].rearrange("h (a s) -> h a s", s=128)),
              reads=["shilo"], writes=["fl0"], dma="f")
            A("vector", (lambda cch=cch: lambda e: e.tensor_copy(
                out=shilo[:, 0:128], in_=slnma[:, cch * 128:(cch + 1) * 128]))(),
              reads=["pk3", "shilo", "fl0"], writes=["shilo"], tag=f"hiL{cg}")
            A("vector", (lambda cch=cch: lambda e: e.tensor_sub(
                shilo[:, 128:256], slnma[:, cch * 128:(cch + 1) * 128],
                shilo[:, 0:128]))(),
              reads=["pk3", "shilo"], writes=["shilo"], tag=f"loL{cg}")
            A("sync", dma(fl_b3.rearrange("a h s -> h a s"),
                          shilo[:].rearrange("h (a s) -> h a s", s=128)),
              reads=["shilo"], writes=["fl1"], dma="f")
            A("gpsimd", (lambda cch=cch: dma(fl_b, ssig[:, cch * 128:(cch + 1) * 128]))(),
              reads=["pk4"], writes=["fl2"], dma="mg")
            A("tensor", (lambda cch=cch: lambda e: e.transpose(
                p[3][0:1, 0:64], scd[:, cch:cch + 1], seyef[0:64, 0:64]))(),
              reads=["pk4", "eyef"], writes=["p3"], tag=f"cdT{cg}")
            A("vector", lambda e: e.tensor_copy(out=sv1[0:1, 0:64],
                                                in_=p[3][0:1, 0:64]),
              reads=["p3"], writes=["sv1"])
            A("tensor", lambda e: e.matmul(p[3][:, 0:64], sones64[0:1, :], sv1[0:1, 0:64],
                                           start=True, stop=True),
              reads=["sv1", "pk4"], writes=["p3"])
            A("vector", lambda e: e.tensor_copy(out=cdP[:], in_=p[3][:, 0:64]),
              reads=["p3"], writes=["cdP"])
            A("tensor", (lambda cch=cch: lambda e: e.matmul(
                p[3][:, 0:128], sB_fm[:, cch * 128:(cch + 1) * 128],
                sC_fm[:, cch * 128:(cch + 1) * 128], start=True, stop=True))(),
              reads=["sB_fm", "sC_fm"], writes=["p3"], tag=f"sc{cg}")
            A("vector", lambda e: e.tensor_tensor(out=ssm[:], in0=p[3][:, 0:128],
                                                  in1=smask[:], op=ALU.mult),
              reads=["p3", "mask"], writes=["ssm"])
            A("sync", (lambda t0=t0, tc0=tc0: dma(
                sxch[:].rearrange("p (j t) -> p j t", t=128),
                x_sc[:, :, t0 + tc0:t0 + tc0 + 128].rearrange("j p t -> p j t")))(),
              reads=[("x", st, j) for j in range(32)] + ["sxch"],
              writes=["sxch"], dma="z", tag=f"xl{cg}", cost=3500)
            for jq in range(4):
                for jj in range(8):
                    j = jq * 8 + jj
                    A("tensor", (lambda j=j, jj=jj: lambda e: e.transpose(
                        p2b[:, jj * 128:(jj + 1) * 128],
                        sxch[:, j * 128:(j + 1) * 128], seyeb[:]))(),
                      reads=["sxch", "eyeb"], writes=["p2"], tag=f"xT{cg}_{j}")
                A("scalar", (lambda jq=jq: lambda e: e.copy(
                    out=x_tm[:, jq * 1024:(jq + 1) * 1024], in_=p2b[:, 0:1024]))(),
                  reads=["p2"], writes=["x_tm"], tag=f"xTe{cg}_{jq}")
            A("tensor", (lambda tc0=tc0: lambda e: e.transpose(
                p2b[:, 0:128], sB_fm[:, tc0:tc0 + 128], seyeb[:]))(),
              reads=["sB_fm", "eyeb"], writes=["p2"], tag=f"BT{cg}")
            A("scalar", lambda e: e.copy(out=sBtm[:], in_=p2b[:, 0:128]),
              reads=["p2"], writes=["sBtm"])
            A("tensor", lambda e: e.transpose(p[3][0:128, 0:64], sw_fm[:], seyef[0:64, 0:64]),
              reads=["pk4", "eyef"], writes=["p3"], tag=f"wT{cg}")
            A("vector", lambda e: e.tensor_copy(out=swT[:], in_=p[3][:, 0:64]),
              reads=["p3"], writes=["swT"])
            A("vector", lambda e: e.tensor_tensor(
                out=sxw[:].rearrange("s (h q) -> s h q", q=64),
                in0=x_tm[:].rearrange("s (h q) -> s h q", q=64),
                in1=swT[:, :, None].broadcast_to([128, 64, 64]), op=ALU.mult),
              reads=["x_tm", "swT"], writes=["sxw"], tag=f"xw{cg}")

            for i in range(8):
                ip = i % 2
                for ab in range(2):
                    A("sync", (lambda i=i, ip=ip, ab=ab: dma(
                        lnd5x[ip][2 + 4 * ab:6 + 4 * ab, :]
                        .rearrange("k (g s) -> k g s", g=2),
                        fl_b3[ab].rearrange("(g k) s -> k g s", k=4)
                        [:, 2 * i:2 * i + 2, :]))(),
                      reads=["fl1", ("lnd5", ip)], writes=[("lnd5", ip)], dma="f",
                      tag=f"l5_{cg}_{i}_{ab}")
                A("sync", (lambda i=i: dma(
                    rhs5[0:2, :],
                    fl_b2.rearrange("a h s -> a (h s)")[:, i * 1024:(i + 1) * 1024]))(),
                  reads=["fl0", "rhs5"], writes=["rhs5"], dma="f", tag=f"r5_{cg}_{i}")
                for kh in range(2):
                    A("scalar", (lambda i=i, ip=ip, kh=kh: lambda e: e.dma_start(
                        out=s_sF[ip][64 * kh:64 * (kh + 1), :]
                        .rearrange("p (j l) -> p j l", l=128),
                        in_=fl_b.rearrange("(j k) l -> k j l", k=2)
                        [kh, 4 * i:4 * i + 4, :][None, :, :]
                        .broadcast_to([64, 4, 128])))(),
                      reads=["fl2", ("s_sF", ip)], writes=[("s_sF", ip)],
                      dma="fs", tag=f"sF{cg}_{i}_{kh}")
                A("scalar", (lambda i=i, ip=ip, t0=t0, tc0=tc0: dma(
                    sz_ep[ip][:].rearrange("p (j t) -> p j t", t=128),
                    z_sc[4 * i:4 * i + 4, :, t0 + tc0:t0 + tc0 + 128]
                    .rearrange("j p t -> p j t")))(),
                  reads=[("z", st, 4 * i + jx) for jx in range(4)] + [("sz_ep", ip)],
                  writes=[("sz_ep", ip)], dma="zs", tag=f"zl{cg}_{i}")
                for gg in range(2):
                    g = 2 * i + gg
                    pd = 3 + gg
                    sl = sSL[gg]
                    A("tensor", (lambda pd=pd, ip=ip, gg=gg: lambda e: e.matmul(
                        p[pd][:, 0:512], lnd5x[ip][:, gg * 128:(gg + 1) * 128],
                        rhs5[:, gg * 512:(gg + 1) * 512], start=True, stop=True))(),
                      reads=[("lnd5", ip), "lnd5r0", "rhs5", "rhs5c"],
                      writes=[f"p{pd}"], tag=f"D{cg}_{g}", cost=350)
                    A("vector", (lambda pd=pd, gg=gg: lambda e: e.tensor_scalar(
                        out=smind2[gg][:], in0=p[pd][:, 0:512], scalar1=30.0,
                        scalar2=None, op0=ALU.min))(),
                      reads=[f"p{pd}"], writes=[("smind", gg)], tag=f"Dm{cg}_{g}")
                    A("scalar", (lambda gg=gg: lambda e: e.activation(
                        out=sexpD2[gg][:], in_=smind2[gg][:], func=AF.Exp))(),
                      reads=[("smind", gg)], writes=[("sexpD", gg)], tag=f"De{cg}_{g}")
                    A("vector", (lambda sl=sl, gg=gg: lambda e: e.tensor_tensor(
                        out=sl[:].rearrange("s (h l) -> s h l", l=128),
                        in0=sexpD2[gg][:].rearrange("s (h l) -> s h l", l=128),
                        in1=ssm[:, None, :].broadcast_to([128, 4, 128]),
                        op=ALU.mult))(),
                      reads=[("sexpD", gg), "ssm"], writes=[("sSL", gg)], tag=f"SL{cg}_{g}")
                def ydiag(i=i):
                    def f(e):
                        r = None
                        for jj in range(4):
                            pr = 4 * i + jj
                            gg, kk = jj // 2, jj % 2
                            sl = sSL[gg]
                            r = e.matmul(p[6][0:64, jj * 128:(jj + 1) * 128],
                                         x_tm[:, pr * 128:pr * 128 + 64],
                                         sl[:, (2 * kk) * 128:(2 * kk + 1) * 128],
                                         start=True, stop=True, tile_position=(0, 0))
                            r = e.matmul(p[6][64:128, jj * 128:(jj + 1) * 128],
                                         x_tm[:, pr * 128 + 64:pr * 128 + 128],
                                         sl[:, (2 * kk + 1) * 128:(2 * kk + 2) * 128],
                                         start=True, stop=True, tile_position=(0, 64))
                        return r
                    return f
                A("tensor", ydiag(), reads=["x_tm", ("sSL", 0), ("sSL", 1)],
                  writes=["p6"], tag=f"Yd{cg}_{i}", cost=900)
                A("vector", (lambda i=i: lambda e: e.tensor_copy(
                    out=scast[:], in_=S[:, i * 512:(i + 1) * 512]))(),
                  reads=[("S", i)], writes=["scast"], tag=f"Sc{cg}_{i}")
                def yoff(i=i, cch=cch):
                    def f(e):
                        r = None
                        for jj in range(4):
                            for hh in range(2):
                                base = (2 * jj + hh) * 64
                                r = e.matmul(
                                    p[7][hh * 64:(hh + 1) * 64, jj * 128:(jj + 1) * 128],
                                    scast[:, base:base + 64],
                                    sC_fm[:, cch * 128:(cch + 1) * 128],
                                    start=True, stop=True,
                                    tile_position=(0, hh * 64))
                        return r
                    return f
                A("tensor", yoff(), reads=["scast", "sC_fm"], writes=["p7"],
                  tag=f"Yo{cg}_{i}", cost=900)
                A("tensor", (lambda i=i: lambda e: e.matmul(
                    p[5][:, 0:512], sBtm[:], sxw[:, i * 512:(i + 1) * 512],
                    start=True, stop=True))(),
                  reads=["sBtm", "sxw"], writes=["p5"], tag=f"St{cg}_{i}")
                A("vector", (lambda i=i: lambda e: e.tensor_tensor(
                    out=sscan[:].rearrange("n (h q) -> n h q", q=64),
                    in0=S[:, i * 512:(i + 1) * 512].rearrange("n (h q) -> n h q", q=64),
                    in1=cdP[:, 8 * i:8 * i + 8, None].broadcast_to([128, 8, 64]),
                    op=ALU.mult))(),
                  reads=[("S", i), "cdP", "scast"], writes=["sscan"], tag=f"scn{cg}_{i}")
                A("vector", (lambda i=i: lambda e: e.tensor_add(
                    S[:, i * 512:(i + 1) * 512], sscan[:], p[5][:, 0:512]))(),
                  reads=["sscan", "p5"], writes=[("S", i)], tag=f"scn2{cg}_{i}")
                A("vector", (lambda ip=ip: lambda e: e.tensor_tensor(
                    out=sta[:], in0=p[7][:, 0:512], in1=s_sF[ip][:], op=ALU.mult))(),
                  reads=["p7", ("s_sF", ip)], writes=["sta"], tag=f"e1_{cg}_{i}")
                A("vector", lambda e: e.tensor_add(stb[:], sta[:], p[6][:, 0:512]),
                  reads=["sta", "p6"], writes=["stb"], tag=f"e2_{cg}_{i}")
                A("gpsimd", (lambda i=i: lambda e: e.tensor_tensor(
                    out=sxd[:].rearrange("p (j l) -> p j l", l=128),
                    in0=sxch[:].rearrange("p (j t) -> p j t", t=128)
                    [:, 4 * i:4 * i + 4, :],
                    in1=sDp[:, 4 * i:4 * i + 4, None].broadcast_to([128, 4, 128]),
                    op=ALU.mult))(),
                  reads=["sxch", "dp"], writes=["sxd"], tag=f"e3_{cg}_{i}")
                A("gpsimd", lambda e: e.tensor_add(sta[:], stb[:], sxd[:]),
                  reads=["stb", "sxd"], writes=["sta"], tag=f"e4_{cg}_{i}")
                A("scalar", (lambda ip=ip: lambda e: e.activation(
                    out=sg[:], in_=sz_ep[ip][:], func=AF.Silu))(),
                  reads=[("sz_ep", ip)], writes=["sg"], tag=f"e5_{cg}_{i}")
                A("vector", (lambda i=i: lambda e: e.tensor_mul(
                    sy[:, i * 512:(i + 1) * 512], sta[:], sg[:]))(),
                  reads=["sta", "sg"], writes=["sy"], tag=f"e6_{cg}_{i}")
                A("scalar", (lambda i=i, ip=ip: lambda e: e.activation(
                    out=sy2[ip][:], in_=sy[:, i * 512:(i + 1) * 512], func=AF.Square))(),
                  reads=["sy"], writes=[("sy2", ip)], tag=f"e7_{cg}_{i}")
                def em_stats(i):
                    ipx = i % 2
                    def stats(e):
                        r = None
                        for jj in range(4):
                            r = e.matmul(p[2][0:1, 0:128], sonescol[:],
                                         sy2[ipx][:, jj * 128:(jj + 1) * 128],
                                         start=(jj == 0), stop=(jj == 3))
                        return r
                    A("tensor", stats, reads=[("sy2", ipx), "onescol"], writes=["p2"],
                      tag=f"st2_{cg}_{i}")
                    spar = i % 2
                    if i == 0:
                        A("vector", lambda e: e.tensor_copy(out=sstat[0][:],
                                                            in_=p[2][0:1, 0:128]),
                          reads=["p2"], writes=[("sstat", 0)], tag=f"sa{cg}_{i}")
                    else:
                        A("vector", (lambda spar=spar: lambda e: e.tensor_add(
                            sstat[spar][:], sstat[1 - spar][:], p[2][0:1, 0:128]))(),
                          reads=[("sstat", 1 - spar), "p2"], writes=[("sstat", spar)],
                          tag=f"sa{cg}_{i}")
                if i > 0:
                    em_stats(i - 1)
                if i == 7:
                    em_stats(7)
            A("vector", lambda e: e.tensor_scalar(
                out=sv1[:], in0=sstat[1][:], scalar1=1.0 / DI, scalar2=EPS,
                op0=ALU.mult, op1=ALU.add),
              reads=[("sstat", 1)], writes=["sv1"], tag=f"rms{cg}")
            A("scalar", lambda e: e.activation(out=sv2[:], in_=sv1[:], func=AF.Ln),
              reads=["sv1", "pk2"], writes=["pk2"])
            A("scalar", lambda e: e.activation(out=srstd1[:], in_=sv2[:], func=AF.Exp,
                                               scale=sconstm05[0:1, :]),
              reads=["pk2", "cpool"], writes=["srstd1"])
            A("tensor", lambda e: e.matmul(p[3][:, 0:128], sones64[0:1, :], srstd1[:],
                                           start=True, stop=True),
              reads=["srstd1", "pk4"], writes=["p3"], tag=f"rE{cg}")
            for i in range(8):
                A("vector", (lambda i=i, tc0=tc0: lambda e: e.tensor_tensor(
                    out=syn[:].rearrange("p (j t) -> p j t", t=T)
                    [:, 4 * i:4 * i + 4, tc0:tc0 + 128],
                    in0=sy[:, i * 512:(i + 1) * 512].rearrange("p (j l) -> p j l", l=128),
                    in1=p[3][:, None, 0:128].broadcast_to([128, 4, 128]),
                    op=ALU.mult))(),
                  reads=["sy", "p3"], writes=["syn"], tag=f"yn{cg}_{i}")

        for half in range(2):
            tcs = (2 * half, 2 * half + 1)
            wo3 = [wo_sb[0], wo_sb[1], sy[:, 0:2048]]
            for k in range(32):
                par = k % 3
                wtok = ("wo_sb", par) if par < 2 else "sy"
                if st == 0 and half == 0:
                    A("gpsimd", (lambda k=k, par=par: dma(wo3[par][:], wo_t[k]))(),
                      reads=[wtok], writes=[wtok], dma="wog", tag=f"woL{k}")
                    A("vector", (lambda k=k, par=par: lambda e: e.tensor_scalar(
                        out=wo3[par][:], in0=wo3[par][:], scalar1=snw[:, k:k + 1],
                        scalar2=None, op0=ALU.mult))(),
                      reads=[wtok, "nw"], writes=[wtok], tag=f"woF{k}")
                    A("sync", (lambda k=k, par=par: dma(wo_sc[k], wo3[par][:]))(),
                      reads=[wtok], writes=[("wosc", k)], dma="wo")
                else:
                    A("sync", (lambda k=k, par=par: dma(wo3[par][:], wo_sc[k]))(),
                      reads=[wtok, ("wosc", k)], writes=[wtok], dma="wo",
                      tag=f"woL{st}_{half}_{k}")
                def opmm(k=k, par=par, tcs=tcs):
                    def f(e):
                        r = None
                        for ti, tcx in enumerate(tcs):
                            for o in range(4):
                                r = e.matmul(p[ti * 4 + o][:, 0:512],
                                             syn[:, k * T + tcx * 128:k * T + tcx * 128 + 128],
                                             wo3[par][:, o * 512:(o + 1) * 512],
                                             start=(k == 0), stop=(k == 31))
                        return r
                    return f
                A("tensor", opmm(), reads=[wtok, "syn"],
                  writes=[f"p{b}" for b in range(8)], tag=f"op{st}_{half}_{k}", cost=1900)
            for ti, tcx in enumerate(tcs):
                x1t = x1[ti]
                for o in range(4):
                    rpar = o % 2
                    A("sync", (lambda t0=t0, tcx=tcx, o=o, rpar=rpar: dma(
                        su_res[rpar][:],
                        u_tm[t0 + tcx * 128:t0 + (tcx + 1) * 128,
                             o * 512:(o + 1) * 512]))(),
                      reads=[("su_res", rpar)], writes=[("su_res", rpar)], dma="u",
                      tag=f"ur{st}_{tcx}_{o}")
                    A("vector", (lambda ti=ti, o=o, rpar=rpar, x1t=x1t: lambda e:
                      e.tensor_add(x1t[:, o * 512:(o + 1) * 512],
                                   p[ti * 4 + o][:, 0:512], su_res[rpar][:]))(),
                      reads=[f"p{ti * 4 + o}", ("su_res", rpar)],
                      writes=[("x1", ti)], tag=f"h1{st}_{tcx}_{o}")
                A("scalar", (lambda x1t=x1t, ti=ti: lambda e: e.activation(
                    out=x_tm[:, 0:2048], in_=x1t[:], func=AF.Copy,
                    accum_out=ssum2_[ti][:]))(),
                  reads=[("x1", ti)], writes=[("ssum", ti), "x_tm"], tag=f"ls{st}_{tcx}")
                A("scalar", (lambda x1t=x1t, ti=ti: lambda e: e.activation(
                    out=x_tm[:, 0:2048], in_=x1t[:], func=AF.Square,
                    accum_out=ssq2_[ti][:]))(),
                  reads=[("x1", ti)], writes=[("ssq", ti), "x_tm"])
                A("vector", (lambda ti=ti: lambda e: e.tensor_scalar_mul(smean2_[ti][:], ssum2_[ti][:], 1.0 / DM))(),
                  reads=[("ssum", ti)], writes=[("smean", ti)])
                A("vector", (lambda ti=ti: lambda e: e.tensor_scalar(
                    out=sm22_[ti][:], in0=smean2_[ti][:], scalar1=smean2_[ti][:], scalar2=EPS,
                    op0=ALU.mult, op1=ALU.subtract))(),
                  reads=[("smean", ti)], writes=[("sm2", ti)])
                A("vector", (lambda ti=ti: lambda e: e.tensor_scalar(
                    out=svar2_[ti][:], in0=ssq2_[ti][:], scalar1=1.0 / DM, scalar2=None,
                    op0=ALU.mult))(),
                  reads=[("ssq", ti)], writes=[("svar", ti)])
                A("vector", (lambda ti=ti: lambda e: e.tensor_sub(svar2_[ti][:], svar2_[ti][:], sm22_[ti][:]))(),
                  reads=[("svar", ti), ("sm2", ti)], writes=[("svar", ti)])
                A("scalar", (lambda ti=ti: lambda e: e.activation(out=svar2_[ti][:], in_=svar2_[ti][:], func=AF.Ln))(),
                  reads=[("svar", ti)], writes=[("svar", ti)])
                A("scalar", (lambda ti=ti: lambda e: e.activation(out=srstd2_[ti][:], in_=svar2_[ti][:],
                                                   func=AF.Exp, scale=sconstm05[:]))(),
                  reads=[("svar", ti), "cpool"], writes=[("srstd", ti)])
                A("vector", (lambda x1t=x1t, ti=ti: lambda e: e.tensor_scalar(
                    out=x1t[:], in0=x1t[:], scalar1=smean2_[ti][:], scalar2=srstd2_[ti][:],
                    op0=ALU.subtract, op1=ALU.mult))(),
                  reads=[("smean", ti), ("srstd", ti), ("x1", ti)], writes=[("x1", ti)],
                  tag=f"ln1_{st}_{tcx}")
                A("vector", (lambda x1t=x1t, ti=ti: lambda e: e.tensor_reduce(
                    out=samax2_[ti][:, :, None],
                    in_=x1t[:].rearrange("p (a b) -> p a b", b=128),
                    axis=mybir.AxisListType.X, op=ALU.max,
                    apply_absolute_value=True))(),
                  reads=[("x1", ti)], writes=[("samax", ti)], tag=f"qa{st}_{tcx}")
                A("vector", (lambda ti=ti: lambda e: e.tensor_scalar_max(
                    samax2_[ti][:], samax2_[ti][:], 1e-4))(),
                  reads=[("samax", ti)], writes=[("samax", ti)])
                A("vector", (lambda ti=ti: lambda e: e.reciprocal(
                    out=srcp2_[ti][:], in_=samax2_[ti][:]))(),
                  reads=[("samax", ti)], writes=[("srcp", ti)])
                A("vector", (lambda ti=ti: lambda e: e.tensor_scalar_mul(
                    sr2242_[ti][:], srcp2_[ti][:], 224.0))(),
                  reads=[("srcp", ti)], writes=[("sr224", ti)])
                A("vector", (lambda ti=ti: lambda e: e.tensor_scalar_mul(
                    sa2242_[ti][:], samax2_[ti][:], 1.0 / 224.0))(),
                  reads=[("samax", ti)], writes=[("sa224", ti)])
                A("vector", (lambda x1t=x1t, ti=ti: lambda e: e.tensor_tensor(
                    out=sq8[:].rearrange("p (a b) -> p a b", b=128),
                    in0=x1t[:].rearrange("p (a b) -> p a b", b=128),
                    in1=sr2242_[ti][:, :, None].broadcast_to([128, 16, 128]),
                    op=ALU.mult))(),
                  reads=[("x1", ti), ("sr224", ti)], writes=["stb"], tag=f"q8{st}_{tcx}")
                A("vector", (lambda ti=ti: lambda e: e.tensor_tensor(
                    out=sxdq[:].rearrange("p (a b) -> p a b", b=128),
                    in0=sq8[:].rearrange("p (a b) -> p a b", b=128),
                    in1=sa2242_[ti][:, :, None].broadcast_to([128, 16, 128]),
                    op=ALU.mult))(),
                  reads=["stb", ("sa224", ti)], writes=["sxdq"], tag=f"dq{st}_{tcx}")
                for iq in range(2):
                    for ii in range(8):
                        i8 = iq * 8 + ii
                        A("tensor", (lambda i8=i8, ii=ii: lambda e: e.transpose(
                            p2b[:, ii * 128:(ii + 1) * 128],
                            sxdq[:, i8 * 128:(i8 + 1) * 128], seyeb[:]))(),
                          reads=["sxdq", "eyeb"], writes=["p2"],
                          tag=f"qT{st}_{tcx}_{i8}")
                    A("scalar", (lambda ti=ti, iq=iq: lambda e: e.copy(
                        out=sxdq_fm[ti][:, iq * 1024:(iq + 1) * 1024],
                        in_=p2b[:, 0:1024]))(),
                      reads=["p2"], writes=[("xdq", ti)], tag=f"qTe{st}_{tcx}_{iq}")
            wdq3 = [wdq_sb[0], wdq_sb[1], sy[:, 2048:3072]]
            for ohalf in range(2):
                for i in range(16):
                    par = i % 3
                    wtok = ("wdq", par) if par < 2 else "sy"
                    A("sync", (lambda i=i, par=par, ohalf=ohalf: dma(
                        wdq3[par][:], wf_sc[i][:, ohalf * 1024:(ohalf + 1) * 1024]))(),
                      reads=[wtok, ("wf", i)], writes=[wtok], dma="wo",
                      tag=f"wfL{st}_{half}_{ohalf}_{i}")
                    def ffmm(i=i, par=par):
                        def f(e):
                            r = None
                            for ti in range(2):
                                for o2 in range(2):
                                    r = e.matmul(p[4 + ti * 2 + o2][:, 0:512],
                                                 sxdq_fm[ti][:, i * 128:(i + 1) * 128],
                                                 wdq3[par][:, o2 * 512:(o2 + 1) * 512],
                                                 start=(i == 0), stop=(i == 15))
                            return r
                        return f
                    A("tensor", ffmm(), reads=[wtok, ("xdq", 0), ("xdq", 1)],
                      writes=[f"p{b}" for b in range(4, 8)], tag=f"ff{st}_{half}_{ohalf}_{i}", cost=1000)
                for ti in range(2):
                    sh2x = sh2 if ti == 0 else sh2b
                    htok = "sxw" if ti == 0 else "sy"
                    for o2 in range(2):
                        o = ohalf * 2 + o2
                        A("scalar", (lambda ti=ti, o2=o2: lambda e: e.copy(
                            out=sfb[:], in_=p[4 + ti * 2 + o2][:, 0:512]))(),
                          reads=[f"p{4 + ti * 2 + o2}"], writes=["scast"],
                          tag=f"fb{st}_{half}_{ti}_{o}")
                        A("vector", (lambda ti=ti, o=o, sh2x=sh2x, htok=htok: lambda e:
                          e.tensor_add(sh2x[:, o * 512:(o + 1) * 512],
                                       x1[ti][:, o * 512:(o + 1) * 512], sfb[:]))(),
                          reads=["scast", ("x1", ti)], writes=[htok],
                          tag=f"h2{st}_{half}_{ti}_{o}")
            for ti, tcx in enumerate(tcs):
                sh2x = sh2 if ti == 0 else sh2b
                htok = "sxw" if ti == 0 else "sy"
                A("scalar", (lambda sh2x=sh2x, ti=ti: lambda e: e.activation(
                    out=x_tm[:, 0:2048], in_=sh2x[:], func=AF.Copy,
                    accum_out=ssum2_[ti][:]))(),
                  reads=[htok], writes=[("ssum", ti), "x_tm"])
                A("scalar", (lambda sh2x=sh2x, ti=ti: lambda e: e.activation(
                    out=x_tm[:, 0:2048], in_=sh2x[:], func=AF.Square,
                    accum_out=ssq2_[ti][:]))(),
                  reads=[htok], writes=[("ssq", ti), "x_tm"])
                A("vector", (lambda ti=ti: lambda e: e.tensor_scalar_mul(smean2_[ti][:], ssum2_[ti][:], 1.0 / DM))(),
                  reads=[("ssum", ti)], writes=[("smean", ti)])
                A("vector", (lambda ti=ti: lambda e: e.tensor_scalar(
                    out=sm22_[ti][:], in0=smean2_[ti][:], scalar1=smean2_[ti][:], scalar2=EPS,
                    op0=ALU.mult, op1=ALU.subtract))(),
                  reads=[("smean", ti)], writes=[("sm2", ti)])
                A("vector", (lambda ti=ti: lambda e: e.tensor_scalar(
                    out=svar2_[ti][:], in0=ssq2_[ti][:], scalar1=1.0 / DM, scalar2=None,
                    op0=ALU.mult))(),
                  reads=[("ssq", ti)], writes=[("svar", ti)])
                A("vector", (lambda ti=ti: lambda e: e.tensor_sub(svar2_[ti][:], svar2_[ti][:], sm22_[ti][:]))(),
                  reads=[("svar", ti), ("sm2", ti)], writes=[("svar", ti)])
                A("scalar", (lambda ti=ti: lambda e: e.activation(out=svar2_[ti][:], in_=svar2_[ti][:],
                                                   func=AF.Ln))(),
                  reads=[("svar", ti)], writes=[("svar", ti)])
                A("scalar", (lambda ti=ti: lambda e: e.activation(out=srstd2_[ti][:], in_=svar2_[ti][:],
                                                   func=AF.Exp, scale=sconstm05[:]))(),
                  reads=[("svar", ti), "cpool"], writes=[("srstd", ti)])
                x1t = x1[ti]
                for o in range(4):
                    A("vector", (lambda o=o, sh2x=sh2x, htok=htok, ti=ti, x1t=x1t:
                      lambda e: e.tensor_scalar(
                        out=x1t[:, o * 512:(o + 1) * 512],
                        in0=sh2x[:, o * 512:(o + 1) * 512],
                        scalar1=smean2_[ti][:], scalar2=srstd2_[ti][:],
                        op0=ALU.subtract, op1=ALU.mult))(),
                      reads=[htok, ("smean", ti), ("srstd", ti), ("x1", ti)],
                      writes=[("x1", ti)], tag=f"ln2_{st}_{tcx}_{o}")
                A("sync", (lambda t0=t0, tcx=tcx, ti=ti, x1t=x1t: dma(
                    out[t0 + tcx * 128:t0 + (tcx + 1) * 128, :], x1t[:]))(),
                  reads=[("x1", ti)], writes=[("x1", ti)],
                  dma="out", tag=f"out{st}_{tcx}")

    s.emit(nc, stack)
    return nc, stack


def _host_prep(inputs, NST=8):
    L = NST * 512
    u = np.ascontiguousarray(np.asarray(inputs["L0_attn_u"], dtype=np.float32))
    Wi = np.asarray(inputs["L0_attn_in_proj_kernel"], dtype=np.float32)
    Wo = np.asarray(inputs["L0_attn_out_proj_kernel"], dtype=np.float32)
    Wf = np.asarray(inputs["L0_ffn_weight"], dtype=np.float32)
    convw = np.asarray(inputs["L0_attn_conv1d_weight"], dtype=np.float32)
    convb = np.asarray(inputs["L0_attn_conv1d_bias"], dtype=np.float32)
    dtb = np.asarray(inputs["L0_attn_dt_bias"], dtype=np.float32)
    alog = np.asarray(inputs["L0_attn_A_log"], dtype=np.float32)
    Dv = np.asarray(inputs["L0_attn_D"], dtype=np.float32)
    nw = np.asarray(inputs["L0_attn_norm_weight"], dtype=np.float32)

    Wi_p = np.zeros((2048, NCT * 128), dtype=np.float32)
    Wi_p[:, :DIP] = Wi
    wi_t = np.ascontiguousarray(
        Wi_p.reshape(16, 128, NCT, 128).transpose(2, 1, 0, 3).reshape(NCT, 128, 2048))
    wo_t = np.ascontiguousarray(Wo.reshape(32, 128, DM))
    wf_t = np.ascontiguousarray(Wf.T.reshape(16, 128, 2048))
    convw_t = np.ascontiguousarray(convw.reshape(CW, 128, 4).transpose(1, 0, 2))
    convb_t = np.ascontiguousarray(convb.reshape(CW, 128).T)
    dp = np.empty((128, 32), dtype=np.float32)
    for pp in range(128):
        for j in range(32):
            dp[pp, j] = Dv[2 * j + pp // 64]
    nw_t = np.ascontiguousarray(nw.reshape(32, 128).T)
    eye_b = np.eye(128, dtype=ml_dtypes.bfloat16)
    eye_f = np.eye(128, dtype=np.float32)
    m01 = np.tril(np.ones((128, 128), np.float32)).T.astype(ml_dtypes.bfloat16)
    ones_col_b = np.ones((128, 1), dtype=ml_dtypes.bfloat16)
    ones_row_f = np.ones((1, 128), dtype=np.float32)
    ind4 = np.zeros((8, 2, 4, 128), dtype=np.float32)
    for k in range(4):
        ind4[k, :, k, :] = 1.0
        ind4[4 + k, :, k, :] = 1.0
    ind4 = ind4.reshape(8, 1024).astype(ml_dtypes.bfloat16)

    shared = dict(wi_t=wi_t, wo_t=wo_t, wf_t=wf_t, convw=convw_t, convb=convb_t,
                  dtb=dtb.reshape(64, 1), alog=alog.reshape(64, 1), dp_in=dp,
                  normw=nw_t, eye_b=eye_b, eye_f=eye_f, mask01=m01,
                  ones_col_b=ones_col_b, ones_row_f=ones_row_f, ind4=ind4)
    in_maps = []
    for b in range(u.shape[0]):
        m = dict(shared)
        m["u_t"] = np.ascontiguousarray(u[b, :L].T.reshape(16, 128, L))
        m["u_tm"] = np.ascontiguousarray(u[b, :L])
        in_maps.append(m)
    return in_maps


def kernel(**inputs):
    NST = 8
    nc, stack = build_program(NST)
    in_maps = _host_prep(inputs, NST)
    try:
        res = run_bass_kernel_spmd(nc, in_maps, core_ids=list(range(len(in_maps))))
    finally:
        stack.close()
    return np.stack([np.asarray(r["out"], dtype=np.float32) for r in res.results])


# version 22
# speedup vs baseline: 1.0353x; 1.0105x over previous
import numpy as np
import ml_dtypes
import concourse.bass as bass
import concourse.mybir as mybir
from concourse.bass_utils import run_bass_kernel_spmd

F32 = mybir.dt.float32
BF16 = mybir.dt.bfloat16
F8 = mybir.dt.float8e4
AF = mybir.ActivationFunctionType
ALU = mybir.AluOpType

DM = 2048
DIP = 8512
DI = 4096
NCT = 67
NK = DM // 128
CW = 34
EPS = 1e-5


class Sched:

    DEF_COST = {"sync": 1200, "gpsimd": 1600, "vector": 650, "scalar": 650,
                "tensor": 400}

    def __init__(self):
        self.ops = []
        self.last_write = {}
        self.readers = {}

    def add(self, engine, fn, reads=(), writes=(), dma=None, tag="", cost=None):
        idx = len(self.ops)
        deps = set()
        for t in reads:
            if t in self.last_write:
                deps.add(self.last_write[t])
        for t in writes:
            if t in self.last_write:
                deps.add(self.last_write[t])
            for r in self.readers.get(t, ()):
                deps.add(r)
        self.ops.append(dict(i=idx, engine=engine, fn=fn, deps=deps, dma=dma,
                             inc=False, tag=tag, r=tuple(reads), w=tuple(writes),
                             cost=cost if cost else self.DEF_COST[engine]))
        for t in reads:
            self.readers.setdefault(t, []).append(idx)
        for t in writes:
            self.last_write[t] = idx
            self.readers[t] = []
        return idx

    def reorder(self):
        import heapq
        ops = self.ops
        n = len(ops)
        assert all(op["i"] == k for k, op in enumerate(ops))
        children = [[] for _ in range(n)]
        byid0 = {op["i"]: op for op in ops}
        ndep = [0] * n
        for op in ops:
            for d in op["deps"]:
                children[d].append(op["i"])
                ndep[op["i"]] += 1
        LAT = 500
        ISSUE = 600
        cp = [0] * n
        for i in range(n - 1, -1, -1):
            m = 0
            ei = ops[i]["engine"]
            for c in children[i]:
                v = cp[c] + (LAT if ops[c]["engine"] != ei else 0)
                if v > m:
                    m = v
            cp[i] = ops[i]["cost"] + m
        engines = ["sync", "gpsimd", "vector", "scalar", "tensor"]
        ready = {e: [] for e in engines}
        for op in ops:
            if ndep[op["i"]] == 0:
                heapq.heappush(ready[op["engine"]], (-cp[op["i"]], op["i"]))
        etime = {e: 0 for e in engines}
        done = [0] * n
        order = []
        remaining = n
        while remaining:
            best = None
            for e in engines:
                while ready[e]:
                    i = ready[e][0][1]
                    break
                if not ready[e]:
                    continue
                i = ready[e][0][1]
                start = etime[e]
                for d in ops[i]["deps"]:
                    dd = done[d] + (LAT if byid0[d]["engine"] != e else 0)
                    if dd > start:
                        start = dd
                if best is None or start < best[0]:
                    best = (start, e, i)
            start, e, i = best
            heapq.heappop(ready[e])
            fin = start + ops[i]["cost"]
            if ops[i]["dma"] is not None:
                etime[e] = start + ISSUE
            else:
                etime[e] = fin
            done[i] = fin
            order.append(i)
            remaining -= 1
            for c in children[i]:
                ndep[c] -= 1
                if ndep[c] == 0:
                    heapq.heappush(ready[ops[c]["engine"]], (-cp[c], c))
        self.ops = [ops[i] for i in order]
        return max(etime.values())

    def emit(self, nc, stack):
        est = self.reorder()
        import os as _os
        if _os.environ.get("DBG_TOK"):
            t = _os.environ["DBG_TOK"]
            for k, op in enumerate(self.ops):
                if t in op["r"] or t in op["w"]:
                    rw = "W" if t in op["w"] else "R"
                    print(f"[tok] pos={k} id={op['i']} {rw} {op['engine']:7s} "
                          f"tag={op['tag']} r={op['r']} w={op['w']}")
        print(f"[sched] est span {est/1e6:.2f} ms over {len(self.ops)} ops")
        ops = self.ops
        byid = {op["i"]: op for op in ops}
        for op in ops:
            for d in op["deps"]:
                byid[d]["inc"] = True
        for op in ops:
            if op["dma"] is not None:
                op["inc"] = True
        engines = ["sync", "gpsimd", "vector", "scalar", "tensor"]
        dma_groups = sorted({op["dma"] for op in ops if op["dma"] is not None})
        R = 4
        esem = {e: stack.enter_context(nc.semaphore(f"e_{e}")) for e in engines}
        dsem = {(g, r): stack.enter_context(nc.semaphore(f"d_{g}{r}"))
                for g in dma_groups for r in range(R)}
        ecnt = {e: 0 for e in engines}
        dcnt = {g: 0 for g in dma_groups}
        for op in ops:
            if op["dma"] is not None:
                k = dcnt[op["dma"]]
                dcnt[op["dma"]] += 1
                sub, rnd = k % R, k // R
                op["ev"] = ("d", (op["dma"], sub), 16 * (rnd + 1))
                op["issue_wait"] = ((op["dma"], sub), 16 * rnd) if rnd else None
            elif op["inc"]:
                ecnt[op["engine"]] += 1
                op["ev"] = ("e", op["engine"], ecnt[op["engine"]])
                op["issue_wait"] = None
            else:
                op["ev"] = None
                op["issue_wait"] = None
        wm = {}
        for op in ops:
            need = {}
            for d in op["deps"]:
                pr = byid[d]
                if pr["ev"] is None:
                    continue
                kind, key, val = pr["ev"]
                sk = (kind, key)
                need[sk] = max(need.get(sk, 0), val)
            if op["issue_wait"] is not None:
                sk, val = ("d", op["issue_wait"][0]), op["issue_wait"][1]
                need[sk] = max(need.get(sk, 0), val)
            waits = []
            for sk, val in sorted(need.items(), key=str):
                w = (op["engine"], sk)
                if wm.get(w, -1) >= val:
                    continue
                wm[w] = val
                sem = esem[sk[1]] if sk[0] == "e" else dsem[sk[1]]
                waits.append((sem, val))
            op["waits"] = waits
        self.totals = ([(esem[e], ecnt[e]) for e in engines if ecnt[e]] +
                       [(dsem[(g, r)], 16 * (dcnt[g] // R + (1 if r < dcnt[g] % R else 0)))
                        for g in dma_groups for r in range(R)
                        if dcnt[g] // R + (1 if r < dcnt[g] % R else 0) > 0])
        blk = stack.enter_context(nc.Block())

        def body_for(name):
            def body(e):
                for op in ops:
                    if op["engine"] != name:
                        continue
                    for sem, val in op["waits"]:
                        e.wait_ge(sem, val)
                    inst = op["fn"](e)
                    if op["dma"] is not None:
                        assert inst is not None, f"op {op['i']} {op['tag']} returned None"
                        inst.then_inc(dsem[op["ev"][1]], 16)
                    elif op["inc"]:
                        assert inst is not None, f"op {op['i']} {op['tag']} returned None"
                        inst.then_inc(esem[op["engine"]], 1)
            return body

        blk.sync(body_for("sync"))
        blk.gpsimd(body_for("gpsimd"))
        blk.vector(body_for("vector"))
        blk.scalar(body_for("scalar"))
        blk.tensor(body_for("tensor"))
        totals = self.totals

        def fin(e):
            for sem, val in totals:
                e.wait_ge(sem, val)
        blk.sync(fin)


def build_program(NST=8):
    T = 512
    L = NST * T
    NCH = 4
    nc = bass.Bass("TRN2", target_bir_lowering=False)

    def inp(name, shape, dt=F32):
        return nc.dram_tensor(name, list(shape), dt, kind="ExternalInput").ap()

    u_t = inp("u_t", (NK, 128, L))
    u_tm = inp("u_tm", (L, DM))
    wi_t = inp("wi_t", (NCT, 128, NK * 128))
    wo_t = inp("wo_t", (32, 128, DM))
    wf_t = inp("wf_t", (16, 128, 2048))
    convw = inp("convw", (128, CW, 4))
    convb = inp("convb", (128, CW))
    dtb = inp("dtb", (64, 1))
    alog = inp("alog", (64, 1))
    dp_in = inp("dp_in", (128, 32))
    normw = inp("normw", (128, 32))
    eye_b = inp("eye_b", (128, 128), BF16)
    eye_f = inp("eye_f", (128, 128))
    mask01 = inp("mask01", (128, 128), BF16)
    ones_col_b = inp("ones_col_b", (128, 1), BF16)
    ones_row_f = inp("ones_row_f", (1, 128))
    ind4 = inp("ind4", (8, 1024), BF16)

    out = nc.dram_tensor("out", [L, DM], F32, kind="ExternalOutput").ap()

    wi_sc = nc.dram_tensor("wi_sc", [NCT, 128, NK * 128], BF16, kind="Internal").ap()
    wo_sc = nc.dram_tensor("wo_sc", [32, 128, DM], BF16, kind="Internal").ap()
    wf_sc = nc.dram_tensor("wf_sc", [16, 128, 2048], BF16, kind="Internal").ap()
    z_sc = nc.dram_tensor("z_sc", [32, 128, L], BF16, kind="Internal").ap()
    x_sc = nc.dram_tensor("x_sc", [32, 128, L], BF16, kind="Internal").ap()
    fl_sc = nc.dram_tensor("fl_sc", [3, 64, 128], F32, kind="Internal").ap()
    fl_b = nc.dram_tensor("fl_b", [64, 128], BF16, kind="Internal").ap()
    fl_b2 = nc.dram_tensor("fl_b2", [2, 64, 128], BF16, kind="Internal").ap()
    fl_b3 = nc.dram_tensor("fl_b3", [2, 64, 128], BF16, kind="Internal").ap()

    s = Sched()
    from contextlib import ExitStack
    stack = ExitStack()

    def sb(name, shape, dt):
        return nc.alloc_sbuf_tensor(name, list(shape), dt)

    su_bf = sb("su_bf", [128, NK * T], BF16)
    wi_sb = [sb(f"wi_sb{i}", [128, NK * 128], BF16) for i in range(2)]
    strip2 = [sb(f"strip{i}", [128, 515], BF16) for i in range(2)]
    halo = sb("halo", [128, CW * 3], BF16)
    conv_a = sb("conv_a", [128, T], F32)
    conv_b2 = sb("conv_b2", [128, T], F32)
    sB_fm = sb("sB_fm", [128, T], BF16)
    sC_fm = sb("sC_fm", [128, T], BF16)
    sxst2 = [sb(f"sxst{i}", [128, T], BF16) for i in range(2)]
    sxch = sb("sxch", [128, 32 * 128], BF16)
    sdt0 = sb("sdt0", [64, T], F32)
    sdt = sb("sdt", [64, T], F32)
    sdt_t1 = sb("sdt_t1", [64, T], F32)
    sdt_e = sb("sdt_e", [64, T], F32)
    sacs = sb("sacs", [64, T], F32)
    slnma = sb("slnma", [64, T], F32)
    ssig = sb("ssig", [64, T], F32)
    sw_fm = sb("sw_fm", [64, 128], F32)
    scd = sb("scd", [64, 4], F32)
    sones64 = sb("sones64", [64, 128], F32)
    sconst1 = sb("sconst1", [128, 1], F32)
    sconstm05 = sb("sconstm05", [128, 1], F32)
    sconstm1 = sb("sconstm1", [128, 1], F32)
    lnd5x = [sb(f"lnd5_{i}", [10, 256], BF16) for i in range(2)]
    shilo = sb("shilo", [64, 256], BF16)
    rhs5 = sb("rhs5", [10, 1024], BF16)
    x_tm = sb("x_tm", [128, 32 * 128], BF16)
    sxw = sb("sxw", [128, 32 * 128], BF16)
    sBtm = sb("sBtm", [128, 128], BF16)
    swT = sb("swT", [128, 64], F32)
    ssm = sb("ssm", [128, 128], BF16)
    smind2 = [sb(f"smind{i}", [128, 512], F32) for i in range(2)]
    sexpD2 = [sb(f"sexpD{i}", [128, 512], BF16) for i in range(2)]
    sSL = [sb(f"sSL{i}", [128, 512], BF16) for i in range(2)]
    S = sb("S", [128, 32 * 128], F32)
    scast = sb("scast", [128, 512], BF16)
    cdP = sb("cdP", [128, 64], F32)
    sscan = sb("sscan", [128, 512], F32)
    s_sF = [sb(f"s_sF{i}", [128, 512], BF16) for i in range(2)]
    sta = sb("sta", [128, 512], F32)
    sq8 = sb("sq8", [128, 2048], F8)
    stb = sq8[:].bitcast(F32)
    sxd = sb("sxd", [128, 512], BF16)
    sg = sb("sg", [128, 512], F32)
    sy = sb("sy", [128, 32 * 128], BF16)
    sy2 = [sb(f"sy2_{i}", [128, 512], BF16) for i in range(2)]
    sstat = [sb(f"sstat{i}", [1, 128], F32) for i in range(2)]
    sv1 = sb("sv1", [1, 128], F32)
    sv2 = sdt_e[0:1, 0:128]
    srstd1 = sb("srstd1", [1, 128], F32)
    syn = sb("syn", [128, 32 * T], BF16)
    sz_ep = [sb(f"sz_ep{i}", [128, 512], BF16) for i in range(2)]
    wo_sb = [sb(f"wo_sb{i}", [128, DM], BF16) for i in range(2)]
    su_res = [sb(f"su_res{i}", [128, 512], F32) for i in range(2)]
    x1 = [sb(f"x1_{i}", [128, DM], F32) for i in range(2)]
    ssum2_ = [sb(f"lnsum{i}", [128, 1], F32) for i in range(2)]
    ssq2_ = [sb(f"lnsq{i}", [128, 1], F32) for i in range(2)]
    smean2_ = [sb(f"lnmean{i}", [128, 1], F32) for i in range(2)]
    sm22_ = [sb(f"lnm2{i}", [128, 1], F32) for i in range(2)]
    svar2_ = [sb(f"lnvar{i}", [128, 1], F32) for i in range(2)]
    srstd2_ = [sb(f"lnrstd{i}", [128, 1], F32) for i in range(2)]
    samax2_ = [sb(f"qamax{i}", [128, 16], F32) for i in range(2)]
    srcp2_ = [sb(f"qrcp{i}", [128, 16], F32) for i in range(2)]
    sr2242_ = [sb(f"qr224{i}", [128, 16], F32) for i in range(2)]
    sa2242_ = [sb(f"qa224{i}", [128, 16], F32) for i in range(2)]
    samax = samax2_[0]
    srcp = srcp2_[0]
    sr224 = sr2242_[0]
    sa224 = sa2242_[0]
    sxdq = sb("sxdq", [128, DM], BF16)
    sxdq_fm = [sb(f"sxdq_fm{i}", [128, 16 * 128], BF16) for i in range(2)]
    wdq_sb = [sb(f"wdq_sb{i}", [128, 1024], BF16) for i in range(2)]
    sfb = scast
    sh2 = sxw[:].bitcast(F32)
    sh2b = sy[:].bitcast(F32)
    samx2 = samax
    samx3 = sb("samx3", [16, 1], F32)
    samx4 = sa224[0:1, :]
    swr = srcp
    swri = sr224
    snw = sb("snw", [128, 32], F32)
    sDp = sb("sDp", [128, 32], BF16)
    sconvw = sb("sconvw", [128, CW * 4], F32)
    sconvb = sb("sconvb", [128, CW], F32)
    sdtb = sb("sdtb", [64, 1], F32)
    salog = sb("salog", [64, 1], F32)
    saneg = sb("saneg", [64, 1], F32)
    seyeb = sb("seyeb", [128, 128], BF16)
    seyef = sb("seyef", [128, 128], F32)
    smask = sb("smask", [128, 128], BF16)
    sonescol = sb("sonescol", [128, 1], BF16)
    sonesrow = None

    p = [nc.alloc_psum_tensor(f"p{i}", [128, 512], F32) for i in range(8)]
    p2b = p[2][:].bitcast(BF16)

    A = s.add

    def dma(e_dst, src):
        def f(e):
            return e.dma_start(out=e_dst, in_=src)
        return f

    A("sync", dma(sconvw[:], convw.rearrange("p j k -> p (j k)")), writes=["convw"], dma="m")
    A("sync", dma(sconvb[:], convb), writes=["convb"], dma="m")
    A("sync", dma(sdtb[:], dtb), writes=["dtb"], dma="m")
    A("sync", dma(salog[:], alog), writes=["alog"], dma="m")
    A("gpsimd", dma(sDp[:], dp_in), writes=["dp"], dma="mg")
    A("sync", dma(snw[:], normw), writes=["nw"], dma="m")
    A("sync", dma(seyeb[:], eye_b), writes=["eyeb"], dma="m")
    A("sync", dma(seyef[:], eye_f), writes=["eyef"], dma="m")
    A("sync", dma(smask[:], mask01), writes=["mask"], dma="m")
    A("sync", dma(sonescol[:], ones_col_b), writes=["onescol"], dma="m")
    A("sync", dma(rhs5[2:10, :], ind4), writes=["rhs5c"], dma="m")
    A("vector", lambda e: e.memset(sones64[:], 1.0), writes=["pk4"])
    A("vector", lambda e: e.memset(sconst1[:], 1.0), writes=["cpool"])
    A("vector", lambda e: e.memset(sconstm05[:], -0.5), writes=["cpool"])
    A("vector", lambda e: e.memset(sconstm1[:], -1.0), writes=["cpool"])
    A("vector", lambda e: e.memset(lnd5x[0][0:2, :], 1.0), writes=["lnd5r0"])
    A("vector", lambda e: e.memset(lnd5x[1][0:2, :], 1.0), writes=["lnd5r0"])
    A("vector", lambda e: e.memset(S[:], 0.0), writes=[("S", i) for i in range(8)])
    A("vector", lambda e: e.memset(halo[:], 0.0), writes=["halo"])
    A("scalar", lambda e: e.activation(out=saneg[:], in_=salog[:], func=AF.Exp),
      reads=["alog"], writes=["aneg"])
    A("vector", lambda e: e.tensor_scalar_mul(saneg[:], saneg[:], -1.0),
      reads=["aneg"], writes=["aneg"])

    for i in range(16):
        A("sync", dma(sh2[:], wf_t[i]), reads=["sxw"], writes=["sxw"], dma="w")
        A("vector", lambda e: e.tensor_reduce(
            out=samx2[:, :, None], in_=sh2[:].rearrange("p (a b) -> p a b", b=128),
            axis=mybir.AxisListType.X, op=ALU.max, apply_absolute_value=True),
          reads=["sxw"], writes=["samax"])
        A("tensor", lambda e: e.transpose(p[3][0:16, 0:128], samx2[:], seyef[:]),
          reads=["samax", "eyef"], writes=["p3"])
        A("vector", lambda e: e.tensor_reduce(
            out=samx3[:, None, :], in_=p[3][0:16, None, 0:128],
            axis=mybir.AxisListType.X, op=ALU.max),
          reads=["p3"], writes=["samx3"])
        A("vector", lambda e: e.tensor_scalar_max(samx3[:], samx3[:], 1e-4),
          reads=["samx3"], writes=["samx3"])
        A("tensor", lambda e: e.transpose(p[4][0:1, 0:16], samx3[:], seyef[0:16, 0:16]),
          reads=["samx3", "eyef"], writes=["p4"])
        A("vector", lambda e: e.tensor_copy(out=samx4[:], in_=p[4][0:1, 0:16]),
          reads=["p4"], writes=["sa224"])
        A("tensor", lambda e: e.matmul(p[4][:, 0:16], sones64[0:1, :], samx4[:],
                                       start=True, stop=True),
          reads=["sa224", "pk4"], writes=["p4"])
        A("vector", lambda e: e.reciprocal(out=swr[:], in_=p[4][:, 0:16]),
          reads=["p4"], writes=["srcp"])
        A("vector", lambda e: e.tensor_scalar_mul(swri[:], swr[:], 224.0),
          reads=["srcp"], writes=["sr224"])
        A("vector", lambda e: e.tensor_scalar_mul(swr[:], p[4][:, 0:16], 1.0 / 224.0),
          reads=["p4"], writes=["srcp"])
        A("vector", lambda e: e.tensor_tensor(
            out=sq8[:].rearrange("p (a b) -> p a b", b=128),
            in0=sh2[:].rearrange("p (a b) -> p a b", b=128),
            in1=swri[:, :, None].broadcast_to([128, 16, 128]), op=ALU.mult),
          reads=["sxw", "sr224"], writes=["stb"])
        A("vector", lambda e: e.tensor_tensor(
            out=sxdq[:].rearrange("p (a b) -> p a b", b=128),
            in0=sq8[:].rearrange("p (a b) -> p a b", b=128),
            in1=swr[:, :, None].broadcast_to([128, 16, 128]), op=ALU.mult),
          reads=["stb", "srcp"], writes=["sxdq"])
        A("sync", dma(wf_sc[i], sxdq[:]), reads=["sxdq"], writes=[("wf", i), "sxdq"],
          dma="w")

    for st in range(NST):
        t0 = st * T
        A("gpsimd", dma(su_bf[:], u_t[:, :, t0:t0 + T].rearrange("k p t -> p k t")),
          writes=["su_bf"], dma="ug", tag=f"u{st}", cost=12000)

        wi3 = [wi_sb[0], wi_sb[1], sxdq]
        for c in range(NCT):
            par = c % 3
            wtok = ("wi_sb", par) if par < 2 else "sxdq"
            if st == 0:
                A("gpsimd", (lambda c=c, par=par: dma(wi3[par][:], wi_t[c]))(),
                  reads=[wtok], writes=[wtok], dma="wig", tag=f"wiL{c}")
                A("sync", (lambda c=c, par=par: dma(wi_sc[c], wi3[par][:]))(),
                  reads=[wtok], writes=[("wisc", c)], dma="wi")
            else:
                A("sync", (lambda c=c, par=par: dma(wi3[par][:], wi_sc[c]))(),
                  reads=[wtok, ("wisc", c)], writes=[wtok], dma="wi",
                  tag=f"wiL{st}_{c}")
            pb = c % 2
            def mk_mm(c=c, par=par, pb=pb):
                def f(e):
                    r = None
                    for k in range(NK):
                        r = e.matmul(p[pb][:], wi3[par][:, k * 128:(k + 1) * 128],
                                     su_bf[:, k * T:(k + 1) * T],
                                     start=(k == 0), stop=(k == NK - 1))
                    return r
                return f
            A("tensor", mk_mm(), reads=[wtok, "su_bf"], writes=[f"p{pb}"],
              tag=f"ip{st}_{c}", cost=3600)
            if c < 32:
                sp = c % 2
                A("vector", (lambda pb=pb, sp=sp: lambda e: e.tensor_copy(
                    out=sxst2[sp][:], in_=p[pb][:]))(),
                  reads=[f"p{pb}", ("sxst", sp)], writes=[("sxst", sp)],
                  tag=f"zev{st}_{c}")
                A("sync", (lambda c=c, t0=t0, sp=sp: dma(
                    z_sc[c][:, t0:t0 + T], sxst2[sp][:]))(),
                  reads=[("sxst", sp)], writes=[("z", st, c), ("sxst", sp)], dma="z")
            elif c < 66:
                j = c - 32
                sp = c % 2
                strip = strip2[sp]
                stok = ("strip", sp)
                A("vector", (lambda j=j, strip=strip, sp=sp: lambda e: e.tensor_copy(
                    out=strip[:, 0:3], in_=halo[:, 3 * j:3 * j + 3]))(),
                  reads=["halo", stok], writes=[stok], tag=f"hin{st}_{j}")
                A("scalar", (lambda pb=pb, strip=strip: lambda e: e.copy(
                    out=strip[:, 3:515], in_=p[pb][:]))(),
                  reads=[f"p{pb}", stok], writes=[stok], tag=f"xbev{st}_{j}")
                A("vector", (lambda j=j, strip=strip: lambda e: e.tensor_copy(
                    out=halo[:, 3 * j:3 * j + 3], in_=strip[:, 512:515]))(),
                  reads=[stok], writes=["halo"], tag=f"hout{st}_{j}")
                A("vector", (lambda j=j, strip=strip: lambda e: e.tensor_scalar(
                    out=conv_a[:], in0=strip[:, 0:T], scalar1=sconvw[:, 4 * j:4 * j + 1],
                    scalar2=None, op0=ALU.mult))(),
                  reads=[stok, "convw"], writes=["conv_a"], tag=f"cv0_{st}_{j}")
                for k in (1, 2):
                    src_, dst_ = (conv_a, conv_b2) if k == 1 else (conv_b2, conv_a)
                    A("vector", (lambda j=j, k=k, src_=src_, dst_=dst_, strip=strip:
                      lambda e: e.scalar_tensor_tensor(
                          out=dst_[:], in0=strip[:, k:k + T],
                          scalar=sconvw[:, 4 * j + k:4 * j + k + 1], in1=src_[:],
                          op0=ALU.mult, op1=ALU.add))(),
                      reads=[stok, "convw", "conv_a" if k == 1 else "conv_b2"],
                      writes=["conv_b2" if k == 1 else "conv_a"], tag=f"cv{k}_{st}_{j}")
                A("vector", (lambda j=j, strip=strip: lambda e: e.scalar_tensor_tensor(
                    out=conv_b2[:], in0=strip[:, 3:3 + T],
                    scalar=sconvw[:, 4 * j + 3:4 * j + 4], in1=conv_a[:],
                    op0=ALU.mult, op1=ALU.add))(),
                  reads=[stok, "convw", "conv_a"], writes=["conv_b2"],
                  tag=f"cv3_{st}_{j}")
                if j < 32:
                    A("scalar", (lambda j=j, sp=sp: lambda e: e.activation(
                        out=sxst2[sp][:], in_=conv_b2[:], func=AF.Silu,
                        bias=sconvb[:, j:j + 1]))(),
                      reads=["conv_b2", "convb", ("sxst", sp)], writes=[("sxst", sp)],
                      tag=f"silu{st}_{j}")
                    A("sync", (lambda j=j, t0=t0, sp=sp: dma(
                        x_sc[j][:, t0:t0 + T], sxst2[sp][:]))(),
                      reads=[("sxst", sp)], writes=[("x", st, j), ("sxst", sp)],
                      dma="z")
                elif j == 32:
                    A("scalar", (lambda j=j: lambda e: e.activation(
                        out=sB_fm[:], in_=conv_b2[:], func=AF.Silu,
                        bias=sconvb[:, j:j + 1]))(),
                      reads=["conv_b2", "convb", "sB_fm"], writes=["sB_fm"],
                      tag=f"silu{st}_{j}")
                else:
                    A("scalar", (lambda j=j: lambda e: e.activation(
                        out=sC_fm[:], in_=conv_b2[:], func=AF.Silu,
                        bias=sconvb[:, j:j + 1]))(),
                      reads=["conv_b2", "convb", "sC_fm"], writes=["sC_fm"],
                      tag=f"silu{st}_{j}")
            else:
                A("vector", (lambda pb=pb: lambda e: e.tensor_scalar(
                    out=sdt0[:], in0=p[pb][0:64, :], scalar1=sdtb[:], scalar2=None,
                    op0=ALU.add))(),
                  reads=[f"p{pb}", "dtb"], writes=["pk1"], tag=f"dt0_{st}")
                A("scalar", lambda e: e.activation(out=sdt_t1[:], in_=sdt0[:],
                                                   func=AF.Abs),
                  reads=["pk1"], writes=["pk2"])
                A("scalar", lambda e: e.activation(out=sdt_e[:], in_=sdt_t1[:],
                                                   func=AF.Exp,
                                                   scale=sconstm1[0:64, :]),
                  reads=["pk2", "cpool"], writes=["pk2"])
                A("scalar", lambda e: e.activation(out=sdt_t1[:], in_=sdt_e[:],
                                                   func=AF.Ln, bias=sconst1[0:64, :]),
                  reads=["pk2", "cpool"], writes=["pk2"])
                A("vector", lambda e: e.tensor_scalar(
                    out=sdt_e[:], in0=sdt0[:], scalar1=0.0, scalar2=None,
                    op0=ALU.max),
                  reads=["pk1"], writes=["pk2"])
                A("vector", lambda e: e.tensor_add(sdt[:], sdt_t1[:], sdt_e[:]),
                  reads=["pk2"], writes=["pk1"], tag=f"dt_{st}")

        A("vector", lambda e: e.tensor_scalar(out=sdt0[:], in0=sdt[:], scalar1=1e-30,
                                              scalar2=None, op0=ALU.max),
          reads=["pk1"], writes=["pk1"])
        A("scalar", lambda e: e.activation(out=sdt_e[:], in_=sdt0[:], func=AF.Ln),
          reads=["pk1"], writes=["pk2"])
        A("vector", lambda e: e.tensor_scalar(out=sdt_t1[:], in0=sdt[:], scalar1=saneg[:],
                                              scalar2=None, op0=ALU.mult),
          reads=["pk1", "aneg"], writes=["pk2"], tag=f"dA_{st}")
        for cch in range(NCH):
            A("vector", (lambda cch=cch: lambda e: e.tensor_tensor_scan(
                out=sacs[:, cch * 128:(cch + 1) * 128], data0=sones64[:],
                data1=sdt_t1[:, cch * 128:(cch + 1) * 128], initial=0.0,
                op0=ALU.mult, op1=ALU.add))(),
              reads=["pk2", "pk4"], writes=["pk3"], tag=f"acs{st}_{cch}")
        A("vector", lambda e: e.tensor_sub(slnma[:], sdt_e[:], sacs[:]),
          reads=["pk2", "pk3"], writes=["pk3"])
        A("scalar", lambda e: e.activation(out=ssig[:], in_=sacs[:], func=AF.Exp),
          reads=["pk3"], writes=["pk4"])
        A("scalar", lambda e: e.activation(
            out=scd[:, :, None],
            in_=sacs[:].rearrange("h (c l) -> h c l", l=128)[:, :, 127:128],
            func=AF.Exp),
          reads=["pk3"], writes=["pk4"], tag=f"cd{st}")

        for cch in range(NCH):
            cg = st * NCH + cch
            tc0 = cch * 128
            A("scalar", (lambda cch=cch: lambda e: e.activation(
                out=sw_fm[:], in_=slnma[:, cch * 128:(cch + 1) * 128], func=AF.Exp,
                bias=sacs[:, cch * 128 + 127:cch * 128 + 128]))(),
              reads=["pk3"], writes=["pk4"], tag=f"wfm{cg}")
            A("vector", (lambda cch=cch: lambda e: e.tensor_copy(
                out=shilo[:, 0:128], in_=sacs[:, cch * 128:(cch + 1) * 128]))(),
              reads=["pk3"], writes=["shilo"], tag=f"hiA{cg}")
            A("vector", (lambda cch=cch: lambda e: e.tensor_sub(
                shilo[:, 128:256], sacs[:, cch * 128:(cch + 1) * 128],
                shilo[:, 0:128]))(),
              reads=["pk3", "shilo"], writes=["shilo"], tag=f"loA{cg}")
            A("sync", dma(fl_b2.rearrange("a h s -> h a s"),
                          shilo[:].rearrange("h (a s) -> h a s", s=128)),
              reads=["shilo"], writes=["fl0"], dma="f")
            A("vector", (lambda cch=cch: lambda e: e.tensor_copy(
                out=shilo[:, 0:128], in_=slnma[:, cch * 128:(cch + 1) * 128]))(),
              reads=["pk3", "shilo", "fl0"], writes=["shilo"], tag=f"hiL{cg}")
            A("vector", (lambda cch=cch: lambda e: e.tensor_sub(
                shilo[:, 128:256], slnma[:, cch * 128:(cch + 1) * 128],
                shilo[:, 0:128]))(),
              reads=["pk3", "shilo"], writes=["shilo"], tag=f"loL{cg}")
            A("sync", dma(fl_b3.rearrange("a h s -> h a s"),
                          shilo[:].rearrange("h (a s) -> h a s", s=128)),
              reads=["shilo"], writes=["fl1"], dma="f")
            A("gpsimd", (lambda cch=cch: dma(fl_b, ssig[:, cch * 128:(cch + 1) * 128]))(),
              reads=["pk4"], writes=["fl2"], dma="mg")
            A("tensor", (lambda cch=cch: lambda e: e.transpose(
                p[3][0:1, 0:64], scd[:, cch:cch + 1], seyef[0:64, 0:64]))(),
              reads=["pk4", "eyef"], writes=["p3"], tag=f"cdT{cg}")
            A("vector", lambda e: e.tensor_copy(out=sv1[0:1, 0:64],
                                                in_=p[3][0:1, 0:64]),
              reads=["p3"], writes=["sv1"])
            A("tensor", lambda e: e.matmul(p[3][:, 0:64], sones64[0:1, :], sv1[0:1, 0:64],
                                           start=True, stop=True),
              reads=["sv1", "pk4"], writes=["p3"])
            A("vector", lambda e: e.tensor_copy(out=cdP[:], in_=p[3][:, 0:64]),
              reads=["p3"], writes=["cdP"])
            A("tensor", (lambda cch=cch: lambda e: e.matmul(
                p[3][:, 0:128], sB_fm[:, cch * 128:(cch + 1) * 128],
                sC_fm[:, cch * 128:(cch + 1) * 128], start=True, stop=True))(),
              reads=["sB_fm", "sC_fm"], writes=["p3"], tag=f"sc{cg}")
            A("vector", lambda e: e.tensor_tensor(out=ssm[:], in0=p[3][:, 0:128],
                                                  in1=smask[:], op=ALU.mult),
              reads=["p3", "mask"], writes=["ssm"])
            A("sync", (lambda t0=t0, tc0=tc0: dma(
                sxch[:].rearrange("p (j t) -> p j t", t=128),
                x_sc[:, :, t0 + tc0:t0 + tc0 + 128].rearrange("j p t -> p j t")))(),
              reads=[("x", st, j) for j in range(32)] + ["sxch"],
              writes=["sxch"], dma="z", tag=f"xl{cg}", cost=3500)
            for jq in range(4):
                for jj in range(8):
                    j = jq * 8 + jj
                    A("tensor", (lambda j=j, jj=jj: lambda e: e.transpose(
                        p2b[:, jj * 128:(jj + 1) * 128],
                        sxch[:, j * 128:(j + 1) * 128], seyeb[:]))(),
                      reads=["sxch", "eyeb"], writes=["p2"], tag=f"xT{cg}_{j}")
                A("scalar", (lambda jq=jq: lambda e: e.copy(
                    out=x_tm[:, jq * 1024:(jq + 1) * 1024], in_=p2b[:, 0:1024]))(),
                  reads=["p2"], writes=["x_tm"], tag=f"xTe{cg}_{jq}")
            A("tensor", (lambda tc0=tc0: lambda e: e.transpose(
                p2b[:, 0:128], sB_fm[:, tc0:tc0 + 128], seyeb[:]))(),
              reads=["sB_fm", "eyeb"], writes=["p2"], tag=f"BT{cg}")
            A("scalar", lambda e: e.copy(out=sBtm[:], in_=p2b[:, 0:128]),
              reads=["p2"], writes=["sBtm"])
            A("tensor", lambda e: e.transpose(p[3][0:128, 0:64], sw_fm[:], seyef[0:64, 0:64]),
              reads=["pk4", "eyef"], writes=["p3"], tag=f"wT{cg}")
            A("vector", lambda e: e.tensor_copy(out=swT[:], in_=p[3][:, 0:64]),
              reads=["p3"], writes=["swT"])
            A("vector", lambda e: e.tensor_tensor(
                out=sxw[:].rearrange("s (h q) -> s h q", q=64),
                in0=x_tm[:].rearrange("s (h q) -> s h q", q=64),
                in1=swT[:, :, None].broadcast_to([128, 64, 64]), op=ALU.mult),
              reads=["x_tm", "swT"], writes=["sxw"], tag=f"xw{cg}")

            for i in range(8):
                ip = i % 2
                for ab in range(2):
                    A("sync", (lambda i=i, ip=ip, ab=ab: dma(
                        lnd5x[ip][2 + 4 * ab:6 + 4 * ab, :]
                        .rearrange("k (g s) -> k g s", g=2),
                        fl_b3[ab].rearrange("(g k) s -> k g s", k=4)
                        [:, 2 * i:2 * i + 2, :]))(),
                      reads=["fl1", ("lnd5", ip)], writes=[("lnd5", ip)], dma="f",
                      tag=f"l5_{cg}_{i}_{ab}")
                A("sync", (lambda i=i: dma(
                    rhs5[0:2, :],
                    fl_b2.rearrange("a h s -> a (h s)")[:, i * 1024:(i + 1) * 1024]))(),
                  reads=["fl0", "rhs5"], writes=["rhs5"], dma="f", tag=f"r5_{cg}_{i}")
                for kh in range(2):
                    A("scalar", (lambda i=i, ip=ip, kh=kh: lambda e: e.dma_start(
                        out=s_sF[ip][64 * kh:64 * (kh + 1), :]
                        .rearrange("p (j l) -> p j l", l=128),
                        in_=fl_b.rearrange("(j k) l -> k j l", k=2)
                        [kh, 4 * i:4 * i + 4, :][None, :, :]
                        .broadcast_to([64, 4, 128])))(),
                      reads=["fl2", ("s_sF", ip)], writes=[("s_sF", ip)],
                      dma="fs", tag=f"sF{cg}_{i}_{kh}")
                A("scalar", (lambda i=i, ip=ip, t0=t0, tc0=tc0: dma(
                    sz_ep[ip][:].rearrange("p (j t) -> p j t", t=128),
                    z_sc[4 * i:4 * i + 4, :, t0 + tc0:t0 + tc0 + 128]
                    .rearrange("j p t -> p j t")))(),
                  reads=[("z", st, 4 * i + jx) for jx in range(4)] + [("sz_ep", ip)],
                  writes=[("sz_ep", ip)], dma="zs", tag=f"zl{cg}_{i}")
                for gg in range(2):
                    g = 2 * i + gg
                    pd = 3 + gg
                    sl = sSL[gg]
                    A("tensor", (lambda pd=pd, ip=ip, gg=gg: lambda e: e.matmul(
                        p[pd][:, 0:512], lnd5x[ip][:, gg * 128:(gg + 1) * 128],
                        rhs5[:, gg * 512:(gg + 1) * 512], start=True, stop=True))(),
                      reads=[("lnd5", ip), "lnd5r0", "rhs5", "rhs5c"],
                      writes=[f"p{pd}"], tag=f"D{cg}_{g}", cost=350)
                    A("vector", (lambda pd=pd, gg=gg: lambda e: e.tensor_scalar(
                        out=smind2[gg][:], in0=p[pd][:, 0:512], scalar1=30.0,
                        scalar2=None, op0=ALU.min))(),
                      reads=[f"p{pd}"], writes=[("smind", gg)], tag=f"Dm{cg}_{g}")
                    A("scalar", (lambda gg=gg: lambda e: e.activation(
                        out=sexpD2[gg][:], in_=smind2[gg][:], func=AF.Exp))(),
                      reads=[("smind", gg)], writes=[("sexpD", gg)], tag=f"De{cg}_{g}")
                    A("vector", (lambda sl=sl, gg=gg: lambda e: e.tensor_tensor(
                        out=sl[:].rearrange("s (h l) -> s h l", l=128),
                        in0=sexpD2[gg][:].rearrange("s (h l) -> s h l", l=128),
                        in1=ssm[:, None, :].broadcast_to([128, 4, 128]),
                        op=ALU.mult))(),
                      reads=[("sexpD", gg), "ssm"], writes=[("sSL", gg)], tag=f"SL{cg}_{g}")
                def ydiag(i=i):
                    def f(e):
                        r = None
                        for jj in range(4):
                            pr = 4 * i + jj
                            gg, kk = jj // 2, jj % 2
                            sl = sSL[gg]
                            r = e.matmul(p[6][0:64, jj * 128:(jj + 1) * 128],
                                         x_tm[:, pr * 128:pr * 128 + 64],
                                         sl[:, (2 * kk) * 128:(2 * kk + 1) * 128],
                                         start=True, stop=True, tile_position=(0, 0))
                            r = e.matmul(p[6][64:128, jj * 128:(jj + 1) * 128],
                                         x_tm[:, pr * 128 + 64:pr * 128 + 128],
                                         sl[:, (2 * kk + 1) * 128:(2 * kk + 2) * 128],
                                         start=True, stop=True, tile_position=(0, 64))
                        return r
                    return f
                A("tensor", ydiag(), reads=["x_tm", ("sSL", 0), ("sSL", 1)],
                  writes=["p6"], tag=f"Yd{cg}_{i}", cost=900)
                A("vector", (lambda i=i: lambda e: e.tensor_copy(
                    out=scast[:], in_=S[:, i * 512:(i + 1) * 512]))(),
                  reads=[("S", i)], writes=["scast"], tag=f"Sc{cg}_{i}")
                def yoff(i=i, cch=cch):
                    def f(e):
                        r = None
                        for jj in range(4):
                            for hh in range(2):
                                base = (2 * jj + hh) * 64
                                r = e.matmul(
                                    p[7][hh * 64:(hh + 1) * 64, jj * 128:(jj + 1) * 128],
                                    scast[:, base:base + 64],
                                    sC_fm[:, cch * 128:(cch + 1) * 128],
                                    start=True, stop=True,
                                    tile_position=(0, hh * 64))
                        return r
                    return f
                A("tensor", yoff(), reads=["scast", "sC_fm"], writes=["p7"],
                  tag=f"Yo{cg}_{i}", cost=900)
                A("tensor", (lambda i=i: lambda e: e.matmul(
                    p[5][:, 0:512], sBtm[:], sxw[:, i * 512:(i + 1) * 512],
                    start=True, stop=True))(),
                  reads=["sBtm", "sxw"], writes=["p5"], tag=f"St{cg}_{i}")
                A("vector", (lambda i=i: lambda e: e.tensor_tensor(
                    out=sscan[:].rearrange("n (h q) -> n h q", q=64),
                    in0=S[:, i * 512:(i + 1) * 512].rearrange("n (h q) -> n h q", q=64),
                    in1=cdP[:, 8 * i:8 * i + 8, None].broadcast_to([128, 8, 64]),
                    op=ALU.mult))(),
                  reads=[("S", i), "cdP", "scast"], writes=["sscan"], tag=f"scn{cg}_{i}")
                A("vector", (lambda i=i: lambda e: e.tensor_add(
                    S[:, i * 512:(i + 1) * 512], sscan[:], p[5][:, 0:512]))(),
                  reads=["sscan", "p5"], writes=[("S", i)], tag=f"scn2{cg}_{i}")
                A("vector", (lambda ip=ip: lambda e: e.tensor_tensor(
                    out=sta[:], in0=p[7][:, 0:512], in1=s_sF[ip][:], op=ALU.mult))(),
                  reads=["p7", ("s_sF", ip)], writes=["sta"], tag=f"e1_{cg}_{i}")
                A("vector", lambda e: e.tensor_add(stb[:], sta[:], p[6][:, 0:512]),
                  reads=["sta", "p6"], writes=["stb"], tag=f"e2_{cg}_{i}")
                A("gpsimd", (lambda i=i: lambda e: e.tensor_tensor(
                    out=sxd[:].rearrange("p (j l) -> p j l", l=128),
                    in0=sxch[:].rearrange("p (j t) -> p j t", t=128)
                    [:, 4 * i:4 * i + 4, :],
                    in1=sDp[:, 4 * i:4 * i + 4, None].broadcast_to([128, 4, 128]),
                    op=ALU.mult))(),
                  reads=["sxch", "dp"], writes=["sxd"], tag=f"e3_{cg}_{i}")
                A("gpsimd", lambda e: e.tensor_add(sta[:], stb[:], sxd[:]),
                  reads=["stb", "sxd"], writes=["sta"], tag=f"e4_{cg}_{i}")
                A("scalar", (lambda ip=ip: lambda e: e.activation(
                    out=sg[:], in_=sz_ep[ip][:], func=AF.Silu))(),
                  reads=[("sz_ep", ip)], writes=["sg"], tag=f"e5_{cg}_{i}")
                A("gpsimd", (lambda i=i: lambda e: e.tensor_mul(
                    sy[:, i * 512:(i + 1) * 512], sta[:], sg[:]))(),
                  reads=["sta", "sg"], writes=["sy"], tag=f"e6_{cg}_{i}")
                A("gpsimd", (lambda i=i, ip=ip: lambda e: e.tensor_tensor(
                    out=sy2[ip][:], in0=sy[:, i * 512:(i + 1) * 512],
                    in1=sy[:, i * 512:(i + 1) * 512], op=ALU.mult))(),
                  reads=["sy"], writes=[("sy2", ip)], tag=f"e7_{cg}_{i}")
                def em_stats(i):
                    ipx = i % 2
                    def stats(e):
                        r = None
                        for jj in range(4):
                            r = e.matmul(p[2][0:1, 0:128], sonescol[:],
                                         sy2[ipx][:, jj * 128:(jj + 1) * 128],
                                         start=(jj == 0), stop=(jj == 3))
                        return r
                    A("tensor", stats, reads=[("sy2", ipx), "onescol"], writes=["p2"],
                      tag=f"st2_{cg}_{i}")
                    spar = i % 2
                    if i == 0:
                        A("vector", lambda e: e.tensor_copy(out=sstat[0][:],
                                                            in_=p[2][0:1, 0:128]),
                          reads=["p2"], writes=[("sstat", 0)], tag=f"sa{cg}_{i}")
                    else:
                        A("vector", (lambda spar=spar: lambda e: e.tensor_add(
                            sstat[spar][:], sstat[1 - spar][:], p[2][0:1, 0:128]))(),
                          reads=[("sstat", 1 - spar), "p2"], writes=[("sstat", spar)],
                          tag=f"sa{cg}_{i}")
                if i > 0:
                    em_stats(i - 1)
                if i == 7:
                    em_stats(7)
            A("vector", lambda e: e.tensor_scalar(
                out=sv1[:], in0=sstat[1][:], scalar1=1.0 / DI, scalar2=EPS,
                op0=ALU.mult, op1=ALU.add),
              reads=[("sstat", 1)], writes=["sv1"], tag=f"rms{cg}")
            A("scalar", lambda e: e.activation(out=sv2[:], in_=sv1[:], func=AF.Ln),
              reads=["sv1", "pk2"], writes=["pk2"])
            A("scalar", lambda e: e.activation(out=srstd1[:], in_=sv2[:], func=AF.Exp,
                                               scale=sconstm05[0:1, :]),
              reads=["pk2", "cpool"], writes=["srstd1"])
            A("tensor", lambda e: e.matmul(p[3][:, 0:128], sones64[0:1, :], srstd1[:],
                                           start=True, stop=True),
              reads=["srstd1", "pk4"], writes=["p3"], tag=f"rE{cg}")
            for i in range(8):
                A("vector", (lambda i=i, tc0=tc0: lambda e: e.tensor_tensor(
                    out=syn[:].rearrange("p (j t) -> p j t", t=T)
                    [:, 4 * i:4 * i + 4, tc0:tc0 + 128],
                    in0=sy[:, i * 512:(i + 1) * 512].rearrange("p (j l) -> p j l", l=128),
                    in1=p[3][:, None, 0:128].broadcast_to([128, 4, 128]),
                    op=ALU.mult))(),
                  reads=["sy", "p3"], writes=["syn"], tag=f"yn{cg}_{i}")

        for half in range(2):
            tcs = (2 * half, 2 * half + 1)
            wo3 = [wo_sb[0], wo_sb[1], sy[:, 0:2048]]
            for k in range(32):
                par = k % 3
                wtok = ("wo_sb", par) if par < 2 else "sy"
                if st == 0 and half == 0:
                    A("gpsimd", (lambda k=k, par=par: dma(wo3[par][:], wo_t[k]))(),
                      reads=[wtok], writes=[wtok], dma="wog", tag=f"woL{k}")
                    A("vector", (lambda k=k, par=par: lambda e: e.tensor_scalar(
                        out=wo3[par][:], in0=wo3[par][:], scalar1=snw[:, k:k + 1],
                        scalar2=None, op0=ALU.mult))(),
                      reads=[wtok, "nw"], writes=[wtok], tag=f"woF{k}")
                    A("sync", (lambda k=k, par=par: dma(wo_sc[k], wo3[par][:]))(),
                      reads=[wtok], writes=[("wosc", k)], dma="wo")
                else:
                    A("sync", (lambda k=k, par=par: dma(wo3[par][:], wo_sc[k]))(),
                      reads=[wtok, ("wosc", k)], writes=[wtok], dma="wo",
                      tag=f"woL{st}_{half}_{k}")
                def opmm(k=k, par=par, tcs=tcs):
                    def f(e):
                        r = None
                        for ti, tcx in enumerate(tcs):
                            for o in range(4):
                                r = e.matmul(p[ti * 4 + o][:, 0:512],
                                             syn[:, k * T + tcx * 128:k * T + tcx * 128 + 128],
                                             wo3[par][:, o * 512:(o + 1) * 512],
                                             start=(k == 0), stop=(k == 31))
                        return r
                    return f
                A("tensor", opmm(), reads=[wtok, "syn"],
                  writes=[f"p{b}" for b in range(8)], tag=f"op{st}_{half}_{k}", cost=1900)
            for ti, tcx in enumerate(tcs):
                x1t = x1[ti]
                for o in range(4):
                    rpar = o % 2
                    A("sync", (lambda t0=t0, tcx=tcx, o=o, rpar=rpar: dma(
                        su_res[rpar][:],
                        u_tm[t0 + tcx * 128:t0 + (tcx + 1) * 128,
                             o * 512:(o + 1) * 512]))(),
                      reads=[("su_res", rpar)], writes=[("su_res", rpar)], dma="u",
                      tag=f"ur{st}_{tcx}_{o}")
                    A("vector", (lambda ti=ti, o=o, rpar=rpar, x1t=x1t: lambda e:
                      e.tensor_add(x1t[:, o * 512:(o + 1) * 512],
                                   p[ti * 4 + o][:, 0:512], su_res[rpar][:]))(),
                      reads=[f"p{ti * 4 + o}", ("su_res", rpar)],
                      writes=[("x1", ti)], tag=f"h1{st}_{tcx}_{o}")
                A("scalar", (lambda x1t=x1t, ti=ti: lambda e: e.activation(
                    out=x_tm[:, 0:2048], in_=x1t[:], func=AF.Copy,
                    accum_out=ssum2_[ti][:]))(),
                  reads=[("x1", ti)], writes=[("ssum", ti), "x_tm"], tag=f"ls{st}_{tcx}")
                A("scalar", (lambda x1t=x1t, ti=ti: lambda e: e.activation(
                    out=x_tm[:, 0:2048], in_=x1t[:], func=AF.Square,
                    accum_out=ssq2_[ti][:]))(),
                  reads=[("x1", ti)], writes=[("ssq", ti), "x_tm"])
                A("vector", (lambda ti=ti: lambda e: e.tensor_scalar_mul(smean2_[ti][:], ssum2_[ti][:], 1.0 / DM))(),
                  reads=[("ssum", ti)], writes=[("smean", ti)])
                A("vector", (lambda ti=ti: lambda e: e.tensor_scalar(
                    out=sm22_[ti][:], in0=smean2_[ti][:], scalar1=smean2_[ti][:], scalar2=EPS,
                    op0=ALU.mult, op1=ALU.subtract))(),
                  reads=[("smean", ti)], writes=[("sm2", ti)])
                A("vector", (lambda ti=ti: lambda e: e.tensor_scalar(
                    out=svar2_[ti][:], in0=ssq2_[ti][:], scalar1=1.0 / DM, scalar2=None,
                    op0=ALU.mult))(),
                  reads=[("ssq", ti)], writes=[("svar", ti)])
                A("vector", (lambda ti=ti: lambda e: e.tensor_sub(svar2_[ti][:], svar2_[ti][:], sm22_[ti][:]))(),
                  reads=[("svar", ti), ("sm2", ti)], writes=[("svar", ti)])
                A("scalar", (lambda ti=ti: lambda e: e.activation(out=svar2_[ti][:], in_=svar2_[ti][:], func=AF.Ln))(),
                  reads=[("svar", ti)], writes=[("svar", ti)])
                A("scalar", (lambda ti=ti: lambda e: e.activation(out=srstd2_[ti][:], in_=svar2_[ti][:],
                                                   func=AF.Exp, scale=sconstm05[:]))(),
                  reads=[("svar", ti), "cpool"], writes=[("srstd", ti)])
                A("vector", (lambda x1t=x1t, ti=ti: lambda e: e.tensor_scalar(
                    out=x1t[:], in0=x1t[:], scalar1=smean2_[ti][:], scalar2=srstd2_[ti][:],
                    op0=ALU.subtract, op1=ALU.mult))(),
                  reads=[("smean", ti), ("srstd", ti), ("x1", ti)], writes=[("x1", ti)],
                  tag=f"ln1_{st}_{tcx}")
                A("vector", (lambda x1t=x1t, ti=ti: lambda e: e.tensor_reduce(
                    out=samax2_[ti][:, :, None],
                    in_=x1t[:].rearrange("p (a b) -> p a b", b=128),
                    axis=mybir.AxisListType.X, op=ALU.max,
                    apply_absolute_value=True))(),
                  reads=[("x1", ti)], writes=[("samax", ti)], tag=f"qa{st}_{tcx}")
                A("vector", (lambda ti=ti: lambda e: e.tensor_scalar_max(
                    samax2_[ti][:], samax2_[ti][:], 1e-4))(),
                  reads=[("samax", ti)], writes=[("samax", ti)])
                A("vector", (lambda ti=ti: lambda e: e.reciprocal(
                    out=srcp2_[ti][:], in_=samax2_[ti][:]))(),
                  reads=[("samax", ti)], writes=[("srcp", ti)])
                A("vector", (lambda ti=ti: lambda e: e.tensor_scalar_mul(
                    sr2242_[ti][:], srcp2_[ti][:], 224.0))(),
                  reads=[("srcp", ti)], writes=[("sr224", ti)])
                A("vector", (lambda ti=ti: lambda e: e.tensor_scalar_mul(
                    sa2242_[ti][:], samax2_[ti][:], 1.0 / 224.0))(),
                  reads=[("samax", ti)], writes=[("sa224", ti)])
                A("vector", (lambda x1t=x1t, ti=ti: lambda e: e.tensor_tensor(
                    out=sq8[:].rearrange("p (a b) -> p a b", b=128),
                    in0=x1t[:].rearrange("p (a b) -> p a b", b=128),
                    in1=sr2242_[ti][:, :, None].broadcast_to([128, 16, 128]),
                    op=ALU.mult))(),
                  reads=[("x1", ti), ("sr224", ti)], writes=["stb"], tag=f"q8{st}_{tcx}")
                A("vector", (lambda ti=ti: lambda e: e.tensor_tensor(
                    out=sxdq[:].rearrange("p (a b) -> p a b", b=128),
                    in0=sq8[:].rearrange("p (a b) -> p a b", b=128),
                    in1=sa2242_[ti][:, :, None].broadcast_to([128, 16, 128]),
                    op=ALU.mult))(),
                  reads=["stb", ("sa224", ti)], writes=["sxdq"], tag=f"dq{st}_{tcx}")
                for iq in range(2):
                    for ii in range(8):
                        i8 = iq * 8 + ii
                        A("tensor", (lambda i8=i8, ii=ii: lambda e: e.transpose(
                            p2b[:, ii * 128:(ii + 1) * 128],
                            sxdq[:, i8 * 128:(i8 + 1) * 128], seyeb[:]))(),
                          reads=["sxdq", "eyeb"], writes=["p2"],
                          tag=f"qT{st}_{tcx}_{i8}")
                    A("scalar", (lambda ti=ti, iq=iq: lambda e: e.copy(
                        out=sxdq_fm[ti][:, iq * 1024:(iq + 1) * 1024],
                        in_=p2b[:, 0:1024]))(),
                      reads=["p2"], writes=[("xdq", ti)], tag=f"qTe{st}_{tcx}_{iq}")
            wdq3 = [wdq_sb[0], wdq_sb[1], sy[:, 2048:3072]]
            for ohalf in range(2):
                for i in range(16):
                    par = i % 3
                    wtok = ("wdq", par) if par < 2 else "sy"
                    A("sync", (lambda i=i, par=par, ohalf=ohalf: dma(
                        wdq3[par][:], wf_sc[i][:, ohalf * 1024:(ohalf + 1) * 1024]))(),
                      reads=[wtok, ("wf", i)], writes=[wtok], dma="wo",
                      tag=f"wfL{st}_{half}_{ohalf}_{i}")
                    def ffmm(i=i, par=par):
                        def f(e):
                            r = None
                            for ti in range(2):
                                for o2 in range(2):
                                    r = e.matmul(p[4 + ti * 2 + o2][:, 0:512],
                                                 sxdq_fm[ti][:, i * 128:(i + 1) * 128],
                                                 wdq3[par][:, o2 * 512:(o2 + 1) * 512],
                                                 start=(i == 0), stop=(i == 15))
                            return r
                        return f
                    A("tensor", ffmm(), reads=[wtok, ("xdq", 0), ("xdq", 1)],
                      writes=[f"p{b}" for b in range(4, 8)], tag=f"ff{st}_{half}_{ohalf}_{i}", cost=1000)
                for ti in range(2):
                    sh2x = sh2 if ti == 0 else sh2b
                    htok = "sxw" if ti == 0 else "sy"
                    for o2 in range(2):
                        o = ohalf * 2 + o2
                        A("scalar", (lambda ti=ti, o2=o2: lambda e: e.copy(
                            out=sfb[:], in_=p[4 + ti * 2 + o2][:, 0:512]))(),
                          reads=[f"p{4 + ti * 2 + o2}"], writes=["scast"],
                          tag=f"fb{st}_{half}_{ti}_{o}")
                        A("vector", (lambda ti=ti, o=o, sh2x=sh2x, htok=htok: lambda e:
                          e.tensor_add(sh2x[:, o * 512:(o + 1) * 512],
                                       x1[ti][:, o * 512:(o + 1) * 512], sfb[:]))(),
                          reads=["scast", ("x1", ti)], writes=[htok],
                          tag=f"h2{st}_{half}_{ti}_{o}")
            for ti, tcx in enumerate(tcs):
                sh2x = sh2 if ti == 0 else sh2b
                htok = "sxw" if ti == 0 else "sy"
                A("scalar", (lambda sh2x=sh2x, ti=ti: lambda e: e.activation(
                    out=x_tm[:, 0:2048], in_=sh2x[:], func=AF.Copy,
                    accum_out=ssum2_[ti][:]))(),
                  reads=[htok], writes=[("ssum", ti), "x_tm"])
                A("scalar", (lambda sh2x=sh2x, ti=ti: lambda e: e.activation(
                    out=x_tm[:, 0:2048], in_=sh2x[:], func=AF.Square,
                    accum_out=ssq2_[ti][:]))(),
                  reads=[htok], writes=[("ssq", ti), "x_tm"])
                A("vector", (lambda ti=ti: lambda e: e.tensor_scalar_mul(smean2_[ti][:], ssum2_[ti][:], 1.0 / DM))(),
                  reads=[("ssum", ti)], writes=[("smean", ti)])
                A("vector", (lambda ti=ti: lambda e: e.tensor_scalar(
                    out=sm22_[ti][:], in0=smean2_[ti][:], scalar1=smean2_[ti][:], scalar2=EPS,
                    op0=ALU.mult, op1=ALU.subtract))(),
                  reads=[("smean", ti)], writes=[("sm2", ti)])
                A("vector", (lambda ti=ti: lambda e: e.tensor_scalar(
                    out=svar2_[ti][:], in0=ssq2_[ti][:], scalar1=1.0 / DM, scalar2=None,
                    op0=ALU.mult))(),
                  reads=[("ssq", ti)], writes=[("svar", ti)])
                A("vector", (lambda ti=ti: lambda e: e.tensor_sub(svar2_[ti][:], svar2_[ti][:], sm22_[ti][:]))(),
                  reads=[("svar", ti), ("sm2", ti)], writes=[("svar", ti)])
                A("scalar", (lambda ti=ti: lambda e: e.activation(out=svar2_[ti][:], in_=svar2_[ti][:],
                                                   func=AF.Ln))(),
                  reads=[("svar", ti)], writes=[("svar", ti)])
                A("scalar", (lambda ti=ti: lambda e: e.activation(out=srstd2_[ti][:], in_=svar2_[ti][:],
                                                   func=AF.Exp, scale=sconstm05[:]))(),
                  reads=[("svar", ti), "cpool"], writes=[("srstd", ti)])
                x1t = x1[ti]
                for o in range(4):
                    A("vector", (lambda o=o, sh2x=sh2x, htok=htok, ti=ti, x1t=x1t:
                      lambda e: e.tensor_scalar(
                        out=x1t[:, o * 512:(o + 1) * 512],
                        in0=sh2x[:, o * 512:(o + 1) * 512],
                        scalar1=smean2_[ti][:], scalar2=srstd2_[ti][:],
                        op0=ALU.subtract, op1=ALU.mult))(),
                      reads=[htok, ("smean", ti), ("srstd", ti), ("x1", ti)],
                      writes=[("x1", ti)], tag=f"ln2_{st}_{tcx}_{o}")
                A("sync", (lambda t0=t0, tcx=tcx, ti=ti, x1t=x1t: dma(
                    out[t0 + tcx * 128:t0 + (tcx + 1) * 128, :], x1t[:]))(),
                  reads=[("x1", ti)], writes=[("x1", ti)],
                  dma="out", tag=f"out{st}_{tcx}")

    s.emit(nc, stack)
    return nc, stack


def _host_prep(inputs, NST=8):
    L = NST * 512
    u = np.ascontiguousarray(np.asarray(inputs["L0_attn_u"], dtype=np.float32))
    Wi = np.asarray(inputs["L0_attn_in_proj_kernel"], dtype=np.float32)
    Wo = np.asarray(inputs["L0_attn_out_proj_kernel"], dtype=np.float32)
    Wf = np.asarray(inputs["L0_ffn_weight"], dtype=np.float32)
    convw = np.asarray(inputs["L0_attn_conv1d_weight"], dtype=np.float32)
    convb = np.asarray(inputs["L0_attn_conv1d_bias"], dtype=np.float32)
    dtb = np.asarray(inputs["L0_attn_dt_bias"], dtype=np.float32)
    alog = np.asarray(inputs["L0_attn_A_log"], dtype=np.float32)
    Dv = np.asarray(inputs["L0_attn_D"], dtype=np.float32)
    nw = np.asarray(inputs["L0_attn_norm_weight"], dtype=np.float32)

    Wi_p = np.zeros((2048, NCT * 128), dtype=np.float32)
    Wi_p[:, :DIP] = Wi
    wi_t = np.ascontiguousarray(
        Wi_p.reshape(16, 128, NCT, 128).transpose(2, 1, 0, 3).reshape(NCT, 128, 2048))
    wo_t = np.ascontiguousarray(Wo.reshape(32, 128, DM))
    wf_t = np.ascontiguousarray(Wf.T.reshape(16, 128, 2048))
    convw_t = np.ascontiguousarray(convw.reshape(CW, 128, 4).transpose(1, 0, 2))
    convb_t = np.ascontiguousarray(convb.reshape(CW, 128).T)
    dp = np.empty((128, 32), dtype=np.float32)
    for pp in range(128):
        for j in range(32):
            dp[pp, j] = Dv[2 * j + pp // 64]
    nw_t = np.ascontiguousarray(nw.reshape(32, 128).T)
    eye_b = np.eye(128, dtype=ml_dtypes.bfloat16)
    eye_f = np.eye(128, dtype=np.float32)
    m01 = np.tril(np.ones((128, 128), np.float32)).T.astype(ml_dtypes.bfloat16)
    ones_col_b = np.ones((128, 1), dtype=ml_dtypes.bfloat16)
    ones_row_f = np.ones((1, 128), dtype=np.float32)
    ind4 = np.zeros((8, 2, 4, 128), dtype=np.float32)
    for k in range(4):
        ind4[k, :, k, :] = 1.0
        ind4[4 + k, :, k, :] = 1.0
    ind4 = ind4.reshape(8, 1024).astype(ml_dtypes.bfloat16)

    shared = dict(wi_t=wi_t, wo_t=wo_t, wf_t=wf_t, convw=convw_t, convb=convb_t,
                  dtb=dtb.reshape(64, 1), alog=alog.reshape(64, 1), dp_in=dp,
                  normw=nw_t, eye_b=eye_b, eye_f=eye_f, mask01=m01,
                  ones_col_b=ones_col_b, ones_row_f=ones_row_f, ind4=ind4)
    in_maps = []
    for b in range(u.shape[0]):
        m = dict(shared)
        m["u_t"] = np.ascontiguousarray(u[b, :L].T.reshape(16, 128, L))
        m["u_tm"] = np.ascontiguousarray(u[b, :L])
        in_maps.append(m)
    return in_maps


def kernel(**inputs):
    NST = 8
    nc, stack = build_program(NST)
    in_maps = _host_prep(inputs, NST)
    try:
        res = run_bass_kernel_spmd(nc, in_maps, core_ids=list(range(len(in_maps))))
    finally:
        stack.close()
    return np.stack([np.asarray(r["out"], dtype=np.float32) for r in res.results])


# version 23
# speedup vs baseline: 1.0448x; 1.0092x over previous
import numpy as np
import ml_dtypes
import concourse.bass as bass
import concourse.mybir as mybir
from concourse.bass_utils import run_bass_kernel_spmd

F32 = mybir.dt.float32
BF16 = mybir.dt.bfloat16
F8 = mybir.dt.float8e4
AF = mybir.ActivationFunctionType
ALU = mybir.AluOpType

DM = 2048
DIP = 8512
DI = 4096
NCT = 67
NK = DM // 128
CW = 34
EPS = 1e-5


class Sched:

    DEF_COST = {"sync": 1200, "gpsimd": 1600, "vector": 650, "scalar": 650,
                "tensor": 400}

    def __init__(self):
        self.ops = []
        self.last_write = {}
        self.readers = {}

    def add(self, engine, fn, reads=(), writes=(), dma=None, tag="", cost=None):
        idx = len(self.ops)
        deps = set()
        for t in reads:
            if t in self.last_write:
                deps.add(self.last_write[t])
        for t in writes:
            if t in self.last_write:
                deps.add(self.last_write[t])
            for r in self.readers.get(t, ()):
                deps.add(r)
        self.ops.append(dict(i=idx, engine=engine, fn=fn, deps=deps, dma=dma,
                             inc=False, tag=tag, r=tuple(reads), w=tuple(writes),
                             cost=cost if cost else self.DEF_COST[engine]))
        for t in reads:
            self.readers.setdefault(t, []).append(idx)
        for t in writes:
            self.last_write[t] = idx
            self.readers[t] = []
        return idx

    def reorder(self):
        import heapq
        ops = self.ops
        n = len(ops)
        assert all(op["i"] == k for k, op in enumerate(ops))
        children = [[] for _ in range(n)]
        byid0 = {op["i"]: op for op in ops}
        ndep = [0] * n
        for op in ops:
            for d in op["deps"]:
                children[d].append(op["i"])
                ndep[op["i"]] += 1
        LAT = 500
        ISSUE = 600
        cp = [0] * n
        for i in range(n - 1, -1, -1):
            m = 0
            ei = ops[i]["engine"]
            for c in children[i]:
                v = cp[c] + (LAT if ops[c]["engine"] != ei else 0)
                if v > m:
                    m = v
            cp[i] = ops[i]["cost"] + m
        engines = ["sync", "gpsimd", "vector", "scalar", "tensor"]
        ready = {e: [] for e in engines}
        for op in ops:
            if ndep[op["i"]] == 0:
                heapq.heappush(ready[op["engine"]], (-cp[op["i"]], op["i"]))
        etime = {e: 0 for e in engines}
        done = [0] * n
        order = []
        remaining = n
        while remaining:
            best = None
            for e in engines:
                while ready[e]:
                    i = ready[e][0][1]
                    break
                if not ready[e]:
                    continue
                i = ready[e][0][1]
                start = etime[e]
                for d in ops[i]["deps"]:
                    dd = done[d] + (LAT if byid0[d]["engine"] != e else 0)
                    if dd > start:
                        start = dd
                if best is None or start < best[0]:
                    best = (start, e, i)
            start, e, i = best
            heapq.heappop(ready[e])
            fin = start + ops[i]["cost"]
            if ops[i]["dma"] is not None:
                etime[e] = start + ISSUE
            else:
                etime[e] = fin
            done[i] = fin
            order.append(i)
            remaining -= 1
            for c in children[i]:
                ndep[c] -= 1
                if ndep[c] == 0:
                    heapq.heappush(ready[ops[c]["engine"]], (-cp[c], c))
        self.ops = [ops[i] for i in order]
        return max(etime.values())

    def emit(self, nc, stack):
        est = self.reorder()
        import os as _os
        if _os.environ.get("DBG_TOK"):
            t = _os.environ["DBG_TOK"]
            for k, op in enumerate(self.ops):
                if t in op["r"] or t in op["w"]:
                    rw = "W" if t in op["w"] else "R"
                    print(f"[tok] pos={k} id={op['i']} {rw} {op['engine']:7s} "
                          f"tag={op['tag']} r={op['r']} w={op['w']}")
        print(f"[sched] est span {est/1e6:.2f} ms over {len(self.ops)} ops")
        ops = self.ops
        byid = {op["i"]: op for op in ops}
        for op in ops:
            for d in op["deps"]:
                byid[d]["inc"] = True
        for op in ops:
            if op["dma"] is not None:
                op["inc"] = True
        engines = ["sync", "gpsimd", "vector", "scalar", "tensor"]
        dma_groups = sorted({op["dma"] for op in ops if op["dma"] is not None})
        R = 4
        esem = {e: stack.enter_context(nc.semaphore(f"e_{e}")) for e in engines}
        dsem = {(g, r): stack.enter_context(nc.semaphore(f"d_{g}{r}"))
                for g in dma_groups for r in range(R)}
        ecnt = {e: 0 for e in engines}
        dcnt = {g: 0 for g in dma_groups}
        for op in ops:
            if op["dma"] is not None:
                k = dcnt[op["dma"]]
                dcnt[op["dma"]] += 1
                sub, rnd = k % R, k // R
                op["ev"] = ("d", (op["dma"], sub), 16 * (rnd + 1))
                op["issue_wait"] = ((op["dma"], sub), 16 * rnd) if rnd else None
            elif op["inc"]:
                ecnt[op["engine"]] += 1
                op["ev"] = ("e", op["engine"], ecnt[op["engine"]])
                op["issue_wait"] = None
            else:
                op["ev"] = None
                op["issue_wait"] = None
        wm = {}
        for op in ops:
            need = {}
            for d in op["deps"]:
                pr = byid[d]
                if pr["ev"] is None:
                    continue
                kind, key, val = pr["ev"]
                sk = (kind, key)
                need[sk] = max(need.get(sk, 0), val)
            if op["issue_wait"] is not None:
                sk, val = ("d", op["issue_wait"][0]), op["issue_wait"][1]
                need[sk] = max(need.get(sk, 0), val)
            waits = []
            for sk, val in sorted(need.items(), key=str):
                w = (op["engine"], sk)
                if wm.get(w, -1) >= val:
                    continue
                wm[w] = val
                sem = esem[sk[1]] if sk[0] == "e" else dsem[sk[1]]
                waits.append((sem, val))
            op["waits"] = waits
        self.totals = ([(esem[e], ecnt[e]) for e in engines if ecnt[e]] +
                       [(dsem[(g, r)], 16 * (dcnt[g] // R + (1 if r < dcnt[g] % R else 0)))
                        for g in dma_groups for r in range(R)
                        if dcnt[g] // R + (1 if r < dcnt[g] % R else 0) > 0])
        blk = stack.enter_context(nc.Block())

        def body_for(name):
            def body(e):
                for op in ops:
                    if op["engine"] != name:
                        continue
                    for sem, val in op["waits"]:
                        e.wait_ge(sem, val)
                    inst = op["fn"](e)
                    if op["dma"] is not None:
                        assert inst is not None, f"op {op['i']} {op['tag']} returned None"
                        inst.then_inc(dsem[op["ev"][1]], 16)
                    elif op["inc"]:
                        assert inst is not None, f"op {op['i']} {op['tag']} returned None"
                        inst.then_inc(esem[op["engine"]], 1)
            return body

        blk.sync(body_for("sync"))
        blk.gpsimd(body_for("gpsimd"))
        blk.vector(body_for("vector"))
        blk.scalar(body_for("scalar"))
        blk.tensor(body_for("tensor"))
        totals = self.totals

        def fin(e):
            for sem, val in totals:
                e.wait_ge(sem, val)
        blk.sync(fin)


def build_program(NST=8):
    T = 512
    L = NST * T
    NCH = 4
    nc = bass.Bass("TRN2", target_bir_lowering=False)

    def inp(name, shape, dt=F32):
        return nc.dram_tensor(name, list(shape), dt, kind="ExternalInput").ap()

    u_t = inp("u_t", (NK, 128, L))
    u_tm = inp("u_tm", (L, DM))
    wi_t = inp("wi_t", (NCT, 128, NK * 128))
    wo_t = inp("wo_t", (32, 128, DM))
    wf_t = inp("wf_t", (16, 128, 2048))
    convw = inp("convw", (128, CW, 4))
    convb = inp("convb", (128, CW))
    dtb = inp("dtb", (64, 1))
    alog = inp("alog", (64, 1))
    dp_in = inp("dp_in", (128, 32))
    normw = inp("normw", (128, 32))
    eye_b = inp("eye_b", (128, 128), BF16)
    eye_f = inp("eye_f", (128, 128))
    mask01 = inp("mask01", (128, 128), BF16)
    ones_col_b = inp("ones_col_b", (128, 1), BF16)
    ones_row_f = inp("ones_row_f", (1, 128))
    ind4 = inp("ind4", (8, 1024), BF16)

    out = nc.dram_tensor("out", [L, DM], F32, kind="ExternalOutput").ap()

    wi_sc = nc.dram_tensor("wi_sc", [NCT, 128, NK * 128], BF16, kind="Internal").ap()
    wo_sc = nc.dram_tensor("wo_sc", [32, 128, DM], BF16, kind="Internal").ap()
    wf_sc = nc.dram_tensor("wf_sc", [16, 128, 2048], BF16, kind="Internal").ap()
    z_sc = nc.dram_tensor("z_sc", [32, 128, L], BF16, kind="Internal").ap()
    x_sc = nc.dram_tensor("x_sc", [32, 128, L], BF16, kind="Internal").ap()
    fl_sc = nc.dram_tensor("fl_sc", [3, 64, 128], F32, kind="Internal").ap()
    fl_b = nc.dram_tensor("fl_b", [64, 128], BF16, kind="Internal").ap()
    fl_b2 = nc.dram_tensor("fl_b2", [2, 64, 128], BF16, kind="Internal").ap()
    fl_b3 = nc.dram_tensor("fl_b3", [2, 64, 128], BF16, kind="Internal").ap()

    s = Sched()
    from contextlib import ExitStack
    stack = ExitStack()

    def sb(name, shape, dt):
        return nc.alloc_sbuf_tensor(name, list(shape), dt)

    su_bf = sb("su_bf", [128, NK * T], BF16)
    wi_sb = [sb(f"wi_sb{i}", [128, NK * 128], BF16) for i in range(2)]
    strip2 = [sb(f"strip{i}", [128, 515], BF16) for i in range(2)]
    halo = sb("halo", [128, CW * 3], BF16)
    conv_a = sb("conv_a", [128, T], F32)
    conv_b2 = sb("conv_b2", [128, T], F32)
    sB_fm = sb("sB_fm", [128, T], BF16)
    sC_fm = sb("sC_fm", [128, T], BF16)
    sxst2 = [sb(f"sxst{i}", [128, T], BF16) for i in range(2)]
    sxch = sb("sxch", [128, 32 * 128], BF16)
    sdt0 = sb("sdt0", [64, T], F32)
    sdt = sb("sdt", [64, T], F32)
    sdt_t1 = sb("sdt_t1", [64, T], F32)
    sdt_e = sb("sdt_e", [64, T], F32)
    sacs = sb("sacs", [64, T], F32)
    slnma = sb("slnma", [64, T], F32)
    ssig = sb("ssig", [64, T], F32)
    sw_fm = sb("sw_fm", [64, 128], F32)
    scd = sb("scd", [64, 4], F32)
    sones64 = sb("sones64", [64, 128], F32)
    sconst1 = sb("sconst1", [128, 1], F32)
    sconstm05 = sb("sconstm05", [128, 1], F32)
    sconstm1 = sb("sconstm1", [128, 1], F32)
    lnd5x = [sb(f"lnd5_{i}", [10, 256], BF16) for i in range(2)]
    shilo = sb("shilo", [64, 256], BF16)
    rhs5 = sb("rhs5", [10, 1024], BF16)
    x_tm = sb("x_tm", [128, 32 * 128], BF16)
    sxw = sb("sxw", [128, 32 * 128], BF16)
    sBtm = sb("sBtm", [128, 128], BF16)
    swT = sb("swT", [128, 64], F32)
    ssm = sb("ssm", [128, 128], BF16)
    smind2 = [sb(f"smind{i}", [128, 512], F32) for i in range(2)]
    sexpD2 = [sb(f"sexpD{i}", [128, 512], BF16) for i in range(2)]
    sSL = [sb(f"sSL{i}", [128, 512], BF16) for i in range(2)]
    S = sb("S", [128, 32 * 128], F32)
    scast = sb("scast", [128, 512], BF16)
    cdP = sb("cdP", [128, 64], F32)
    sscan = sb("sscan", [128, 512], F32)
    s_sF = [sb(f"s_sF{i}", [128, 512], BF16) for i in range(2)]
    sta = sb("sta", [128, 512], F32)
    sq8 = sb("sq8", [128, 2048], F8)
    stb = sq8[:].bitcast(F32)
    sxd = sb("sxd", [128, 512], BF16)
    sg = sb("sg", [128, 512], F32)
    sy = sb("sy", [128, 32 * 128], BF16)
    sy2 = [sb(f"sy2_{i}", [128, 512], BF16) for i in range(2)]
    sstat = [sb(f"sstat{i}", [1, 128], F32) for i in range(2)]
    sv1 = sb("sv1", [1, 128], F32)
    sv2 = sdt_e[0:1, 0:128]
    srstd1 = sb("srstd1", [1, 128], F32)
    syn = sb("syn", [128, 32 * T], BF16)
    sz_ep = [sb(f"sz_ep{i}", [128, 512], BF16) for i in range(2)]
    wo_sb = [sb(f"wo_sb{i}", [128, DM], BF16) for i in range(2)]
    su_res = [sb(f"su_res{i}", [128, 512], F32) for i in range(2)]
    x1 = [sb(f"x1_{i}", [128, DM], F32) for i in range(2)]
    ssum2_ = [sb(f"lnsum{i}", [128, 1], F32) for i in range(2)]
    ssq2_ = [sb(f"lnsq{i}", [128, 1], F32) for i in range(2)]
    smean2_ = [sb(f"lnmean{i}", [128, 1], F32) for i in range(2)]
    sm22_ = [sb(f"lnm2{i}", [128, 1], F32) for i in range(2)]
    svar2_ = [sb(f"lnvar{i}", [128, 1], F32) for i in range(2)]
    srstd2_ = [sb(f"lnrstd{i}", [128, 1], F32) for i in range(2)]
    samax2_ = [sb(f"qamax{i}", [128, 16], F32) for i in range(2)]
    srcp2_ = [sb(f"qrcp{i}", [128, 16], F32) for i in range(2)]
    sr2242_ = [sb(f"qr224{i}", [128, 16], F32) for i in range(2)]
    sa2242_ = [sb(f"qa224{i}", [128, 16], F32) for i in range(2)]
    samax = samax2_[0]
    srcp = srcp2_[0]
    sr224 = sr2242_[0]
    sa224 = sa2242_[0]
    sxdq = sb("sxdq", [128, DM], BF16)
    sxdq_fm = [sb(f"sxdq_fm{i}", [128, 16 * 128], BF16) for i in range(2)]
    wdq_sb = [sb(f"wdq_sb{i}", [128, 1024], BF16) for i in range(2)]
    sfb = scast
    sh2 = sxw[:].bitcast(F32)
    sh2b = sy[:].bitcast(F32)
    samx2 = samax
    samx3 = sb("samx3", [16, 1], F32)
    samx4 = sa224[0:1, :]
    swr = srcp
    swri = sr224
    snw = sb("snw", [128, 32], F32)
    sDp = sb("sDp", [128, 32], BF16)
    sconvw = sb("sconvw", [128, CW * 4], F32)
    sconvb = sb("sconvb", [128, CW], F32)
    sdtb = sb("sdtb", [64, 1], F32)
    salog = sb("salog", [64, 1], F32)
    saneg = sb("saneg", [64, 1], F32)
    seyeb = sb("seyeb", [128, 128], BF16)
    seyef = sb("seyef", [128, 128], F32)
    smask = sb("smask", [128, 128], BF16)
    sonescol = sb("sonescol", [128, 1], BF16)
    sonesrow = None

    p = [nc.alloc_psum_tensor(f"p{i}", [128, 512], F32) for i in range(8)]
    p2b = p[2][:].bitcast(BF16)

    A = s.add

    def dma(e_dst, src):
        def f(e):
            return e.dma_start(out=e_dst, in_=src)
        return f

    A("sync", dma(sconvw[:], convw.rearrange("p j k -> p (j k)")), writes=["convw"], dma="m")
    A("sync", dma(sconvb[:], convb), writes=["convb"], dma="m")
    A("sync", dma(sdtb[:], dtb), writes=["dtb"], dma="m")
    A("sync", dma(salog[:], alog), writes=["alog"], dma="m")
    A("gpsimd", dma(sDp[:], dp_in), writes=["dp"], dma="mg")
    A("sync", dma(snw[:], normw), writes=["nw"], dma="m")
    A("sync", dma(seyeb[:], eye_b), writes=["eyeb"], dma="m")
    A("sync", dma(seyef[:], eye_f), writes=["eyef"], dma="m")
    A("sync", dma(smask[:], mask01), writes=["mask"], dma="m")
    A("sync", dma(sonescol[:], ones_col_b), writes=["onescol"], dma="m")
    A("sync", dma(rhs5[2:10, :], ind4), writes=["rhs5c"], dma="m")
    A("vector", lambda e: e.memset(sones64[:], 1.0), writes=["pk4"])
    A("vector", lambda e: e.memset(sconst1[:], 1.0), writes=["cpool"])
    A("vector", lambda e: e.memset(sconstm05[:], -0.5), writes=["cpool"])
    A("vector", lambda e: e.memset(sconstm1[:], -1.0), writes=["cpool"])
    A("vector", lambda e: e.memset(lnd5x[0][0:2, :], 1.0), writes=["lnd5r0"])
    A("vector", lambda e: e.memset(lnd5x[1][0:2, :], 1.0), writes=["lnd5r0"])
    A("vector", lambda e: e.memset(S[:], 0.0), writes=[("S", i) for i in range(8)])
    A("vector", lambda e: e.memset(halo[:], 0.0), writes=["halo"])
    A("scalar", lambda e: e.activation(out=saneg[:], in_=salog[:], func=AF.Exp),
      reads=["alog"], writes=["aneg"])
    A("vector", lambda e: e.tensor_scalar_mul(saneg[:], saneg[:], -1.0),
      reads=["aneg"], writes=["aneg"])

    for i in range(16):
        A("sync", dma(sh2[:], wf_t[i]), reads=["sxw"], writes=["sxw"], dma="w")
        A("vector", lambda e: e.tensor_reduce(
            out=samx2[:, :, None], in_=sh2[:].rearrange("p (a b) -> p a b", b=128),
            axis=mybir.AxisListType.X, op=ALU.max, apply_absolute_value=True),
          reads=["sxw"], writes=["samax"])
        A("tensor", lambda e: e.transpose(p[3][0:16, 0:128], samx2[:], seyef[:]),
          reads=["samax", "eyef"], writes=["p3"])
        A("vector", lambda e: e.tensor_reduce(
            out=samx3[:, None, :], in_=p[3][0:16, None, 0:128],
            axis=mybir.AxisListType.X, op=ALU.max),
          reads=["p3"], writes=["samx3"])
        A("vector", lambda e: e.tensor_scalar_max(samx3[:], samx3[:], 1e-4),
          reads=["samx3"], writes=["samx3"])
        A("tensor", lambda e: e.transpose(p[4][0:1, 0:16], samx3[:], seyef[0:16, 0:16]),
          reads=["samx3", "eyef"], writes=["p4"])
        A("vector", lambda e: e.tensor_copy(out=samx4[:], in_=p[4][0:1, 0:16]),
          reads=["p4"], writes=["sa224"])
        A("tensor", lambda e: e.matmul(p[4][:, 0:16], sones64[0:1, :], samx4[:],
                                       start=True, stop=True),
          reads=["sa224", "pk4"], writes=["p4"])
        A("vector", lambda e: e.reciprocal(out=swr[:], in_=p[4][:, 0:16]),
          reads=["p4"], writes=["srcp"])
        A("vector", lambda e: e.tensor_scalar_mul(swri[:], swr[:], 224.0),
          reads=["srcp"], writes=["sr224"])
        A("vector", lambda e: e.tensor_scalar_mul(swr[:], p[4][:, 0:16], 1.0 / 224.0),
          reads=["p4"], writes=["srcp"])
        A("vector", lambda e: e.tensor_tensor(
            out=sq8[:].rearrange("p (a b) -> p a b", b=128),
            in0=sh2[:].rearrange("p (a b) -> p a b", b=128),
            in1=swri[:, :, None].broadcast_to([128, 16, 128]), op=ALU.mult),
          reads=["sxw", "sr224"], writes=["stb"])
        A("vector", lambda e: e.tensor_tensor(
            out=sxdq[:].rearrange("p (a b) -> p a b", b=128),
            in0=sq8[:].rearrange("p (a b) -> p a b", b=128),
            in1=swr[:, :, None].broadcast_to([128, 16, 128]), op=ALU.mult),
          reads=["stb", "srcp"], writes=["sxdq"])
        A("sync", dma(wf_sc[i], sxdq[:]), reads=["sxdq"], writes=[("wf", i), "sxdq"],
          dma="w")

    for st in range(NST):
        t0 = st * T
        A("gpsimd", dma(su_bf[:], u_t[:, :, t0:t0 + T].rearrange("k p t -> p k t")),
          writes=["su_bf"], dma="ug", tag=f"u{st}", cost=12000)

        wi3 = [wi_sb[0], wi_sb[1], sxdq]
        for c in range(NCT):
            par = c % 3
            wtok = ("wi_sb", par) if par < 2 else "sxdq"
            if st == 0:
                A("gpsimd", (lambda c=c, par=par: dma(wi3[par][:], wi_t[c]))(),
                  reads=[wtok], writes=[wtok], dma="wig", tag=f"wiL{c}")
                A("sync", (lambda c=c, par=par: dma(wi_sc[c], wi3[par][:]))(),
                  reads=[wtok], writes=[("wisc", c)], dma="wi")
            else:
                A("sync", (lambda c=c, par=par: dma(wi3[par][:], wi_sc[c]))(),
                  reads=[wtok, ("wisc", c)], writes=[wtok], dma="wi",
                  tag=f"wiL{st}_{c}")
            pb = c % 2
            def mk_mm(c=c, par=par, pb=pb):
                def f(e):
                    r = None
                    for k in range(NK):
                        r = e.matmul(p[pb][:], wi3[par][:, k * 128:(k + 1) * 128],
                                     su_bf[:, k * T:(k + 1) * T],
                                     start=(k == 0), stop=(k == NK - 1))
                    return r
                return f
            A("tensor", mk_mm(), reads=[wtok, "su_bf"], writes=[f"p{pb}"],
              tag=f"ip{st}_{c}", cost=3600)
            if c < 32:
                sp = c % 2
                A("vector", (lambda pb=pb, sp=sp: lambda e: e.tensor_copy(
                    out=sxst2[sp][:], in_=p[pb][:]))(),
                  reads=[f"p{pb}", ("sxst", sp)], writes=[("sxst", sp)],
                  tag=f"zev{st}_{c}")
                A("sync", (lambda c=c, t0=t0, sp=sp: dma(
                    z_sc[c][:, t0:t0 + T], sxst2[sp][:]))(),
                  reads=[("sxst", sp)], writes=[("z", st, c), ("sxst", sp)], dma="z")
            elif c < 66:
                j = c - 32
                sp = c % 2
                strip = strip2[sp]
                stok = ("strip", sp)
                A("vector", (lambda j=j, strip=strip, sp=sp: lambda e: e.tensor_copy(
                    out=strip[:, 0:3], in_=halo[:, 3 * j:3 * j + 3]))(),
                  reads=["halo", stok], writes=[stok], tag=f"hin{st}_{j}")
                A("scalar", (lambda pb=pb, strip=strip: lambda e: e.copy(
                    out=strip[:, 3:515], in_=p[pb][:]))(),
                  reads=[f"p{pb}", stok], writes=[stok], tag=f"xbev{st}_{j}")
                A("vector", (lambda j=j, strip=strip: lambda e: e.tensor_copy(
                    out=halo[:, 3 * j:3 * j + 3], in_=strip[:, 512:515]))(),
                  reads=[stok], writes=["halo"], tag=f"hout{st}_{j}")
                A("vector", (lambda j=j, strip=strip: lambda e: e.tensor_scalar(
                    out=conv_a[:], in0=strip[:, 0:T], scalar1=sconvw[:, 4 * j:4 * j + 1],
                    scalar2=None, op0=ALU.mult))(),
                  reads=[stok, "convw"], writes=["conv_a"], tag=f"cv0_{st}_{j}")
                for k in (1, 2):
                    src_, dst_ = (conv_a, conv_b2) if k == 1 else (conv_b2, conv_a)
                    A("vector", (lambda j=j, k=k, src_=src_, dst_=dst_, strip=strip:
                      lambda e: e.scalar_tensor_tensor(
                          out=dst_[:], in0=strip[:, k:k + T],
                          scalar=sconvw[:, 4 * j + k:4 * j + k + 1], in1=src_[:],
                          op0=ALU.mult, op1=ALU.add))(),
                      reads=[stok, "convw", "conv_a" if k == 1 else "conv_b2"],
                      writes=["conv_b2" if k == 1 else "conv_a"], tag=f"cv{k}_{st}_{j}")
                A("vector", (lambda j=j, strip=strip: lambda e: e.scalar_tensor_tensor(
                    out=conv_b2[:], in0=strip[:, 3:3 + T],
                    scalar=sconvw[:, 4 * j + 3:4 * j + 4], in1=conv_a[:],
                    op0=ALU.mult, op1=ALU.add))(),
                  reads=[stok, "convw", "conv_a"], writes=["conv_b2"],
                  tag=f"cv3_{st}_{j}")
                if j < 32:
                    A("scalar", (lambda j=j, sp=sp: lambda e: e.activation(
                        out=sxst2[sp][:], in_=conv_b2[:], func=AF.Silu,
                        bias=sconvb[:, j:j + 1]))(),
                      reads=["conv_b2", "convb", ("sxst", sp)], writes=[("sxst", sp)],
                      tag=f"silu{st}_{j}")
                    A("sync", (lambda j=j, t0=t0, sp=sp: dma(
                        x_sc[j][:, t0:t0 + T], sxst2[sp][:]))(),
                      reads=[("sxst", sp)], writes=[("x", st, j), ("sxst", sp)],
                      dma="z")
                elif j == 32:
                    A("scalar", (lambda j=j: lambda e: e.activation(
                        out=sB_fm[:], in_=conv_b2[:], func=AF.Silu,
                        bias=sconvb[:, j:j + 1]))(),
                      reads=["conv_b2", "convb", "sB_fm"], writes=["sB_fm"],
                      tag=f"silu{st}_{j}")
                else:
                    A("scalar", (lambda j=j: lambda e: e.activation(
                        out=sC_fm[:], in_=conv_b2[:], func=AF.Silu,
                        bias=sconvb[:, j:j + 1]))(),
                      reads=["conv_b2", "convb", "sC_fm"], writes=["sC_fm"],
                      tag=f"silu{st}_{j}")
            else:
                A("vector", (lambda pb=pb: lambda e: e.tensor_scalar(
                    out=sdt0[:], in0=p[pb][0:64, :], scalar1=sdtb[:], scalar2=None,
                    op0=ALU.add))(),
                  reads=[f"p{pb}", "dtb"], writes=["pk1"], tag=f"dt0_{st}")
                A("scalar", lambda e: e.activation(out=sdt_t1[:], in_=sdt0[:],
                                                   func=AF.Abs),
                  reads=["pk1"], writes=["pk2"])
                A("scalar", lambda e: e.activation(out=sdt_e[:], in_=sdt_t1[:],
                                                   func=AF.Exp,
                                                   scale=sconstm1[0:64, :]),
                  reads=["pk2", "cpool"], writes=["pk2"])
                A("scalar", lambda e: e.activation(out=sdt_t1[:], in_=sdt_e[:],
                                                   func=AF.Ln, bias=sconst1[0:64, :]),
                  reads=["pk2", "cpool"], writes=["pk2"])
                A("vector", lambda e: e.tensor_scalar(
                    out=sdt_e[:], in0=sdt0[:], scalar1=0.0, scalar2=None,
                    op0=ALU.max),
                  reads=["pk1"], writes=["pk2"])
                A("vector", lambda e: e.tensor_add(sdt[:], sdt_t1[:], sdt_e[:]),
                  reads=["pk2"], writes=["pk1"], tag=f"dt_{st}")

        A("vector", lambda e: e.tensor_scalar(out=sdt0[:], in0=sdt[:], scalar1=1e-30,
                                              scalar2=None, op0=ALU.max),
          reads=["pk1"], writes=["pk1"])
        A("scalar", lambda e: e.activation(out=sdt_e[:], in_=sdt0[:], func=AF.Ln),
          reads=["pk1"], writes=["pk2"])
        A("vector", lambda e: e.tensor_scalar(out=sdt_t1[:], in0=sdt[:], scalar1=saneg[:],
                                              scalar2=None, op0=ALU.mult),
          reads=["pk1", "aneg"], writes=["pk2"], tag=f"dA_{st}")
        for cch in range(NCH):
            A("vector", (lambda cch=cch: lambda e: e.tensor_tensor_scan(
                out=sacs[:, cch * 128:(cch + 1) * 128], data0=sones64[:],
                data1=sdt_t1[:, cch * 128:(cch + 1) * 128], initial=0.0,
                op0=ALU.mult, op1=ALU.add))(),
              reads=["pk2", "pk4"], writes=["pk3"], tag=f"acs{st}_{cch}")
        A("vector", lambda e: e.tensor_sub(slnma[:], sdt_e[:], sacs[:]),
          reads=["pk2", "pk3"], writes=["pk3"])
        A("scalar", lambda e: e.activation(out=ssig[:], in_=sacs[:], func=AF.Exp),
          reads=["pk3"], writes=["pk4"])
        A("scalar", lambda e: e.activation(
            out=scd[:, :, None],
            in_=sacs[:].rearrange("h (c l) -> h c l", l=128)[:, :, 127:128],
            func=AF.Exp),
          reads=["pk3"], writes=["pk4"], tag=f"cd{st}")

        for cch in range(NCH):
            cg = st * NCH + cch
            tc0 = cch * 128
            A("scalar", (lambda cch=cch: lambda e: e.activation(
                out=sw_fm[:], in_=slnma[:, cch * 128:(cch + 1) * 128], func=AF.Exp,
                bias=sacs[:, cch * 128 + 127:cch * 128 + 128]))(),
              reads=["pk3"], writes=["pk4"], tag=f"wfm{cg}")
            A("vector", (lambda cch=cch: lambda e: e.tensor_copy(
                out=shilo[:, 0:128], in_=sacs[:, cch * 128:(cch + 1) * 128]))(),
              reads=["pk3"], writes=["shilo"], tag=f"hiA{cg}")
            A("vector", (lambda cch=cch: lambda e: e.tensor_sub(
                shilo[:, 128:256], sacs[:, cch * 128:(cch + 1) * 128],
                shilo[:, 0:128]))(),
              reads=["pk3", "shilo"], writes=["shilo"], tag=f"loA{cg}")
            A("sync", dma(fl_b2.rearrange("a h s -> h a s"),
                          shilo[:].rearrange("h (a s) -> h a s", s=128)),
              reads=["shilo"], writes=["fl0"], dma="f")
            A("vector", (lambda cch=cch: lambda e: e.tensor_copy(
                out=shilo[:, 0:128], in_=slnma[:, cch * 128:(cch + 1) * 128]))(),
              reads=["pk3", "shilo", "fl0"], writes=["shilo"], tag=f"hiL{cg}")
            A("vector", (lambda cch=cch: lambda e: e.tensor_sub(
                shilo[:, 128:256], slnma[:, cch * 128:(cch + 1) * 128],
                shilo[:, 0:128]))(),
              reads=["pk3", "shilo"], writes=["shilo"], tag=f"loL{cg}")
            A("sync", dma(fl_b3.rearrange("a h s -> h a s"),
                          shilo[:].rearrange("h (a s) -> h a s", s=128)),
              reads=["shilo"], writes=["fl1"], dma="f")
            A("gpsimd", (lambda cch=cch: dma(fl_b, ssig[:, cch * 128:(cch + 1) * 128]))(),
              reads=["pk4"], writes=["fl2"], dma="mg")
            A("tensor", (lambda cch=cch: lambda e: e.transpose(
                p[3][0:1, 0:64], scd[:, cch:cch + 1], seyef[0:64, 0:64]))(),
              reads=["pk4", "eyef"], writes=["p3"], tag=f"cdT{cg}")
            A("vector", lambda e: e.tensor_copy(out=sv1[0:1, 0:64],
                                                in_=p[3][0:1, 0:64]),
              reads=["p3"], writes=["sv1"])
            A("tensor", lambda e: e.matmul(p[3][:, 0:64], sones64[0:1, :], sv1[0:1, 0:64],
                                           start=True, stop=True),
              reads=["sv1", "pk4"], writes=["p3"])
            A("vector", lambda e: e.tensor_copy(out=cdP[:], in_=p[3][:, 0:64]),
              reads=["p3"], writes=["cdP"])
            A("tensor", (lambda cch=cch: lambda e: e.matmul(
                p[3][:, 0:128], sB_fm[:, cch * 128:(cch + 1) * 128],
                sC_fm[:, cch * 128:(cch + 1) * 128], start=True, stop=True))(),
              reads=["sB_fm", "sC_fm"], writes=["p3"], tag=f"sc{cg}")
            A("vector", lambda e: e.tensor_tensor(out=ssm[:], in0=p[3][:, 0:128],
                                                  in1=smask[:], op=ALU.mult),
              reads=["p3", "mask"], writes=["ssm"])
            A("sync", (lambda t0=t0, tc0=tc0: dma(
                sxch[:].rearrange("p (j t) -> p j t", t=128),
                x_sc[:, :, t0 + tc0:t0 + tc0 + 128].rearrange("j p t -> p j t")))(),
              reads=[("x", st, j) for j in range(32)] + ["sxch"],
              writes=["sxch"], dma="z", tag=f"xl{cg}", cost=3500)
            for jq in range(4):
                for jj in range(8):
                    j = jq * 8 + jj
                    A("tensor", (lambda j=j, jj=jj: lambda e: e.transpose(
                        p2b[:, jj * 128:(jj + 1) * 128],
                        sxch[:, j * 128:(j + 1) * 128], seyeb[:]))(),
                      reads=["sxch", "eyeb"], writes=["p2"], tag=f"xT{cg}_{j}")
                A("scalar", (lambda jq=jq: lambda e: e.copy(
                    out=x_tm[:, jq * 1024:(jq + 1) * 1024], in_=p2b[:, 0:1024]))(),
                  reads=["p2"], writes=["x_tm"], tag=f"xTe{cg}_{jq}")
            A("tensor", (lambda tc0=tc0: lambda e: e.transpose(
                p2b[:, 0:128], sB_fm[:, tc0:tc0 + 128], seyeb[:]))(),
              reads=["sB_fm", "eyeb"], writes=["p2"], tag=f"BT{cg}")
            A("scalar", lambda e: e.copy(out=sBtm[:], in_=p2b[:, 0:128]),
              reads=["p2"], writes=["sBtm"])
            A("tensor", lambda e: e.transpose(p[3][0:128, 0:64], sw_fm[:], seyef[0:64, 0:64]),
              reads=["pk4", "eyef"], writes=["p3"], tag=f"wT{cg}")
            A("vector", lambda e: e.tensor_copy(out=swT[:], in_=p[3][:, 0:64]),
              reads=["p3"], writes=["swT"])
            A("vector", lambda e: e.tensor_tensor(
                out=sxw[:].rearrange("s (h q) -> s h q", q=64),
                in0=x_tm[:].rearrange("s (h q) -> s h q", q=64),
                in1=swT[:, :, None].broadcast_to([128, 64, 64]), op=ALU.mult),
              reads=["x_tm", "swT"], writes=["sxw"], tag=f"xw{cg}")

            for i in range(8):
                ip = i % 2
                for ab in range(2):
                    A("sync", (lambda i=i, ip=ip, ab=ab: dma(
                        lnd5x[ip][2 + 4 * ab:6 + 4 * ab, :]
                        .rearrange("k (g s) -> k g s", g=2),
                        fl_b3[ab].rearrange("(g k) s -> k g s", k=4)
                        [:, 2 * i:2 * i + 2, :]))(),
                      reads=["fl1", ("lnd5", ip)], writes=[("lnd5", ip)], dma="f",
                      tag=f"l5_{cg}_{i}_{ab}")
                A("sync", (lambda i=i: dma(
                    rhs5[0:2, :],
                    fl_b2.rearrange("a h s -> a (h s)")[:, i * 1024:(i + 1) * 1024]))(),
                  reads=["fl0", "rhs5"], writes=["rhs5"], dma="f", tag=f"r5_{cg}_{i}")
                for kh in range(2):
                    A("scalar", (lambda i=i, ip=ip, kh=kh: lambda e: e.dma_start(
                        out=s_sF[ip][64 * kh:64 * (kh + 1), :]
                        .rearrange("p (j l) -> p j l", l=128),
                        in_=fl_b.rearrange("(j k) l -> k j l", k=2)
                        [kh, 4 * i:4 * i + 4, :][None, :, :]
                        .broadcast_to([64, 4, 128])))(),
                      reads=["fl2", ("s_sF", ip)], writes=[("s_sF", ip)],
                      dma="fs", tag=f"sF{cg}_{i}_{kh}")
                A("scalar", (lambda i=i, ip=ip, t0=t0, tc0=tc0: dma(
                    sz_ep[ip][:].rearrange("p (j t) -> p j t", t=128),
                    z_sc[4 * i:4 * i + 4, :, t0 + tc0:t0 + tc0 + 128]
                    .rearrange("j p t -> p j t")))(),
                  reads=[("z", st, 4 * i + jx) for jx in range(4)] + [("sz_ep", ip)],
                  writes=[("sz_ep", ip)], dma="zs", tag=f"zl{cg}_{i}")
                for gg in range(2):
                    g = 2 * i + gg
                    pd = 3 + gg
                    sl = sSL[gg]
                    A("tensor", (lambda pd=pd, ip=ip, gg=gg: lambda e: e.matmul(
                        p[pd][:, 0:512], lnd5x[ip][:, gg * 128:(gg + 1) * 128],
                        rhs5[:, gg * 512:(gg + 1) * 512], start=True, stop=True))(),
                      reads=[("lnd5", ip), "lnd5r0", "rhs5", "rhs5c"],
                      writes=[f"p{pd}"], tag=f"D{cg}_{g}", cost=350)
                    A("vector", (lambda pd=pd, gg=gg: lambda e: e.tensor_scalar(
                        out=smind2[gg][:], in0=p[pd][:, 0:512], scalar1=30.0,
                        scalar2=None, op0=ALU.min))(),
                      reads=[f"p{pd}"], writes=[("smind", gg)], tag=f"Dm{cg}_{g}")
                    A("scalar", (lambda gg=gg: lambda e: e.activation(
                        out=sexpD2[gg][:], in_=smind2[gg][:], func=AF.Exp))(),
                      reads=[("smind", gg)], writes=[("sexpD", gg)], tag=f"De{cg}_{g}")
                    A("vector", (lambda sl=sl, gg=gg: lambda e: e.tensor_tensor(
                        out=sl[:].rearrange("s (h l) -> s h l", l=128),
                        in0=sexpD2[gg][:].rearrange("s (h l) -> s h l", l=128),
                        in1=ssm[:, None, :].broadcast_to([128, 4, 128]),
                        op=ALU.mult))(),
                      reads=[("sexpD", gg), "ssm"], writes=[("sSL", gg)], tag=f"SL{cg}_{g}")
                def ydiag(i=i):
                    def f(e):
                        r = None
                        for jj in range(4):
                            pr = 4 * i + jj
                            gg, kk = jj // 2, jj % 2
                            sl = sSL[gg]
                            r = e.matmul(p[6][0:64, jj * 128:(jj + 1) * 128],
                                         x_tm[:, pr * 128:pr * 128 + 64],
                                         sl[:, (2 * kk) * 128:(2 * kk + 1) * 128],
                                         start=True, stop=True, tile_position=(0, 0))
                            r = e.matmul(p[6][64:128, jj * 128:(jj + 1) * 128],
                                         x_tm[:, pr * 128 + 64:pr * 128 + 128],
                                         sl[:, (2 * kk + 1) * 128:(2 * kk + 2) * 128],
                                         start=True, stop=True, tile_position=(0, 64))
                        return r
                    return f
                A("tensor", ydiag(), reads=["x_tm", ("sSL", 0), ("sSL", 1)],
                  writes=["p6"], tag=f"Yd{cg}_{i}", cost=900)
                A("vector", (lambda i=i: lambda e: e.tensor_copy(
                    out=scast[:], in_=S[:, i * 512:(i + 1) * 512]))(),
                  reads=[("S", i)], writes=["scast"], tag=f"Sc{cg}_{i}")
                def yoff(i=i, cch=cch):
                    def f(e):
                        r = None
                        for jj in range(4):
                            for hh in range(2):
                                base = (2 * jj + hh) * 64
                                r = e.matmul(
                                    p[7][hh * 64:(hh + 1) * 64, jj * 128:(jj + 1) * 128],
                                    scast[:, base:base + 64],
                                    sC_fm[:, cch * 128:(cch + 1) * 128],
                                    start=True, stop=True,
                                    tile_position=(0, hh * 64))
                        return r
                    return f
                A("tensor", yoff(), reads=["scast", "sC_fm"], writes=["p7"],
                  tag=f"Yo{cg}_{i}", cost=900)
                A("tensor", (lambda i=i: lambda e: e.matmul(
                    p[5][:, 0:512], sBtm[:], sxw[:, i * 512:(i + 1) * 512],
                    start=True, stop=True))(),
                  reads=["sBtm", "sxw"], writes=["p5"], tag=f"St{cg}_{i}")
                A("vector", (lambda i=i: lambda e: e.tensor_tensor(
                    out=sscan[:].rearrange("n (h q) -> n h q", q=64),
                    in0=S[:, i * 512:(i + 1) * 512].rearrange("n (h q) -> n h q", q=64),
                    in1=cdP[:, 8 * i:8 * i + 8, None].broadcast_to([128, 8, 64]),
                    op=ALU.mult))(),
                  reads=[("S", i), "cdP", "scast"], writes=["sscan"], tag=f"scn{cg}_{i}")
                A("vector", (lambda i=i: lambda e: e.tensor_add(
                    S[:, i * 512:(i + 1) * 512], sscan[:], p[5][:, 0:512]))(),
                  reads=["sscan", "p5"], writes=[("S", i)], tag=f"scn2{cg}_{i}")
                A("vector", (lambda ip=ip: lambda e: e.tensor_tensor(
                    out=sta[:], in0=p[7][:, 0:512], in1=s_sF[ip][:], op=ALU.mult))(),
                  reads=["p7", ("s_sF", ip)], writes=["sta"], tag=f"e1_{cg}_{i}")
                A("vector", lambda e: e.tensor_add(stb[:], sta[:], p[6][:, 0:512]),
                  reads=["sta", "p6"], writes=["stb"], tag=f"e2_{cg}_{i}")
                A("gpsimd", (lambda i=i: lambda e: e.tensor_tensor(
                    out=sxd[:].rearrange("p (j l) -> p j l", l=128),
                    in0=sxch[:].rearrange("p (j t) -> p j t", t=128)
                    [:, 4 * i:4 * i + 4, :],
                    in1=sDp[:, 4 * i:4 * i + 4, None].broadcast_to([128, 4, 128]),
                    op=ALU.mult))(),
                  reads=["sxch", "dp"], writes=["sxd"], tag=f"e3_{cg}_{i}")
                A("gpsimd", lambda e: e.tensor_add(sta[:], stb[:], sxd[:]),
                  reads=["stb", "sxd"], writes=["sta"], tag=f"e4_{cg}_{i}")
                A("scalar", (lambda ip=ip: lambda e: e.activation(
                    out=sg[:], in_=sz_ep[ip][:], func=AF.Silu))(),
                  reads=[("sz_ep", ip)], writes=["sg"], tag=f"e5_{cg}_{i}")
                A("vector", (lambda i=i: lambda e: e.tensor_mul(
                    sy[:, i * 512:(i + 1) * 512], sta[:], sg[:]))(),
                  reads=["sta", "sg"], writes=["sy"], tag=f"e6_{cg}_{i}")
                A("gpsimd", (lambda i=i, ip=ip: lambda e: e.tensor_tensor(
                    out=sy2[ip][:], in0=sy[:, i * 512:(i + 1) * 512],
                    in1=sy[:, i * 512:(i + 1) * 512], op=ALU.mult))(),
                  reads=["sy"], writes=[("sy2", ip)], tag=f"e7_{cg}_{i}")
                def em_stats(i):
                    ipx = i % 2
                    def stats(e):
                        r = None
                        for jj in range(4):
                            r = e.matmul(p[2][0:1, 0:128], sonescol[:],
                                         sy2[ipx][:, jj * 128:(jj + 1) * 128],
                                         start=(jj == 0), stop=(jj == 3))
                        return r
                    A("tensor", stats, reads=[("sy2", ipx), "onescol"], writes=["p2"],
                      tag=f"st2_{cg}_{i}")
                    spar = i % 2
                    if i == 0:
                        A("vector", lambda e: e.tensor_copy(out=sstat[0][:],
                                                            in_=p[2][0:1, 0:128]),
                          reads=["p2"], writes=[("sstat", 0)], tag=f"sa{cg}_{i}")
                    else:
                        A("vector", (lambda spar=spar: lambda e: e.tensor_add(
                            sstat[spar][:], sstat[1 - spar][:], p[2][0:1, 0:128]))(),
                          reads=[("sstat", 1 - spar), "p2"], writes=[("sstat", spar)],
                          tag=f"sa{cg}_{i}")
                if i > 0:
                    em_stats(i - 1)
                if i == 7:
                    em_stats(7)
            A("vector", lambda e: e.tensor_scalar(
                out=sv1[:], in0=sstat[1][:], scalar1=1.0 / DI, scalar2=EPS,
                op0=ALU.mult, op1=ALU.add),
              reads=[("sstat", 1)], writes=["sv1"], tag=f"rms{cg}")
            A("scalar", lambda e: e.activation(out=sv2[:], in_=sv1[:], func=AF.Ln),
              reads=["sv1", "pk2"], writes=["pk2"])
            A("scalar", lambda e: e.activation(out=srstd1[:], in_=sv2[:], func=AF.Exp,
                                               scale=sconstm05[0:1, :]),
              reads=["pk2", "cpool"], writes=["srstd1"])
            A("tensor", lambda e: e.matmul(p[3][:, 0:128], sones64[0:1, :], srstd1[:],
                                           start=True, stop=True),
              reads=["srstd1", "pk4"], writes=["p3"], tag=f"rE{cg}")
            for i in range(8):
                A("vector", (lambda i=i, tc0=tc0: lambda e: e.tensor_tensor(
                    out=syn[:].rearrange("p (j t) -> p j t", t=T)
                    [:, 4 * i:4 * i + 4, tc0:tc0 + 128],
                    in0=sy[:, i * 512:(i + 1) * 512].rearrange("p (j l) -> p j l", l=128),
                    in1=p[3][:, None, 0:128].broadcast_to([128, 4, 128]),
                    op=ALU.mult))(),
                  reads=["sy", "p3"], writes=["syn"], tag=f"yn{cg}_{i}")

        for half in range(2):
            tcs = (2 * half, 2 * half + 1)
            wo3 = [wo_sb[0], wo_sb[1], sy[:, 0:2048]]
            for k in range(32):
                par = k % 3
                wtok = ("wo_sb", par) if par < 2 else "sy"
                if st == 0 and half == 0:
                    A("gpsimd", (lambda k=k, par=par: dma(wo3[par][:], wo_t[k]))(),
                      reads=[wtok], writes=[wtok], dma="wog", tag=f"woL{k}")
                    A("vector", (lambda k=k, par=par: lambda e: e.tensor_scalar(
                        out=wo3[par][:], in0=wo3[par][:], scalar1=snw[:, k:k + 1],
                        scalar2=None, op0=ALU.mult))(),
                      reads=[wtok, "nw"], writes=[wtok], tag=f"woF{k}")
                    A("sync", (lambda k=k, par=par: dma(wo_sc[k], wo3[par][:]))(),
                      reads=[wtok], writes=[("wosc", k)], dma="wo")
                else:
                    A("sync", (lambda k=k, par=par: dma(wo3[par][:], wo_sc[k]))(),
                      reads=[wtok, ("wosc", k)], writes=[wtok], dma="wo",
                      tag=f"woL{st}_{half}_{k}")
                def opmm(k=k, par=par, tcs=tcs):
                    def f(e):
                        r = None
                        for ti, tcx in enumerate(tcs):
                            for o in range(4):
                                r = e.matmul(p[ti * 4 + o][:, 0:512],
                                             syn[:, k * T + tcx * 128:k * T + tcx * 128 + 128],
                                             wo3[par][:, o * 512:(o + 1) * 512],
                                             start=(k == 0), stop=(k == 31))
                        return r
                    return f
                A("tensor", opmm(), reads=[wtok, "syn"],
                  writes=[f"p{b}" for b in range(8)], tag=f"op{st}_{half}_{k}", cost=1900)
            for ti, tcx in enumerate(tcs):
                x1t = x1[ti]
                for o in range(4):
                    rpar = o % 2
                    A("sync", (lambda t0=t0, tcx=tcx, o=o, rpar=rpar: dma(
                        su_res[rpar][:],
                        u_tm[t0 + tcx * 128:t0 + (tcx + 1) * 128,
                             o * 512:(o + 1) * 512]))(),
                      reads=[("su_res", rpar)], writes=[("su_res", rpar)], dma="u",
                      tag=f"ur{st}_{tcx}_{o}")
                    A("vector", (lambda ti=ti, o=o, rpar=rpar, x1t=x1t: lambda e:
                      e.tensor_add(x1t[:, o * 512:(o + 1) * 512],
                                   p[ti * 4 + o][:, 0:512], su_res[rpar][:]))(),
                      reads=[f"p{ti * 4 + o}", ("su_res", rpar)],
                      writes=[("x1", ti)], tag=f"h1{st}_{tcx}_{o}")
                A("scalar", (lambda x1t=x1t, ti=ti: lambda e: e.activation(
                    out=x_tm[:, 0:2048], in_=x1t[:], func=AF.Copy,
                    accum_out=ssum2_[ti][:]))(),
                  reads=[("x1", ti)], writes=[("ssum", ti), "x_tm"], tag=f"ls{st}_{tcx}")
                A("scalar", (lambda x1t=x1t, ti=ti: lambda e: e.activation(
                    out=x_tm[:, 0:2048], in_=x1t[:], func=AF.Square,
                    accum_out=ssq2_[ti][:]))(),
                  reads=[("x1", ti)], writes=[("ssq", ti), "x_tm"])
                A("vector", (lambda ti=ti: lambda e: e.tensor_scalar_mul(smean2_[ti][:], ssum2_[ti][:], 1.0 / DM))(),
                  reads=[("ssum", ti)], writes=[("smean", ti)])
                A("vector", (lambda ti=ti: lambda e: e.tensor_scalar(
                    out=sm22_[ti][:], in0=smean2_[ti][:], scalar1=smean2_[ti][:], scalar2=EPS,
                    op0=ALU.mult, op1=ALU.subtract))(),
                  reads=[("smean", ti)], writes=[("sm2", ti)])
                A("vector", (lambda ti=ti: lambda e: e.tensor_scalar(
                    out=svar2_[ti][:], in0=ssq2_[ti][:], scalar1=1.0 / DM, scalar2=None,
                    op0=ALU.mult))(),
                  reads=[("ssq", ti)], writes=[("svar", ti)])
                A("vector", (lambda ti=ti: lambda e: e.tensor_sub(svar2_[ti][:], svar2_[ti][:], sm22_[ti][:]))(),
                  reads=[("svar", ti), ("sm2", ti)], writes=[("svar", ti)])
                A("scalar", (lambda ti=ti: lambda e: e.activation(out=svar2_[ti][:], in_=svar2_[ti][:], func=AF.Ln))(),
                  reads=[("svar", ti)], writes=[("svar", ti)])
                A("scalar", (lambda ti=ti: lambda e: e.activation(out=srstd2_[ti][:], in_=svar2_[ti][:],
                                                   func=AF.Exp, scale=sconstm05[:]))(),
                  reads=[("svar", ti), "cpool"], writes=[("srstd", ti)])
                A("vector", (lambda x1t=x1t, ti=ti: lambda e: e.tensor_scalar(
                    out=x1t[:], in0=x1t[:], scalar1=smean2_[ti][:], scalar2=srstd2_[ti][:],
                    op0=ALU.subtract, op1=ALU.mult))(),
                  reads=[("smean", ti), ("srstd", ti), ("x1", ti)], writes=[("x1", ti)],
                  tag=f"ln1_{st}_{tcx}")
                A("vector", (lambda x1t=x1t, ti=ti: lambda e: e.tensor_reduce(
                    out=samax2_[ti][:, :, None],
                    in_=x1t[:].rearrange("p (a b) -> p a b", b=128),
                    axis=mybir.AxisListType.X, op=ALU.max,
                    apply_absolute_value=True))(),
                  reads=[("x1", ti)], writes=[("samax", ti)], tag=f"qa{st}_{tcx}")
                A("vector", (lambda ti=ti: lambda e: e.tensor_scalar_max(
                    samax2_[ti][:], samax2_[ti][:], 1e-4))(),
                  reads=[("samax", ti)], writes=[("samax", ti)])
                A("vector", (lambda ti=ti: lambda e: e.reciprocal(
                    out=srcp2_[ti][:], in_=samax2_[ti][:]))(),
                  reads=[("samax", ti)], writes=[("srcp", ti)])
                A("vector", (lambda ti=ti: lambda e: e.tensor_scalar_mul(
                    sr2242_[ti][:], srcp2_[ti][:], 224.0))(),
                  reads=[("srcp", ti)], writes=[("sr224", ti)])
                A("vector", (lambda ti=ti: lambda e: e.tensor_scalar_mul(
                    sa2242_[ti][:], samax2_[ti][:], 1.0 / 224.0))(),
                  reads=[("samax", ti)], writes=[("sa224", ti)])
                A("vector", (lambda x1t=x1t, ti=ti: lambda e: e.tensor_tensor(
                    out=sq8[:].rearrange("p (a b) -> p a b", b=128),
                    in0=x1t[:].rearrange("p (a b) -> p a b", b=128),
                    in1=sr2242_[ti][:, :, None].broadcast_to([128, 16, 128]),
                    op=ALU.mult))(),
                  reads=[("x1", ti), ("sr224", ti)], writes=["stb"], tag=f"q8{st}_{tcx}")
                A("vector", (lambda ti=ti: lambda e: e.tensor_tensor(
                    out=sxdq[:].rearrange("p (a b) -> p a b", b=128),
                    in0=sq8[:].rearrange("p (a b) -> p a b", b=128),
                    in1=sa2242_[ti][:, :, None].broadcast_to([128, 16, 128]),
                    op=ALU.mult))(),
                  reads=["stb", ("sa224", ti)], writes=["sxdq"], tag=f"dq{st}_{tcx}")
                for iq in range(2):
                    for ii in range(8):
                        i8 = iq * 8 + ii
                        A("tensor", (lambda i8=i8, ii=ii: lambda e: e.transpose(
                            p2b[:, ii * 128:(ii + 1) * 128],
                            sxdq[:, i8 * 128:(i8 + 1) * 128], seyeb[:]))(),
                          reads=["sxdq", "eyeb"], writes=["p2"],
                          tag=f"qT{st}_{tcx}_{i8}")
                    A("scalar", (lambda ti=ti, iq=iq: lambda e: e.copy(
                        out=sxdq_fm[ti][:, iq * 1024:(iq + 1) * 1024],
                        in_=p2b[:, 0:1024]))(),
                      reads=["p2"], writes=[("xdq", ti)], tag=f"qTe{st}_{tcx}_{iq}")
            wdq3 = [wdq_sb[0], wdq_sb[1], sy[:, 2048:3072]]
            for ohalf in range(2):
                for i in range(16):
                    par = i % 3
                    wtok = ("wdq", par) if par < 2 else "sy"
                    A("sync", (lambda i=i, par=par, ohalf=ohalf: dma(
                        wdq3[par][:], wf_sc[i][:, ohalf * 1024:(ohalf + 1) * 1024]))(),
                      reads=[wtok, ("wf", i)], writes=[wtok], dma="wo",
                      tag=f"wfL{st}_{half}_{ohalf}_{i}")
                    def ffmm(i=i, par=par):
                        def f(e):
                            r = None
                            for ti in range(2):
                                for o2 in range(2):
                                    r = e.matmul(p[4 + ti * 2 + o2][:, 0:512],
                                                 sxdq_fm[ti][:, i * 128:(i + 1) * 128],
                                                 wdq3[par][:, o2 * 512:(o2 + 1) * 512],
                                                 start=(i == 0), stop=(i == 15))
                            return r
                        return f
                    A("tensor", ffmm(), reads=[wtok, ("xdq", 0), ("xdq", 1)],
                      writes=[f"p{b}" for b in range(4, 8)], tag=f"ff{st}_{half}_{ohalf}_{i}", cost=1000)
                for ti in range(2):
                    sh2x = sh2 if ti == 0 else sh2b
                    htok = "sxw" if ti == 0 else "sy"
                    for o2 in range(2):
                        o = ohalf * 2 + o2
                        A("scalar", (lambda ti=ti, o2=o2: lambda e: e.copy(
                            out=sfb[:], in_=p[4 + ti * 2 + o2][:, 0:512]))(),
                          reads=[f"p{4 + ti * 2 + o2}"], writes=["scast"],
                          tag=f"fb{st}_{half}_{ti}_{o}")
                        A("vector", (lambda ti=ti, o=o, sh2x=sh2x, htok=htok: lambda e:
                          e.tensor_add(sh2x[:, o * 512:(o + 1) * 512],
                                       x1[ti][:, o * 512:(o + 1) * 512], sfb[:]))(),
                          reads=["scast", ("x1", ti)], writes=[htok],
                          tag=f"h2{st}_{half}_{ti}_{o}")
            for ti, tcx in enumerate(tcs):
                sh2x = sh2 if ti == 0 else sh2b
                htok = "sxw" if ti == 0 else "sy"
                A("scalar", (lambda sh2x=sh2x, ti=ti: lambda e: e.activation(
                    out=x_tm[:, 0:2048], in_=sh2x[:], func=AF.Copy,
                    accum_out=ssum2_[ti][:]))(),
                  reads=[htok], writes=[("ssum", ti), "x_tm"])
                A("scalar", (lambda sh2x=sh2x, ti=ti: lambda e: e.activation(
                    out=x_tm[:, 0:2048], in_=sh2x[:], func=AF.Square,
                    accum_out=ssq2_[ti][:]))(),
                  reads=[htok], writes=[("ssq", ti), "x_tm"])
                A("vector", (lambda ti=ti: lambda e: e.tensor_scalar_mul(smean2_[ti][:], ssum2_[ti][:], 1.0 / DM))(),
                  reads=[("ssum", ti)], writes=[("smean", ti)])
                A("vector", (lambda ti=ti: lambda e: e.tensor_scalar(
                    out=sm22_[ti][:], in0=smean2_[ti][:], scalar1=smean2_[ti][:], scalar2=EPS,
                    op0=ALU.mult, op1=ALU.subtract))(),
                  reads=[("smean", ti)], writes=[("sm2", ti)])
                A("vector", (lambda ti=ti: lambda e: e.tensor_scalar(
                    out=svar2_[ti][:], in0=ssq2_[ti][:], scalar1=1.0 / DM, scalar2=None,
                    op0=ALU.mult))(),
                  reads=[("ssq", ti)], writes=[("svar", ti)])
                A("vector", (lambda ti=ti: lambda e: e.tensor_sub(svar2_[ti][:], svar2_[ti][:], sm22_[ti][:]))(),
                  reads=[("svar", ti), ("sm2", ti)], writes=[("svar", ti)])
                A("scalar", (lambda ti=ti: lambda e: e.activation(out=svar2_[ti][:], in_=svar2_[ti][:],
                                                   func=AF.Ln))(),
                  reads=[("svar", ti)], writes=[("svar", ti)])
                A("scalar", (lambda ti=ti: lambda e: e.activation(out=srstd2_[ti][:], in_=svar2_[ti][:],
                                                   func=AF.Exp, scale=sconstm05[:]))(),
                  reads=[("svar", ti), "cpool"], writes=[("srstd", ti)])
                x1t = x1[ti]
                for o in range(4):
                    A("vector", (lambda o=o, sh2x=sh2x, htok=htok, ti=ti, x1t=x1t:
                      lambda e: e.tensor_scalar(
                        out=x1t[:, o * 512:(o + 1) * 512],
                        in0=sh2x[:, o * 512:(o + 1) * 512],
                        scalar1=smean2_[ti][:], scalar2=srstd2_[ti][:],
                        op0=ALU.subtract, op1=ALU.mult))(),
                      reads=[htok, ("smean", ti), ("srstd", ti), ("x1", ti)],
                      writes=[("x1", ti)], tag=f"ln2_{st}_{tcx}_{o}")
                A("sync", (lambda t0=t0, tcx=tcx, ti=ti, x1t=x1t: dma(
                    out[t0 + tcx * 128:t0 + (tcx + 1) * 128, :], x1t[:]))(),
                  reads=[("x1", ti)], writes=[("x1", ti)],
                  dma="out", tag=f"out{st}_{tcx}")

    s.emit(nc, stack)
    return nc, stack


def _host_prep(inputs, NST=8):
    L = NST * 512
    u = np.ascontiguousarray(np.asarray(inputs["L0_attn_u"], dtype=np.float32))
    Wi = np.asarray(inputs["L0_attn_in_proj_kernel"], dtype=np.float32)
    Wo = np.asarray(inputs["L0_attn_out_proj_kernel"], dtype=np.float32)
    Wf = np.asarray(inputs["L0_ffn_weight"], dtype=np.float32)
    convw = np.asarray(inputs["L0_attn_conv1d_weight"], dtype=np.float32)
    convb = np.asarray(inputs["L0_attn_conv1d_bias"], dtype=np.float32)
    dtb = np.asarray(inputs["L0_attn_dt_bias"], dtype=np.float32)
    alog = np.asarray(inputs["L0_attn_A_log"], dtype=np.float32)
    Dv = np.asarray(inputs["L0_attn_D"], dtype=np.float32)
    nw = np.asarray(inputs["L0_attn_norm_weight"], dtype=np.float32)

    Wi_p = np.zeros((2048, NCT * 128), dtype=np.float32)
    Wi_p[:, :DIP] = Wi
    wi_t = np.ascontiguousarray(
        Wi_p.reshape(16, 128, NCT, 128).transpose(2, 1, 0, 3).reshape(NCT, 128, 2048))
    wo_t = np.ascontiguousarray(Wo.reshape(32, 128, DM))
    wf_t = np.ascontiguousarray(Wf.T.reshape(16, 128, 2048))
    convw_t = np.ascontiguousarray(convw.reshape(CW, 128, 4).transpose(1, 0, 2))
    convb_t = np.ascontiguousarray(convb.reshape(CW, 128).T)
    dp = np.empty((128, 32), dtype=np.float32)
    for pp in range(128):
        for j in range(32):
            dp[pp, j] = Dv[2 * j + pp // 64]
    nw_t = np.ascontiguousarray(nw.reshape(32, 128).T)
    eye_b = np.eye(128, dtype=ml_dtypes.bfloat16)
    eye_f = np.eye(128, dtype=np.float32)
    m01 = np.tril(np.ones((128, 128), np.float32)).T.astype(ml_dtypes.bfloat16)
    ones_col_b = np.ones((128, 1), dtype=ml_dtypes.bfloat16)
    ones_row_f = np.ones((1, 128), dtype=np.float32)
    ind4 = np.zeros((8, 2, 4, 128), dtype=np.float32)
    for k in range(4):
        ind4[k, :, k, :] = 1.0
        ind4[4 + k, :, k, :] = 1.0
    ind4 = ind4.reshape(8, 1024).astype(ml_dtypes.bfloat16)

    shared = dict(wi_t=wi_t, wo_t=wo_t, wf_t=wf_t, convw=convw_t, convb=convb_t,
                  dtb=dtb.reshape(64, 1), alog=alog.reshape(64, 1), dp_in=dp,
                  normw=nw_t, eye_b=eye_b, eye_f=eye_f, mask01=m01,
                  ones_col_b=ones_col_b, ones_row_f=ones_row_f, ind4=ind4)
    in_maps = []
    for b in range(u.shape[0]):
        m = dict(shared)
        m["u_t"] = np.ascontiguousarray(u[b, :L].T.reshape(16, 128, L))
        m["u_tm"] = np.ascontiguousarray(u[b, :L])
        in_maps.append(m)
    return in_maps


def kernel(**inputs):
    NST = 8
    nc, stack = build_program(NST)
    in_maps = _host_prep(inputs, NST)
    try:
        res = run_bass_kernel_spmd(nc, in_maps, core_ids=list(range(len(in_maps))))
    finally:
        stack.close()
    return np.stack([np.asarray(r["out"], dtype=np.float32) for r in res.results])
